# version 02
# baseline (speedup 1.0000x reference)
import sys
sys.path.insert(0, "/opt/trn_rl_repo")
import numpy as np
import ml_dtypes
from contextlib import ExitStack, nullcontext

import concourse.bass as bass
import concourse.mybir as mybir
from concourse.bass_utils import run_bass_kernel_spmd

F32 = mybir.dt.float32
BF16 = mybir.dt.bfloat16
U8 = mybir.dt.uint8
AF = mybir.ActivationFunctionType
ALU = mybir.AluOpType
BF = ml_dtypes.bfloat16

NCORES = 8
B, S, D = 2, 8192, 1024
TPC = S * B // NCORES
NT = TPC // 128
NC = D // 128
NKT = S // 128
H, HD, NG, WIN = 16, 64, 4, 256
EPS = 1e-5
B0 = 40.0

GLOBAL_UNITS = [(b, h) for b in range(B) for h in range(NG)]
LOCAL_UNITS = [(b, h) for b in range(B) for h in range(NG, H)]
CORE_UNITS = [[GLOBAL_UNITS[c]] + LOCAL_UNITS[3 * c:3 * c + 3] for c in range(NCORES)]



def emit_stats(nc, ctx, src_sb, nt, stats, tag):
    scrA = ctx.enter_context(nc.sbuf_tensor(f"stA{tag}", [128, D], BF16))
    scrB = ctx.enter_context(nc.sbuf_tensor(f"stB{tag}", [128, D], BF16))
    SU, SQ, MU, RS = 0, nt, 2 * nt, 3 * nt
    with nc.Block() as blk:
        @blk.vector
        def _(e):
            for t in range(nt):
                if t:
                    e.drain()
                e.tensor_scalar(scrA[:, :], src_sb[:, t * D:(t + 1) * D], 1.0, 0.0,
                                ALU.mult, ALU.add, accum_out=stats[:, SU + t:SU + t + 1])
        @blk.scalar
        def _(e):
            for t in range(nt):
                if t:
                    e.drain()
                e.activation(scrB[:, :], src_sb[:, t * D:(t + 1) * D], AF.Square,
                             accum_out=stats[:, SQ + t:SQ + t + 1])
    with nc.Block() as blk:
        @blk.vector
        def _(e):
            e.tensor_scalar(stats[:, MU:MU + nt], stats[:, SU:SU + nt], 1.0 / D, 0.0,
                            ALU.mult, ALU.add)
            e.drain()
            e.tensor_tensor(stats[:, SU:SU + nt], stats[:, MU:MU + nt],
                            stats[:, MU:MU + nt], ALU.mult)
            e.tensor_scalar(stats[:, SQ:SQ + nt], stats[:, SQ:SQ + nt], 1.0 / D, EPS,
                            ALU.mult, ALU.add)
            e.drain()
            e.scalar_tensor_tensor(stats[:, SQ:SQ + nt], stats[:, SQ:SQ + nt], 1.0,
                                   stats[:, SU:SU + nt], ALU.mult, ALU.subtract)
    with nc.Block() as blk:
        @blk.scalar
        def _(e):
            e.activation(stats[:, SQ:SQ + nt], stats[:, SQ:SQ + nt], AF.Sqrt)
    with nc.Block() as blk:
        @blk.vector
        def _(e):
            e.reciprocal(stats[:, RS:RS + nt], stats[:, SQ:SQ + nt])
            e.drain()


def emit_ln(nc, ctx, x_sb, nt, out_bf, tag):
    stats = ctx.enter_context(nc.sbuf_tensor(f"lnst{tag}", [128, 4 * nt], F32))
    emit_stats(nc, ctx, x_sb, nt, stats, tag)
    with nc.Block() as blk:
        @blk.vector
        def _(e):
            for t in range(nt):
                e.tensor_scalar(out_bf[:, t * D:(t + 1) * D], x_sb[:, t * D:(t + 1) * D],
                                stats[:, 2 * nt + t:2 * nt + t + 1],
                                stats[:, 3 * nt + t:3 * nt + t + 1],
                                ALU.subtract, ALU.mult)


def emit_ln_resid_streamx(nc, ctx, y_sb, x_dram, out_dram, nt, tag):
    stats = ctx.enter_context(nc.sbuf_tensor(f"lrst{tag}", [128, 4 * nt], F32))
    emit_stats(nc, ctx, y_sb, nt, stats, tag + "r")
    xb = [ctx.enter_context(nc.sbuf_tensor(f"lrx{tag}{i}", [128, D], F32)) for i in range(2)]
    ob = [ctx.enter_context(nc.sbuf_tensor(f"lro{tag}{i}", [128, D], F32)) for i in range(2)]
    tmp = ctx.enter_context(nc.sbuf_tensor(f"lrt{tag}", [128, D], F32))
    sld = [ctx.enter_context(nc.semaphore(f"lrl{tag}{i}")) for i in range(2)]
    scp = ctx.enter_context(nc.semaphore(f"lrc{tag}"))
    sst = [ctx.enter_context(nc.semaphore(f"lrs{tag}{i}")) for i in range(2)]
    with nc.Block() as blk:
        @blk.sync
        def _(e):
            for t in range(nt):
                if t >= 2:
                    e.wait_ge(scp, t - 1)
                    e.dma_start(out=out_dram[(t - 2) * 128:(t - 1) * 128, :],
                                in_=ob[t % 2][:, :]).then_inc(sst[t % 2], 16)
                e.dma_start(out=xb[t % 2][:, :],
                            in_=x_dram[t * 128:(t + 1) * 128, :]).then_inc(sld[t % 2], 16)
            for t in (nt - 2, nt - 1):
                e.wait_ge(scp, t + 1)
                e.dma_start(out=out_dram[t * 128:(t + 1) * 128, :],
                            in_=ob[t % 2][:, :]).then_inc(sst[t % 2], 16)
            e.wait_ge(sst[0], 16 * (nt - nt // 2))
            e.wait_ge(sst[1], 16 * (nt // 2))
        @blk.vector
        def _(e):
            for t in range(nt):
                e.wait_ge(sld[t % 2], 16 * (t // 2 + 1))
                if t >= 2:
                    e.wait_ge(sst[t % 2], 16 * (t // 2))
                e.tensor_scalar(tmp[:, :], y_sb[:, t * D:(t + 1) * D],
                                stats[:, 2 * nt + t:2 * nt + t + 1],
                                stats[:, 3 * nt + t:3 * nt + t + 1],
                                ALU.subtract, ALU.mult)
                e.drain()
                e.scalar_tensor_tensor(ob[t % 2][:, :], tmp[:, :], 1.0,
                                       xb[t % 2][:, :], ALU.mult,
                                       ALU.add).then_inc(scp, 1)
                e.drain()


def emit_load_tiles(e, sem, x_sb, src, nt, ds):
    for t in range(nt):
        e.dma_start(out=x_sb[:, t * D:(t + 1) * D],
                    in_=src[t * 128:(t + 1) * 128, :]).then_inc(sem, 16)
        ds[0] += 16


def emit_store_tiles(e, sem, dst, x_sb, nt, ds):
    for t in range(nt):
        e.dma_start(out=dst[t * 128:(t + 1) * 128, :],
                    in_=x_sb[:, t * D:(t + 1) * D]).then_inc(sem, 16)
        ds[0] += 16


def col_load(e, sem, dst_col, src_row, ds):
    e.dma_start(out=dst_col, in_=src_row.rearrange("o p -> p o")).then_inc(sem, 16)
    ds[0] += 16


def wrap_names(nc):
    if getattr(nc, "_name_wrapped", False):
        return
    nc._name_wrapped = True
    nc._cur_pfx = ""
    for meth in ("sbuf_tensor", "psum_tensor", "dram_tensor"):
        orig = getattr(nc, meth)
        def mk(orig):
            def f(name, *a, **k):
                return orig(nc._cur_pfx + name, *a, **k)
            return f
        setattr(nc, meth, mk(orig))
    orig_sem = nc.semaphore
    def sem_f(name, *a, **k):
        cm = orig_sem(nc._cur_pfx + name, *a, **k)
        st = getattr(nc, "_sem_stack", None)
        if st is None:
            return cm
        return nullcontext(st.enter_context(cm))
    nc.semaphore = sem_f


def set_pfx(nc, pfx):
    wrap_names(nc)
    nc._cur_pfx = pfx



def build_A():
    nc = bass.Bass("TRN2", target_bir_lowering=False)
    io = {"x": nc.dram_tensor("x", [TPC, D], F32, kind="ExternalInput").ap(),
          "wq": nc.dram_tensor("wq", [D, 3 * D], F32, kind="ExternalInput").ap(),
          "bq": nc.dram_tensor("bq", [1, 3 * D], F32, kind="ExternalInput").ap(),
          "qkv": nc.dram_tensor("qkv", [TPC, 3 * D], BF16, kind="ExternalOutput").ap()}
    emit_A(nc, io)
    return nc


def emit_A(nc, io):
    x_in, wq_in, bq_in, qkv_out = io["x"], io["wq"], io["bq"], io["qkv"]
    xn_d = nc.dram_tensor("xn_d", [TPC, D], BF16)

    with ExitStack() as ctx:
        ds, gs = [0], [0]
        sd = ctx.enter_context(nc.semaphore("sd"))
        sg = ctx.enter_context(nc.semaphore("sg"))
        smm = ctx.enter_context(nc.semaphore("smm"))
        sev = ctx.enter_context(nc.semaphore("sev"))
        ssa = [ctx.enter_context(nc.semaphore(f"ssa{i}")) for i in range(2)]
        with ExitStack() as ph1:
            xs = ph1.enter_context(nc.sbuf_tensor("xs", [128, NT * D], F32))
            xbf = ph1.enter_context(nc.sbuf_tensor("xbf", [128, NT * D], BF16))
            with nc.Block() as blk:
                @blk.sync
                def _(e):
                    emit_load_tiles(e, sd, xs, x_in, NT, ds)
                    e.wait_ge(sd, ds[0])
            emit_ln(nc, ph1, xs, NT, xbf, "A")
            with nc.Block() as blk:
                @blk.sync
                def _(e):
                    emit_store_tiles(e, sd, xn_d, xbf, NT, ds)
                    e.wait_ge(sd, ds[0])
        xT = ctx.enter_context(nc.sbuf_tensor("xT", [128, NC * TPC], BF16))
        wqb = ctx.enter_context(nc.sbuf_tensor("wqb", [128, NC * 3 * D], BF16))
        bias = ctx.enter_context(nc.sbuf_tensor("bias", [128, 3 * D], F32))
        brow = ctx.enter_context(nc.sbuf_tensor("brow", [1, 3 * D], F32))
        ones32 = ctx.enter_context(nc.sbuf_tensor("ones32", [1, 128], F32))
        stage = ctx.enter_context(nc.sbuf_tensor("stage", [128, 2 * 3 * D], BF16))
        psb = [ctx.enter_context(nc.psum_tensor(f"psb{i}", [128, 512], F32)) for i in range(2)]
        with nc.Block() as blk:
            @blk.sync
            def _(e):
                for c in range(NC):
                    e.dma_start(out=xT[:, c * TPC:(c + 1) * TPC],
                                in_=xn_d[:, c * 128:(c + 1) * 128],
                                transpose=True).then_inc(sd, 16)
                    ds[0] += 16
                e.dma_start(out=brow[:, :], in_=bq_in).then_inc(sd, 16)
                ds[0] += 16
                e.wait_ge(sd, ds[0])
            @blk.gpsimd
            def _(e):
                for c in range(NC):
                    e.dma_start(out=wqb[:, c * 3 * D:(c + 1) * 3 * D],
                                in_=wq_in[c * 128:(c + 1) * 128, :]).then_inc(sg, 16)
                    gs[0] += 16
                e.wait_ge(sg, gs[0])
            @blk.vector
            def _(e):
                e.memset(ones32[:, :], 1.0)
        with nc.Block() as blk:
            @blk.tensor
            def _(e):
                for i in range(6):
                    if i >= 2:
                        e.wait_ge(sev, i - 1)
                    e.matmul(psb[i % 2][:, :], ones32[:1, :],
                             brow[:1, i * 512:(i + 1) * 512],
                             start=True, stop=True).then_inc(smm, 1)
            @blk.vector
            def _(e):
                for i in range(6):
                    e.wait_ge(smm, i + 1)
                    e.tensor_copy(bias[:, i * 512:(i + 1) * 512],
                                  psb[i % 2][:, :]).then_inc(sev, 1)
        ps = [ctx.enter_context(nc.psum_tensor(f"ps{n}", [128, 512], F32)) for n in range(6)]
        mm0, ev0 = 6, 6
        with nc.Block() as blk:
            @blk.tensor
            def _(e):
                for t in range(NT):
                    if t > 0:
                        e.wait_ge(sev, ev0 + 6 * t)
                    for c in range(NC):
                        lhsT = xT[:, c * TPC + t * 128:c * TPC + (t + 1) * 128]
                        for n in range(6):
                            ins = e.matmul(
                                ps[n][:, :], lhsT,
                                wqb[:, c * 3 * D + n * 512:c * 3 * D + (n + 1) * 512],
                                start=(c == 0), stop=(c == NC - 1))
                    ins.then_inc(smm, 1)
            @blk.vector
            def _(e):
                for t in range(NT):
                    e.wait_ge(smm, mm0 + t + 1)
                    if t >= 2:
                        e.wait_ge(ssa[t % 2], 16 * (t // 2))
                    for n in range(6):
                        off = (t % 2) * 3 * D + n * 512
                        e.scalar_tensor_tensor(stage[:, off:off + 512], ps[n][:, :],
                                               1.0, bias[:, n * 512:(n + 1) * 512],
                                               ALU.mult, ALU.add).then_inc(sev, 1)
            @blk.sync
            def _(e):
                for t in range(NT):
                    e.wait_ge(sev, ev0 + 6 * (t + 1))
                    e.dma_start(out=qkv_out[t * 128:(t + 1) * 128, :],
                                in_=stage[:, (t % 2) * 3 * D:(t % 2 + 1) * 3 * D]
                                ).then_inc(ssa[t % 2], 16)
                e.wait_ge(ssa[0], 16 * (NT - NT // 2))
                e.wait_ge(ssa[1], 16 * (NT // 2))



def build_B(SS=S):
    nc = bass.Bass("TRN2", target_bir_lowering=False)
    io = {"ctxn": nc.dram_tensor("ctxn", [4 * 64, SS], BF16, kind="ExternalOutput").ap()}
    for p in range(2):
        io[f"kt{p}"] = nc.dram_tensor(f"kt{p}", [128, SS], BF16, kind="ExternalInput").ap()
        io[f"qt{p}"] = nc.dram_tensor(f"qt{p}", [128, SS], BF16, kind="ExternalInput").ap()
        io[f"vt{p}"] = nc.dram_tensor(f"vt{p}", [128, SS], BF16, kind="ExternalInput").ap()
    emit_B(nc, io, SS=SS)
    return nc


def emit_B(nc, io, SS=S):
    NKT_ = SS // 128
    nsb = SS // 1024
    nblocks = SS // 512
    fused = "qkv_g" in io
    if not fused:
        kt = [io[f"kt{p}"] for p in range(2)]
        qt = [io[f"qt{p}"] for p in range(2)]
        vt = [io[f"vt{p}"] for p in range(2)]
    ctx_out = io["ctxn"]

    with ExitStack() as ctx:
        ds = [0]
        sd = ctx.enter_context(nc.semaphore("sd"))
        s_mm = ctx.enter_context(nc.semaphore("s_mm"))
        s_exp = ctx.enter_context(nc.semaphore("s_exp"))
        s_msk = ctx.enter_context(nc.semaphore("s_msk"))
        s_av = ctx.enter_context(nc.semaphore("s_av"))
        s_ev = ctx.enter_context(nc.semaphore("s_ev"))
        s_tr = ctx.enter_context(nc.semaphore("s_tr"))

        ktb = [ctx.enter_context(nc.sbuf_tensor(f"ktb{p}", [128, SS], BF16)) for p in range(2)]
        qtb = [ctx.enter_context(nc.sbuf_tensor(f"qtb{p}", [128, SS], BF16)) for p in range(2)]
        vpt = [ctx.enter_context(nc.sbuf_tensor(f"vpt{u}", [128, 65 * NKT_], BF16))
               for u in range(4)]
        identb = ctx.enter_context(nc.sbuf_tensor("identb", [128, 128], BF16))
        m_lo = ctx.enter_context(nc.sbuf_tensor("m_lo", [128, 128], BF16))
        m_hi = ctx.enter_context(nc.sbuf_tensor("m_hi", [128, 128], BF16))
        rowi = ctx.enter_context(nc.sbuf_tensor("rowi", [128, 128], F32))
        coli = ctx.enter_context(nc.sbuf_tensor("coli", [128, 128], F32))
        pbuf = [ctx.enter_context(nc.sbuf_tensor(f"pbuf{i}", [128, 1024], BF16))
                for i in range(2)]
        den = ctx.enter_context(nc.sbuf_tensor("den", [1, SS], F32))
        rbc = ctx.enter_context(nc.sbuf_tensor("rbc", [64, SS], BF16))
        ctxn = ctx.enter_context(nc.sbuf_tensor("ctxn_sb", [64, SS], BF16))
        ctxraw = ctxn
        ones32 = ctx.enter_context(nc.sbuf_tensor("ones32", [1, 64], F32))
        b0col = ctx.enter_context(nc.sbuf_tensor("b0col", [128, 1], F32))

        with nc.Block() as blk:
            @blk.gpsimd
            def _(e):
                e.iota(rowi[:, :], [[0, 128]], channel_multiplier=1,
                       allow_small_or_imprecise_dtypes=True)
                e.iota(coli[:, :], [[1, 128]], channel_multiplier=0,
                       allow_small_or_imprecise_dtypes=True)
            @blk.sync
            def _(e):
                if fused:
                    offsb = io["offsb"]
                    qkv_g = io["qkv_g"]
                    with e.register("dynoff") as off:
                        for p in range(2):
                            for j, dst in ((0 + p, ktb[p]), (2 + p, qtb[p])):
                                e.reg_load(off, offsb[0:1, j:j + 1])
                                e.dma_start(out=dst[:, :],
                                            in_=bass.AP(qkv_g.tensor, off,
                                                        [[3 * D, SS], [1, 128]]),
                                            transpose=True).then_inc(sd, 16)
                                ds[0] += 16
                else:
                    for p in range(2):
                        e.dma_start(out=ktb[p][:, :], in_=kt[p]).then_inc(sd, 16)
                        e.dma_start(out=qtb[p][:, :], in_=qt[p]).then_inc(sd, 16)
                        ds[0] += 32
                e.wait_ge(sd, ds[0])
            @blk.vector
            def _(e):
                e.memset(ones32[:, :], 1.0)
                e.memset(b0col[:, :], -B0)
                for u in range(4):
                    e.memset(vpt[u][:, :], 1.0)
        with nc.Block() as blk:
            @blk.vector
            def _(e):
                e.tensor_tensor(m_lo[:, :], coli[:, :], rowi[:, :], ALU.is_ge)
                e.tensor_tensor(m_hi[:, :], coli[:, :], rowi[:, :], ALU.is_le)
                e.tensor_tensor(identb[:, :], coli[:, :], rowi[:, :], ALU.is_equal)

        with ExitStack() as trctx:
            vts = trctx.enter_context(nc.sbuf_tensor("vts", [128, SS], BF16))
            pstr = [trctx.enter_context(nc.psum_tensor(f"pstr{i}", [128, 128], BF16))
                    for i in range(2)]
            ntr = 0
            for p in range(2):
                with nc.Block() as blk:
                    @blk.sync
                    def _(e):
                        if fused:
                            with e.register(f"dynoffv{p}") as off:
                                e.reg_load(off, io["offsb"][0:1, 4 + p:5 + p])
                                e.dma_start(out=vts[:, :],
                                            in_=bass.AP(io["qkv_g"].tensor, off,
                                                        [[3 * D, SS], [1, 128]]),
                                            transpose=True).then_inc(sd, 16)
                        else:
                            e.dma_start(out=vts[:, :], in_=vt[p]).then_inc(sd, 16)
                        ds[0] += 16
                        e.wait_ge(sd, ds[0])
                base_tr = ntr
                with nc.Block() as blk:
                    @blk.tensor
                    def _(e):
                        for i in range(NKT_):
                            if base_tr + i >= 2:
                                e.wait_ge(s_tr, base_tr + i - 1)
                            e.transpose(pstr[(base_tr + i) % 2][:, :],
                                        vts[:, 128 * i:128 * (i + 1)],
                                        identb[:, :]).then_inc(s_mm, 1)
                    @blk.vector
                    def _(e):
                        for i in range(NKT_):
                            e.wait_ge(s_mm, base_tr + i + 1)
                            e.tensor_copy(vpt[2 * p][:, 65 * i:65 * i + 64],
                                          pstr[(base_tr + i) % 2][:, 0:64])
                            e.tensor_copy(vpt[2 * p + 1][:, 65 * i:65 * i + 64],
                                          pstr[(base_tr + i) % 2][:, 64:128]).then_inc(s_tr, 1)
                ntr += NKT_

        ps_s = [ctx.enter_context(nc.psum_tensor(f"ps_s{i}", [128, 1024], F32))
                for i in range(2)]
        ps_c = [ctx.enter_context(nc.psum_tensor(f"ps_c{i}", [128, 512], F32))
                for i in range(4)]
        cnt = {"mm": ntr, "exp": 0, "msk": 0, "av": 0, "ev": 0}

        def unit_global(base, K2, Q2, VT):
            plan = []
            for m in range(nsb):
                for j in range(8 * (m + 1)):
                    plan.append((len(plan), 128 * j, 1024 * m))
            msk_tgt, c = {}, cnt["msk"]
            for (i, tk0, sb0) in plan:
                if tk0 >= sb0:
                    c += 1
                msk_tgt[i] = c
            sb_last = {}
            idx = 0
            for m in range(nsb):
                idx += 8 * (m + 1)
                sb_last[m] = idx
            b_mm, b_exp, b_av, b_ev = cnt["mm"], cnt["exp"], cnt["av"], cnt["ev"]
            with nc.Block() as blk:
                @blk.tensor
                def _(e):
                    for (i, tk0, sb0) in plan:
                        buf, m = i % 2, sb0 // 1024
                        if i >= 2:
                            e.wait_ge(s_exp, b_exp + i - 1)
                        lhsT = K2[base:base + 64, tk0:tk0 + 128]
                        for hh in range(2):
                            lo = sb0 + 512 * hh
                            ins = e.matmul(ps_s[buf][:, 512 * hh:512 * (hh + 1)],
                                           lhsT, Q2[base:base + 64, lo:lo + 512],
                                           start=True, stop=True)
                        ins.then_inc(s_mm, 1)
                        first = (plan[i - 1][2] != sb0) if i > 0 else True
                        last = (plan[i + 1][2] != sb0) if i + 1 < len(plan) else True
                        if first and m >= 2:
                            e.wait_ge(s_ev, b_ev + 2 * (m - 1))
                        if tk0 >= sb0:
                            e.wait_ge(s_msk, msk_tgt[i])
                        else:
                            e.wait_ge(s_exp, b_exp + i + 1)
                        j = tk0 // 128
                        for hh in range(2):
                            ins = e.matmul(ps_c[(2 * m + hh) % 4][0:65, :],
                                           VT[:, 65 * j:65 * j + 65],
                                           pbuf[buf][:, 512 * hh:512 * (hh + 1)],
                                           start=first, stop=last)
                        ins.then_inc(s_av, 1)
                @blk.scalar
                def _(e):
                    for (i, tk0, sb0) in plan:
                        buf = i % 2
                        v0 = max(0, tk0 - sb0)
                        e.wait_ge(s_mm, b_mm + i + 1)
                        if i >= 2:
                            e.wait_ge(s_av, b_av + i - 1)
                        e.activation(pbuf[buf][:, v0:1024], ps_s[buf][:, v0:1024],
                                     AF.Exp, bias=b0col[:, 0:1], scale=0.125).then_inc(s_exp, 1)
                @blk.gpsimd
                def _(e):
                    for (i, tk0, sb0) in plan:
                        if tk0 < sb0:
                            continue
                        buf = i % 2
                        e.wait_ge(s_exp, b_exp + i + 1)
                        jj = (tk0 - sb0) // 128
                        if jj > 0:
                            e.memset(pbuf[buf][:, 0:128 * jj], 0.0)
                            e.drain()
                        e.tensor_tensor(pbuf[buf][:, 128 * jj:128 * (jj + 1)],
                                        pbuf[buf][:, 128 * jj:128 * (jj + 1)],
                                        m_lo[:, :], ALU.mult).then_inc(s_msk, 1)
                @blk.vector
                def _(e):
                    for m in range(nsb):
                        e.wait_ge(s_av, b_av + sb_last[m])
                        for hh in range(2):
                            bank = (2 * m + hh) % 4
                            lo = 1024 * m + 512 * hh
                            e.tensor_copy(ctxraw[:, lo:lo + 512], ps_c[bank][0:64, :])
                            e.tensor_copy(den[:, lo:lo + 512],
                                          ps_c[bank][64:65, :]).then_inc(s_ev, 1)
            cnt["mm"] = b_mm + len(plan)
            cnt["exp"] = b_exp + len(plan)
            cnt["msk"] = msk_tgt[len(plan) - 1]
            cnt["av"] = b_av + len(plan)
            cnt["ev"] = b_ev + 2 * nsb

        def unit_local(base, K2, Q2, VT):
            plan = [(j, 128 * j, min(512, SS - 128 * j)) for j in range(NKT_)]
            blk_last = {m: min(NKT_ - 1, 4 * m + 3) for m in range(nblocks)}
            navs = {}
            nav = 0
            for (j, tk0, width) in plan:
                off = tk0 % 512
                w1 = min(width, 512 - off)
                nav += 1 + (1 if width > w1 else 0)
                navs[j] = nav
            b_mm, b_exp, b_av, b_ev = cnt["mm"], cnt["exp"], cnt["av"], cnt["ev"]
            b_msk = cnt["msk"]
            with nc.Block() as blk:
                @blk.tensor
                def _(e):
                    for (i, tk0, width) in plan:
                        buf = i % 2
                        if i >= 2:
                            e.wait_ge(s_exp, b_exp + i - 1)
                        lhsT = K2[base:base + 64, tk0:tk0 + 128]
                        e.matmul(ps_s[buf][:, 0:width], lhsT,
                                 Q2[base:base + 64, tk0:tk0 + width],
                                 start=True, stop=True).then_inc(s_mm, 1)
                        e.wait_ge(s_msk, b_msk + i + 1)
                        A = tk0 // 512
                        off = tk0 % 512
                        w1 = min(width, 512 - off)
                        firstA = (i == max(0, 4 * A - 3))
                        lastA = (i == blk_last[A])
                        if firstA and A >= 4:
                            e.wait_ge(s_ev, b_ev + (A - 3))
                        e.matmul(ps_c[A % 4][0:65, off:off + w1],
                                 VT[:, 65 * i:65 * i + 65], pbuf[buf][:, 0:w1],
                                 start=firstA, stop=lastA).then_inc(s_av, 1)
                        if width > w1:
                            A2 = A + 1
                            firstA2 = (i == max(0, 4 * A2 - 3))
                            lastA2 = (i == blk_last[A2])
                            if firstA2 and A2 >= 4:
                                e.wait_ge(s_ev, b_ev + (A2 - 3))
                            e.matmul(ps_c[A2 % 4][0:65, 0:width - w1],
                                     VT[:, 65 * i:65 * i + 65], pbuf[buf][:, w1:width],
                                     start=firstA2, stop=lastA2).then_inc(s_av, 1)
                @blk.scalar
                def _(e):
                    for (i, tk0, width) in plan:
                        buf = i % 2
                        e.wait_ge(s_mm, b_mm + i + 1)
                        if i >= 2:
                            e.wait_ge(s_av, b_av + navs[i - 2])
                        e.activation(pbuf[buf][:, 0:width], ps_s[buf][:, 0:width],
                                     AF.Exp, bias=b0col[:, 0:1], scale=0.125).then_inc(s_exp, 1)
                @blk.gpsimd
                def _(e):
                    for (i, tk0, width) in plan:
                        buf = i % 2
                        e.wait_ge(s_exp, b_exp + i + 1)
                        ins = e.tensor_tensor(pbuf[buf][:, 0:128], pbuf[buf][:, 0:128],
                                              m_lo[:, :], ALU.mult)
                        if width > 256:
                            ins = e.tensor_tensor(pbuf[buf][:, 256:384],
                                                  pbuf[buf][:, 256:384],
                                                  m_hi[:, :], ALU.mult)
                        if width > 384:
                            ins = e.memset(pbuf[buf][:, 384:width], 0.0)
                        ins.then_inc(s_msk, 1)
                @blk.vector
                def _(e):
                    for (i, tk0, width) in plan:
                        for m in range(nblocks):
                            if blk_last[m] == i:
                                e.wait_ge(s_av, b_av + navs[i])
                                e.tensor_copy(ctxraw[:, 512 * m:512 * (m + 1)],
                                              ps_c[m % 4][0:64, :])
                                e.tensor_copy(den[:, 512 * m:512 * (m + 1)],
                                              ps_c[m % 4][64:65, :]).then_inc(s_ev, 1)
            cnt["mm"] = b_mm + len(plan)
            cnt["exp"] = b_exp + len(plan)
            cnt["msk"] = b_msk + len(plan)
            cnt["av"] = b_av + nav
            cnt["ev"] = b_ev + nblocks

        for u in range(4):
            p, base = u // 2, 64 * (u % 2)
            if u == 0:
                unit_global(base, ktb[p], qtb[p], vpt[u])
            else:
                unit_local(base, ktb[p], qtb[p], vpt[u])
            with nc.Block() as blk:
                @blk.vector
                def _(e):
                    e.reciprocal(den[:, :], den[:, :])
                    e.drain()
            b_mm, b_ev = cnt["mm"], cnt["ev"]
            with nc.Block() as blk:
                @blk.tensor
                def _(e):
                    for i in range(SS // 512):
                        if i >= 2:
                            e.wait_ge(s_ev, b_ev + i - 1)
                        e.matmul(ps_c[i % 2][0:64, :], ones32[:1, :],
                                 den[:1, 512 * i:512 * (i + 1)],
                                 start=True, stop=True).then_inc(s_mm, 1)
                @blk.scalar
                def _(e):
                    for i in range(SS // 512):
                        e.wait_ge(s_mm, b_mm + i + 1)
                        e.copy(rbc[:, 512 * i:512 * (i + 1)],
                               ps_c[i % 2][0:64, :]).then_inc(s_ev, 1)
            cnt["mm"] = b_mm + SS // 512
            cnt["ev"] = b_ev + SS // 512
            with nc.Block() as blk:
                @blk.vector
                def _(e):
                    e.tensor_tensor(ctxn[:, :], ctxn[:, :], rbc[:, :], ALU.mult)
            with nc.Block() as blk:
                @blk.sync
                def _(e):
                    e.dma_start(out=ctx_out[64 * u:64 * (u + 1), :],
                                in_=ctxn[:, :]).then_inc(sd, 16)
                    ds[0] += 16
                    e.wait_ge(sd, ds[0])



def build_C():
    nc = bass.Bass("TRN2", target_bir_lowering=False)
    io = {"ctxf": nc.dram_tensor("ctxf", [D, TPC], BF16, kind="ExternalInput").ap(),
          "wo": nc.dram_tensor("wo", [D, D], F32, kind="ExternalInput").ap(),
          "bo": nc.dram_tensor("bo", [1, D], F32, kind="ExternalInput").ap(),
          "x": nc.dram_tensor("x", [TPC, D], F32, kind="ExternalInput").ap(),
          "x1": nc.dram_tensor("x1", [TPC, D], F32, kind="ExternalOutput").ap()}
    emit_C(nc, io)
    return nc


def emit_C(nc, io):
    wo_in, bo_in, x_in, x1_out = io["wo"], io["bo"], io["x"], io["x1"]
    fused = "ctxn_g" in io
    if not fused:
        ctxf_in = io["ctxf"]

    with ExitStack() as ctx:
        ds, gs = [0], [0]
        sd = ctx.enter_context(nc.semaphore("sd"))
        sg = ctx.enter_context(nc.semaphore("sg"))
        smm = ctx.enter_context(nc.semaphore("smm"))
        sev = ctx.enter_context(nc.semaphore("sev"))
        ctxf = ctx.enter_context(nc.sbuf_tensor("ctxf_sb", [128, NC * TPC], BF16))
        wob = ctx.enter_context(nc.sbuf_tensor("wob", [128, NC * D], BF16))
        bias = ctx.enter_context(nc.sbuf_tensor("bias", [128, D], F32))
        brow = ctx.enter_context(nc.sbuf_tensor("brow", [1, D], F32))
        ones32 = ctx.enter_context(nc.sbuf_tensor("ones32", [1, 128], F32))
        y_sb = ctx.enter_context(nc.sbuf_tensor("y_sb", [128, NT * D], F32))
        ps = [ctx.enter_context(nc.psum_tensor(f"ps{n}", [128, 512], F32)) for n in range(4)]
        with nc.Block() as blk:
            @blk.sync
            def _(e):
                if fused:
                    with e.register("dynoffc") as off:
                        for c in range(NC):
                            e.reg_load(off, io["offsb"][0:1, 6 + c:7 + c])
                            e.dma_start(out=ctxf[:, c * TPC:(c + 1) * TPC],
                                        in_=bass.AP(io["ctxn_g"].tensor, off,
                                                    [[S, 128], [1, TPC]])).then_inc(sd, 16)
                            ds[0] += 16
                else:
                    for c in range(NC):
                        e.dma_start(out=ctxf[:, c * TPC:(c + 1) * TPC],
                                    in_=ctxf_in[c * 128:(c + 1) * 128, :]).then_inc(sd, 16)
                        ds[0] += 16
                e.dma_start(out=brow[:, :], in_=bo_in).then_inc(sd, 16)
                ds[0] += 16
                e.wait_ge(sd, ds[0])
            @blk.gpsimd
            def _(e):
                for c in range(NC):
                    e.dma_start(out=wob[:, c * D:(c + 1) * D],
                                in_=wo_in[c * 128:(c + 1) * 128, :]).then_inc(sg, 16)
                    gs[0] += 16
                e.wait_ge(sg, gs[0])
            @blk.vector
            def _(e):
                e.memset(ones32[:, :], 1.0)
        with nc.Block() as blk:
            @blk.tensor
            def _(e):
                for i in range(2):
                    e.matmul(ps[2 + i][:, :], ones32[:1, :],
                             brow[:1, i * 512:(i + 1) * 512],
                             start=True, stop=True).then_inc(smm, 1)
            @blk.vector
            def _(e):
                for i in range(2):
                    e.wait_ge(smm, i + 1)
                    e.tensor_copy(bias[:, i * 512:(i + 1) * 512], ps[2 + i][:, :])
        with nc.Block() as blk:
            @blk.tensor
            def _(e):
                for t in range(NT):
                    if t >= 2:
                        e.wait_ge(sev, 2 * (t - 1))
                    for c in range(NC):
                        lhsT = ctxf[:, c * TPC + t * 128:c * TPC + (t + 1) * 128]
                        for n in range(2):
                            ins = e.matmul(ps[2 * (t % 2) + n][:, :], lhsT,
                                           wob[:, c * D + n * 512:c * D + (n + 1) * 512],
                                           start=(c == 0), stop=(c == NC - 1))
                    ins.then_inc(smm, 1)
            @blk.vector
            def _(e):
                for t in range(NT):
                    e.wait_ge(smm, 2 + t + 1)
                    for n in range(2):
                        e.scalar_tensor_tensor(
                            y_sb[:, t * D + n * 512:t * D + (n + 1) * 512],
                            ps[2 * (t % 2) + n][:, :], 1.0,
                            bias[:, n * 512:(n + 1) * 512], ALU.mult,
                            ALU.add).then_inc(sev, 1)
        emit_ln_resid_streamx(nc, ctx, y_sb, x_in, x1_out, NT, "C")



def build_D():
    nc = bass.Bass("TRN2", target_bir_lowering=False)
    io = {"x": nc.dram_tensor("x", [TPC, D], F32, kind="ExternalInput").ap(),
          "halo": nc.dram_tensor("halo", [1, D], F32, kind="ExternalInput").ap(),
          "wrT": nc.dram_tensor("wrT", [D, D], F32, kind="ExternalInput").ap(),
          "wkT": nc.dram_tensor("wkT", [D, D], F32, kind="ExternalInput").ap(),
          "wvT": nc.dram_tensor("wvT", [D, D], F32, kind="ExternalInput").ap(),
          "mur": nc.dram_tensor("mur", [1, D], F32, kind="ExternalInput").ap(),
          "muk": nc.dram_tensor("muk", [1, D], F32, kind="ExternalInput").ap(),
          "x2": nc.dram_tensor("x2", [TPC, D], F32, kind="ExternalOutput").ap()}
    emit_D(nc, io)
    return nc


def emit_D(nc, io):
    x_in, x2_out = io["x"], io["x2"]
    wrT_in, wkT_in, wvT_in = io["wrT"], io["wkT"], io["wvT"]
    mur_in, muk_in = io["mur"], io["muk"]
    fused = "lr_g" in io
    if not fused:
        halo_in = io["halo"]
    xn_d = nc.dram_tensor("xn_d", [TPC, D], BF16)
    xp_d = nc.dram_tensor("xp_d", [TPC, D], BF16)

    with ExitStack() as ctx:
        ds, gs = [0], [0]
        sd = ctx.enter_context(nc.semaphore("sd"))
        sg = ctx.enter_context(nc.semaphore("sg"))
        smm = ctx.enter_context(nc.semaphore("smm"))
        sev = ctx.enter_context(nc.semaphore("sev"))
        cnt = {"mm": 0, "ev": 0}
        r_sb = ctx.enter_context(nc.sbuf_tensor("r_sb", [128, NT * D], BF16))
        k2T = ctx.enter_context(nc.sbuf_tensor("k2T", [128, NC * TPC], BF16))
        wbuf = ctx.enter_context(nc.sbuf_tensor("wbuf", [128, NC * D], BF16))
        ps = [ctx.enter_context(nc.psum_tensor(f"ps{n}", [128, 512], F32)) for n in range(4)]

        with ExitStack() as ph1:
            x_sb = ph1.enter_context(nc.sbuf_tensor("x_sb", [128, NT * D], F32))
            xbf = ph1.enter_context(nc.sbuf_tensor("xbf", [128, NT * D], BF16))
            h_sb = ph1.enter_context(nc.sbuf_tensor("h_sb", [1, D], F32))
            hst = ph1.enter_context(nc.sbuf_tensor("hst", [1, 4], F32))
            hbf = ph1.enter_context(nc.sbuf_tensor("hbf", [1, D], BF16))
            hscr = ph1.enter_context(nc.sbuf_tensor("hscr", [1, D], BF16))
            with nc.Block() as blk:
                @blk.sync
                def _(e):
                    emit_load_tiles(e, sd, x_sb, x_in, NT, ds)
                    if fused:
                        with e.register("dynoffh") as off:
                            e.reg_load(off, io["offsb"][0:1, 14:15])
                            e.dma_start(out=h_sb[:, :],
                                        in_=bass.AP(io["lr_g"].tensor, off,
                                                    [[D, 1], [1, D]])).then_inc(sd, 16)
                    else:
                        e.dma_start(out=h_sb[:, :], in_=halo_in).then_inc(sd, 16)
                    ds[0] += 16
                    e.wait_ge(sd, ds[0])
            emit_ln(nc, ph1, x_sb, NT, xbf, "D")
            with nc.Block() as blk:
                @blk.vector
                def _(e):
                    e.tensor_scalar(hbf[:, :], h_sb[:, :], 1.0, 0.0, ALU.mult, ALU.add,
                                    accum_out=hst[:, 0:1])
                @blk.scalar
                def _(e):
                    e.activation(hscr[:, :], h_sb[:, :], AF.Square, accum_out=hst[:, 1:2])
            with nc.Block() as blk:
                @blk.vector
                def _(e):
                    e.tensor_scalar(hst[:, 2:3], hst[:, 0:1], 1.0 / D, 0.0, ALU.mult, ALU.add)
                    e.drain()
                    e.tensor_tensor(hst[:, 0:1], hst[:, 2:3], hst[:, 2:3], ALU.mult)
                    e.tensor_scalar(hst[:, 1:2], hst[:, 1:2], 1.0 / D, EPS, ALU.mult, ALU.add)
                    e.drain()
                    e.scalar_tensor_tensor(hst[:, 1:2], hst[:, 1:2], 1.0, hst[:, 0:1],
                                           ALU.mult, ALU.subtract)
            with nc.Block() as blk:
                @blk.scalar
                def _(e):
                    e.activation(hst[:, 1:2], hst[:, 1:2], AF.Sqrt)
            with nc.Block() as blk:
                @blk.vector
                def _(e):
                    e.reciprocal(hst[:, 3:4], hst[:, 1:2])
                    e.drain()
                    e.tensor_scalar(hbf[:, :], h_sb[:, :], hst[:, 2:3], hst[:, 3:4],
                                    ALU.subtract, ALU.mult)
            with nc.Block() as blk:
                @blk.sync
                def _(e):
                    emit_store_tiles(e, sd, xn_d, xbf, NT, ds)
                    e.dma_start(out=xp_d[0:1, :], in_=hbf[:, :]).then_inc(sd, 16)
                    ds[0] += 16
                    for t in range(NT):
                        nrows = 128 if t < NT - 1 else 127
                        e.dma_start(out=xp_d[t * 128 + 1:t * 128 + 1 + nrows, :],
                                    in_=xbf[0:nrows, t * D:(t + 1) * D]).then_inc(sd, 16)
                        ds[0] += 16
                    e.wait_ge(sd, ds[0])

        with ExitStack() as ph2:
            xT = ph2.enter_context(nc.sbuf_tensor("xT", [128, NC * TPC], BF16))
            xP = ph2.enter_context(nc.sbuf_tensor("xP", [128, NC * TPC], BF16))
            xm = ph2.enter_context(nc.sbuf_tensor("xm", [128, NC * TPC], BF16))
            dmix = ph2.enter_context(nc.sbuf_tensor("dmix", [128, TPC], BF16))
            mucol = ph2.enter_context(nc.sbuf_tensor("mucol", [128, 2 * NC], F32))
            krelu = ph2.enter_context(nc.sbuf_tensor("krelu", [128, 4 * 512], F32))
            with nc.Block() as blk:
                @blk.sync
                def _(e):
                    for c in range(NC):
                        e.dma_start(out=xT[:, c * TPC:(c + 1) * TPC],
                                    in_=xn_d[:, c * 128:(c + 1) * 128],
                                    transpose=True).then_inc(sd, 16)
                        ds[0] += 16
                    for c in range(NC):
                        e.dma_start(out=xP[:, c * TPC:(c + 1) * TPC],
                                    in_=xp_d[:, c * 128:(c + 1) * 128],
                                    transpose=True).then_inc(sd, 16)
                        ds[0] += 16
                    e.wait_ge(sd, ds[0])
                @blk.gpsimd
                def _(e):
                    for c in range(NC):
                        col_load(e, sg, mucol[:, c:c + 1],
                                 mur_in[0:1, c * 128:(c + 1) * 128], gs)
                        col_load(e, sg, mucol[:, NC + c:NC + c + 1],
                                 muk_in[0:1, c * 128:(c + 1) * 128], gs)
                    for c in range(NC):
                        e.dma_start(out=wbuf[:, c * D:(c + 1) * D],
                                    in_=wrT_in[c * 128:(c + 1) * 128, :]).then_inc(sg, 16)
                        gs[0] += 16
                    e.wait_ge(sg, gs[0])

            def emit_mix(mucol_off):
                with nc.Block() as blk:
                    @blk.vector
                    def _(e):
                        for c in range(NC):
                            cur = xT[:, c * TPC:(c + 1) * TPC]
                            prev = xP[:, c * TPC:(c + 1) * TPC]
                            e.tensor_tensor(dmix[:, :], cur, prev, ALU.subtract)
                            e.drain()
                            e.scalar_tensor_tensor(xm[:, c * TPC:(c + 1) * TPC],
                                                   dmix[:, :],
                                                   mucol[:, mucol_off + c:mucol_off + c + 1],
                                                   prev, ALU.mult, ALU.add)
                            e.drain()

            emit_mix(0)
            b_mm, b_ev = cnt["mm"], cnt["ev"]
            with nc.Block() as blk:
                @blk.tensor
                def _(e):
                    for t in range(NT):
                        if t >= 2:
                            e.wait_ge(sev, b_ev + 2 * (t - 1))
                        for c in range(NC):
                            lhsT = xm[:, c * TPC + t * 128:c * TPC + (t + 1) * 128]
                            for n in range(2):
                                ins = e.matmul(ps[2 * (t % 2) + n][:, :], lhsT,
                                               wbuf[:, c * D + n * 512:c * D + (n + 1) * 512],
                                               start=(c == 0), stop=(c == NC - 1))
                        ins.then_inc(smm, 1)
                @blk.scalar
                def _(e):
                    for t in range(NT):
                        e.wait_ge(smm, b_mm + t + 1)
                        for n in range(2):
                            e.activation(r_sb[:, t * D + n * 512:t * D + (n + 1) * 512],
                                         ps[2 * (t % 2) + n][:, :],
                                         AF.Sigmoid).then_inc(sev, 1)
            cnt["mm"] = b_mm + NT
            cnt["ev"] = b_ev + 2 * NT
            with nc.Block() as blk:
                @blk.gpsimd
                def _(e):
                    for c in range(NC):
                        e.dma_start(out=wbuf[:, c * D:(c + 1) * D],
                                    in_=wkT_in[c * 128:(c + 1) * 128, :]).then_inc(sg, 16)
                        gs[0] += 16
                    e.wait_ge(sg, gs[0])
            emit_mix(NC)
            b_mm, b_ev = cnt["mm"], cnt["ev"]
            with nc.Block() as blk:
                @blk.tensor
                def _(e):
                    for dt in range(NC):
                        if dt >= 1:
                            e.wait_ge(sev, b_ev + dt)
                        for c in range(NC):
                            lhsT = wbuf[:, c * D + dt * 128:c * D + (dt + 1) * 128]
                            for q in range(4):
                                ins = e.matmul(ps[q][:, :], lhsT,
                                               xm[:, c * TPC + q * 512:c * TPC + (q + 1) * 512],
                                               start=(c == 0), stop=(c == NC - 1))
                        ins.then_inc(smm, 1)
                @blk.scalar
                def _(e):
                    for dt in range(NC):
                        e.wait_ge(smm, b_mm + dt + 1)
                        for q in range(4):
                            e.activation(krelu[:, q * 512:(q + 1) * 512], ps[q][:, :],
                                         AF.Relu)
                        e.drain()
                        e.activation(k2T[:, dt * TPC:(dt + 1) * TPC], krelu[:, :],
                                     AF.Square).then_inc(sev, 1)
            cnt["mm"] = b_mm + NC
            cnt["ev"] = b_ev + NC
        y_sb = ctx.enter_context(nc.sbuf_tensor("y_sb", [128, NT * D], F32))
        with nc.Block() as blk:
            @blk.gpsimd
            def _(e):
                for c in range(NC):
                    e.dma_start(out=wbuf[:, c * D:(c + 1) * D],
                                in_=wvT_in[c * 128:(c + 1) * 128, :]).then_inc(sg, 16)
                    gs[0] += 16
                e.wait_ge(sg, gs[0])
        b_mm, b_ev = cnt["mm"], cnt["ev"]
        with nc.Block() as blk:
            @blk.tensor
            def _(e):
                for t in range(NT):
                    if t >= 2:
                        e.wait_ge(sev, b_ev + 2 * (t - 1))
                    for c in range(NC):
                        lhsT = k2T[:, c * TPC + t * 128:c * TPC + (t + 1) * 128]
                        for n in range(2):
                            ins = e.matmul(ps[2 * (t % 2) + n][:, :], lhsT,
                                           wbuf[:, c * D + n * 512:c * D + (n + 1) * 512],
                                           start=(c == 0), stop=(c == NC - 1))
                    ins.then_inc(smm, 1)
            @blk.vector
            def _(e):
                for t in range(NT):
                    e.wait_ge(smm, b_mm + t + 1)
                    for n in range(2):
                        sl = slice(t * D + n * 512, t * D + (n + 1) * 512)
                        e.tensor_tensor(y_sb[:, sl], ps[2 * (t % 2) + n][:, :],
                                        r_sb[:, sl], ALU.mult).then_inc(sev, 1)
        cnt["mm"] = b_mm + NT
        cnt["ev"] = b_ev + 2 * NT
        emit_ln_resid_streamx(nc, ctx, y_sb, x_in, x2_out, NT, "D")



def build_E0():
    nc = bass.Bass("TRN2", target_bir_lowering=False)
    io = {"x": nc.dram_tensor("x", [TPC, D], F32, kind="ExternalInput").ap(),
          "wvT": nc.dram_tensor("wvT", [D, D], F32, kind="ExternalInput").ap(),
          "wrT": nc.dram_tensor("wrT", [D, D], F32, kind="ExternalInput").ap(),
          "td": nc.dram_tensor("td", [1, D], F32, kind="ExternalInput").ap(),
          "vT": nc.dram_tensor("vT", [D, TPC], BF16, kind="ExternalOutput").ap(),
          "rT": nc.dram_tensor("rT", [D, TPC], BF16, kind="ExternalOutput").ap(),
          "P": nc.dram_tensor("P", [128, NC], F32, kind="ExternalOutput").ap(),
          "Bb": nc.dram_tensor("Bb", [128, NC], F32, kind="ExternalOutput").ap()}
    emit_E0(nc, io)
    return nc


def emit_E0(nc, io):
    x_in, wvT_in, wrT_in, td_in = io["x"], io["wvT"], io["wrT"], io["td"]
    vT_out, rT_out = io["vT"], io["rT"]
    fused = "PBfull" in io
    if not fused:
        P_out, Bb_out = io["P"], io["Bb"]
    xn_d = nc.dram_tensor("xn_d", [TPC, D], BF16)

    with ExitStack() as ctx:
        ds, gs = [0], [0]
        sd = ctx.enter_context(nc.semaphore("sd"))
        sg = ctx.enter_context(nc.semaphore("sg"))
        smm = ctx.enter_context(nc.semaphore("smm"))
        sev = ctx.enter_context(nc.semaphore("sev"))
        vT_sb = ctx.enter_context(nc.sbuf_tensor("vT_sb", [128, NC * TPC], BF16))
        rT_sb = ctx.enter_context(nc.sbuf_tensor("rT_sb", [128, NC * TPC], BF16))
        wbuf = ctx.enter_context(nc.sbuf_tensor("wbuf", [128, NC * D], BF16))
        ps = [ctx.enter_context(nc.psum_tensor(f"ps{n}", [128, 512], F32)) for n in range(4)]
        with ExitStack() as ph1:
            x_sb = ph1.enter_context(nc.sbuf_tensor("x_sb", [128, NT * D], F32))
            xbf = ph1.enter_context(nc.sbuf_tensor("xbf", [128, NT * D], BF16))
            with nc.Block() as blk:
                @blk.sync
                def _(e):
                    emit_load_tiles(e, sd, x_sb, x_in, NT, ds)
                    e.wait_ge(sd, ds[0])
            emit_ln(nc, ph1, x_sb, NT, xbf, "E0")
            with nc.Block() as blk:
                @blk.sync
                def _(e):
                    emit_store_tiles(e, sd, xn_d, xbf, NT, ds)
                    e.wait_ge(sd, ds[0])
        xT = ctx.enter_context(nc.sbuf_tensor("xT", [128, NC * TPC], BF16))
        with nc.Block() as blk:
            @blk.sync
            def _(e):
                for c in range(NC):
                    e.dma_start(out=xT[:, c * TPC:(c + 1) * TPC],
                                in_=xn_d[:, c * 128:(c + 1) * 128],
                                transpose=True).then_inc(sd, 16)
                    ds[0] += 16
                e.wait_ge(sd, ds[0])
            @blk.gpsimd
            def _(e):
                for c in range(NC):
                    e.dma_start(out=wbuf[:, c * D:(c + 1) * D],
                                in_=wvT_in[c * 128:(c + 1) * 128, :]).then_inc(sg, 16)
                    gs[0] += 16
                e.wait_ge(sg, gs[0])
        cnt = {"mm": 0, "ev": 0}

        def fm_matmul(dst_sb, act):
            b_mm, b_ev = cnt["mm"], cnt["ev"]
            with nc.Block() as blk:
                @blk.tensor
                def _(e):
                    for dt in range(NC):
                        if dt >= 1:
                            e.wait_ge(sev, b_ev + dt)
                        for c in range(NC):
                            lhsT = wbuf[:, c * D + dt * 128:c * D + (dt + 1) * 128]
                            for q in range(4):
                                ins = e.matmul(ps[q][:, :], lhsT,
                                               xT[:, c * TPC + q * 512:c * TPC + (q + 1) * 512],
                                               start=(c == 0), stop=(c == NC - 1))
                        ins.then_inc(smm, 1)
                @blk.scalar
                def _(e):
                    for dt in range(NC):
                        e.wait_ge(smm, b_mm + dt + 1)
                        for q in range(4):
                            ins = e.activation(
                                dst_sb[:, dt * TPC + q * 512:dt * TPC + (q + 1) * 512],
                                ps[q][:, :], act)
                        ins.then_inc(sev, 1)
            cnt["mm"] = b_mm + NC
            cnt["ev"] = b_ev + NC

        fm_matmul(vT_sb, AF.Copy)
        with nc.Block() as blk:
            @blk.gpsimd
            def _(e):
                for c in range(NC):
                    e.dma_start(out=wbuf[:, c * D:(c + 1) * D],
                                in_=wrT_in[c * 128:(c + 1) * 128, :]).then_inc(sg, 16)
                    gs[0] += 16
                e.wait_ge(sg, gs[0])
        fm_matmul(rT_sb, AF.Sigmoid)
        tdc = ctx.enter_context(nc.sbuf_tensor("tdc", [128, NC], F32))
        cc = ctx.enter_context(nc.sbuf_tensor("cc", [128, 2 * NC], F32))
        onesT = ctx.enter_context(nc.sbuf_tensor("onesT", [128, TPC], F32))
        c1bc = ctx.enter_context(nc.sbuf_tensor("c1bc", [128, TPC], F32))
        cv = ctx.enter_context(nc.sbuf_tensor("cv", [128, TPC], F32))
        Sa = ctx.enter_context(nc.sbuf_tensor("Sa", [128, TPC], F32))
        PB = ctx.enter_context(nc.sbuf_tensor("PB", [128, 2 * NC], F32))
        zz = ctx.enter_context(nc.sbuf_tensor("zzpb", [128, 2 * NC], F32))
        with nc.Block() as blk:
            @blk.gpsimd
            def _(e):
                for c in range(NC):
                    col_load(e, sg, tdc[:, c:c + 1], td_in[0:1, c * 128:(c + 1) * 128], gs)
                e.wait_ge(sg, gs[0])
            @blk.vector
            def _(e):
                e.memset(onesT[:, :], 1.0)
                e.memset(zz[:, :], 0.0)
        with nc.Block() as blk:
            @blk.vector
            def _(e):
                e.tensor_scalar_max(cc[:, 0:NC], tdc[:, :], 0.0)
                e.tensor_scalar_min(cc[:, NC:2 * NC], tdc[:, :], 0.0)
        with nc.Block() as blk:
            @blk.scalar
            def _(e):
                e.activation(cc[:, 0:NC], cc[:, 0:NC], AF.Exp, scale=-1.0)
                e.drain()
                e.activation(cc[:, NC:2 * NC], cc[:, NC:2 * NC], AF.Exp)
        for c in range(NC):
            with nc.Block() as blk:
                @blk.vector
                def _(e):
                    e.tensor_scalar(c1bc[:, :], onesT[:, :], cc[:, c:c + 1], 0.0,
                                    ALU.mult, ALU.add)
                    e.tensor_scalar(cv[:, :], vT_sb[:, c * TPC:(c + 1) * TPC],
                                    cc[:, NC + c:NC + c + 1], 0.0, ALU.mult, ALU.add)
                    e.drain()
                    e.tensor_tensor_scan(Sa[:, :], c1bc[:, :], cv[:, :], 0.0,
                                         ALU.mult, ALU.add)
                    e.drain()
                    e.tensor_copy(PB[:, c:c + 1], Sa[:, TPC - 1:TPC])
                    e.tensor_scalar(cv[:, :], onesT[:, :], cc[:, NC + c:NC + c + 1], 0.0,
                                    ALU.mult, ALU.add)
                    e.drain()
                    e.tensor_tensor_scan(Sa[:, :], c1bc[:, :], cv[:, :], 0.0,
                                         ALU.mult, ALU.add)
                    e.drain()
                    e.tensor_copy(PB[:, NC + c:NC + c + 1], Sa[:, TPC - 1:TPC])
        with nc.Block() as blk:
            @blk.sync
            def _(e):
                for c in range(NC):
                    e.dma_start(out=vT_out[c * 128:(c + 1) * 128, :],
                                in_=vT_sb[:, c * TPC:(c + 1) * TPC]).then_inc(sd, 16)
                    e.dma_start(out=rT_out[c * 128:(c + 1) * 128, :],
                                in_=rT_sb[:, c * TPC:(c + 1) * TPC]).then_inc(sd, 16)
                    ds[0] += 32
                if fused:
                    e.dma_start(out=io["PBfull"][0:128, :], in_=PB[:, :]).then_inc(sd, 16)
                    e.dma_start(out=io["PBfull"][128:256, :],
                                in_=zz[:, :]).then_inc(sd, 16)
                else:
                    e.dma_start(out=P_out, in_=PB[:, 0:NC]).then_inc(sd, 16)
                    e.dma_start(out=Bb_out, in_=PB[:, NC:2 * NC]).then_inc(sd, 16)
                ds[0] += 32
                e.wait_ge(sd, ds[0])



def build_E():
    nc = bass.Bass("TRN2", target_bir_lowering=False)
    io = {"x": nc.dram_tensor("x", [TPC, D], F32, kind="ExternalInput").ap(),
          "vT": nc.dram_tensor("vT", [D, TPC], BF16, kind="ExternalInput").ap(),
          "rT": nc.dram_tensor("rT", [D, TPC], BF16, kind="ExternalInput").ap(),
          "Pp": nc.dram_tensor("Pp", [128, 3 * NC], F32, kind="ExternalInput").ap(),
          "Bp": nc.dram_tensor("Bp", [128, 3 * NC], F32, kind="ExternalInput").ap(),
          "td": nc.dram_tensor("td", [1, D], F32, kind="ExternalInput").ap(),
          "tf": nc.dram_tensor("tf", [1, D], F32, kind="ExternalInput").ap(),
          "woT": nc.dram_tensor("woT", [D, D], F32, kind="ExternalInput").ap(),
          "x3": nc.dram_tensor("x3", [TPC, D], F32, kind="ExternalOutput").ap()}
    emit_E(nc, io)
    return nc


def emit_E(nc, io):
    x_in, vT_in, rT_in = io["x"], io["vT"], io["rT"]
    td_in, tf_in, woT_in, x3_out = io["td"], io["tf"], io["woT"], io["x3"]
    fused = "PB_g" in io
    if not fused:
        Pp_in, Bp_in = io["Pp"], io["Bp"]

    with ExitStack() as ctx:
        ds, gs = [0], [0]
        sd = ctx.enter_context(nc.semaphore("sd"))
        sg = ctx.enter_context(nc.semaphore("sg"))
        smm = ctx.enter_context(nc.semaphore("smm"))
        sev = ctx.enter_context(nc.semaphore("sev"))
        y_sb = ctx.enter_context(nc.sbuf_tensor("y_sb", [128, NT * D], F32))
        rwT = ctx.enter_context(nc.sbuf_tensor("rwT", [128, NC * TPC], BF16))
        wbuf = ctx.enter_context(nc.sbuf_tensor("wbuf", [128, NC * D], BF16))
        ps = [ctx.enter_context(nc.psum_tensor(f"ps{n}", [128, 512], F32)) for n in range(4)]
        with ExitStack() as ph1:
            vT_sb = ph1.enter_context(nc.sbuf_tensor("vT_sb", [128, NC * TPC], BF16))
            consts = ph1.enter_context(nc.sbuf_tensor("consts", [128, 8 * NC], F32))
            Pp = ph1.enter_context(nc.sbuf_tensor("Pp_sb", [128, 3 * NC], F32))
            Bp = ph1.enter_context(nc.sbuf_tensor("Bp_sb", [128, 3 * NC], F32))
            tdc = ph1.enter_context(nc.sbuf_tensor("tdc", [128, 2 * NC], F32))
            onesT = ph1.enter_context(nc.sbuf_tensor("onesT", [128, TPC], F32))
            c1bc = ph1.enter_context(nc.sbuf_tensor("c1bc", [128, TPC], F32))
            cv = ph1.enter_context(nc.sbuf_tensor("cv", [128, TPC], F32))
            Sa = ph1.enter_context(nc.sbuf_tensor("Sa", [128, TPC + 1], F32))
            Sb = ph1.enter_context(nc.sbuf_tensor("Sb", [128, TPC + 1], F32))
            num = ph1.enter_context(nc.sbuf_tensor("num", [128, TPC], F32))
            dde = ph1.enter_context(nc.sbuf_tensor("dde", [128, TPC], F32))
            C1, C2, E1, E2E, AAIN, BBIN, D2048, E2R = (0, NC, 2 * NC, 3 * NC, 4 * NC,
                                                       5 * NC, 6 * NC, 7 * NC)
            with nc.Block() as blk:
                @blk.sync
                def _(e):
                    for c in range(NC):
                        e.dma_start(out=vT_sb[:, c * TPC:(c + 1) * TPC],
                                    in_=vT_in[c * 128:(c + 1) * 128, :]).then_inc(sd, 16)
                        e.dma_start(out=rwT[:, c * TPC:(c + 1) * TPC],
                                    in_=rT_in[c * 128:(c + 1) * 128, :]).then_inc(sd, 16)
                        ds[0] += 32
                    if fused:
                        with e.register("dynoffp") as off:
                            for k in range(3):
                                e.reg_load(off, io["offsb"][0:1, 15 + k:16 + k])
                                e.dma_start(out=Pp[:, k * NC:(k + 1) * NC],
                                            in_=bass.AP(io["PB_g"].tensor, off,
                                                        [[2 * NC, 128], [1, NC]])
                                            ).then_inc(sd, 16)
                                e.reg_load(off, io["offsb"][0:1, 18 + k:19 + k])
                                e.dma_start(out=Bp[:, k * NC:(k + 1) * NC],
                                            in_=bass.AP(io["PB_g"].tensor, off,
                                                        [[2 * NC, 128], [1, NC]])
                                            ).then_inc(sd, 16)
                                ds[0] += 32
                    else:
                        e.dma_start(out=Pp[:, :], in_=Pp_in).then_inc(sd, 16)
                        e.dma_start(out=Bp[:, :], in_=Bp_in).then_inc(sd, 16)
                        ds[0] += 32
                    e.wait_ge(sd, ds[0])
                @blk.gpsimd
                def _(e):
                    for c in range(NC):
                        col_load(e, sg, tdc[:, c:c + 1], td_in[0:1, c * 128:(c + 1) * 128], gs)
                        col_load(e, sg, tdc[:, NC + c:NC + c + 1],
                                 tf_in[0:1, c * 128:(c + 1) * 128], gs)
                    e.wait_ge(sg, gs[0])
                @blk.vector
                def _(e):
                    e.memset(onesT[:, :], 1.0)
            with nc.Block() as blk:
                @blk.vector
                def _(e):
                    e.tensor_scalar_max(consts[:, C1:C1 + NC], tdc[:, 0:NC], 0.0)
                    e.tensor_scalar_min(consts[:, C2:C2 + NC], tdc[:, 0:NC], 0.0)
                    e.tensor_scalar_max(consts[:, E1:E1 + NC], tdc[:, NC:2 * NC], 0.0)
                    e.tensor_scalar_min(consts[:, E2E:E2E + NC], tdc[:, NC:2 * NC], 0.0)
                    e.drain()
                    e.tensor_scalar(consts[:, D2048:D2048 + NC], consts[:, C1:C1 + NC],
                                    -float(TPC), 0.0, ALU.mult, ALU.add)
            with nc.Block() as blk:
                @blk.scalar
                def _(e):
                    e.activation(consts[:, C1:C1 + NC], consts[:, C1:C1 + NC],
                                 AF.Exp, scale=-1.0)
                    e.drain()
                    e.activation(consts[:, C2:C2 + NC], consts[:, C2:C2 + NC], AF.Exp)
                    e.drain()
                    e.activation(consts[:, E1:E1 + NC], consts[:, E1:E1 + NC],
                                 AF.Exp, scale=-1.0)
                    e.drain()
                    e.activation(consts[:, E2E:E2E + NC], consts[:, E2E:E2E + NC], AF.Exp)
                    e.drain()
                    e.activation(consts[:, D2048:D2048 + NC], consts[:, D2048:D2048 + NC],
                                 AF.Exp)
            with nc.Block() as blk:
                @blk.vector
                def _(e):
                    e.tensor_copy(consts[:, E2R:E2R + NC], consts[:, E2E:E2E + NC])
                    e.drain()
                    e.tensor_scalar(consts[:, E2E:E2E + NC], consts[:, E2E:E2E + NC],
                                    1.0, 1e-6, ALU.mult, ALU.add)
                    e.drain()
                    for (dst, Psrc) in ((AAIN, Pp), (BBIN, Bp)):
                        e.tensor_tensor(consts[:, dst:dst + NC], Psrc[:, 2 * NC:3 * NC],
                                        consts[:, D2048:D2048 + NC], ALU.mult)
                        e.drain()
                        e.tensor_tensor(consts[:, dst:dst + NC], consts[:, dst:dst + NC],
                                        Psrc[:, NC:2 * NC], ALU.add)
                        e.drain()
                        e.tensor_tensor(consts[:, dst:dst + NC], consts[:, dst:dst + NC],
                                        consts[:, D2048:D2048 + NC], ALU.mult)
                        e.drain()
                        e.tensor_tensor(consts[:, dst:dst + NC], consts[:, dst:dst + NC],
                                        Psrc[:, 0:NC], ALU.add)
                        e.drain()
            for c in range(NC):
                with nc.Block() as blk:
                    @blk.vector
                    def _(e):
                        e.tensor_scalar(c1bc[:, :], onesT[:, :],
                                        consts[:, C1 + c:C1 + c + 1], 0.0, ALU.mult, ALU.add)
                        e.tensor_scalar(cv[:, :], vT_sb[:, c * TPC:(c + 1) * TPC],
                                        consts[:, C2 + c:C2 + c + 1], 0.0, ALU.mult, ALU.add)
                        e.tensor_copy(Sa[:, 0:1], consts[:, AAIN + c:AAIN + c + 1])
                        e.drain()
                        e.tensor_tensor_scan(Sa[:, 1:TPC + 1], c1bc[:, :], cv[:, :],
                                             consts[:, AAIN + c:AAIN + c + 1],
                                             ALU.mult, ALU.add)
                        e.tensor_scalar(cv[:, :], onesT[:, :],
                                        consts[:, C2 + c:C2 + c + 1], 0.0, ALU.mult, ALU.add)
                        e.tensor_copy(Sb[:, 0:1], consts[:, BBIN + c:BBIN + c + 1])
                        e.drain()
                        e.tensor_tensor_scan(Sb[:, 1:TPC + 1], c1bc[:, :], cv[:, :],
                                             consts[:, BBIN + c:BBIN + c + 1],
                                             ALU.mult, ALU.add)
                        e.drain()
                        e.tensor_scalar(num[:, :], Sa[:, 0:TPC],
                                        consts[:, E1 + c:E1 + c + 1], 0.0, ALU.mult, ALU.add)
                        e.drain()
                        e.scalar_tensor_tensor(num[:, :], vT_sb[:, c * TPC:(c + 1) * TPC],
                                               consts[:, E2R + c:E2R + c + 1],
                                               num[:, :], ALU.mult, ALU.add)
                        e.tensor_scalar(dde[:, :], Sb[:, 0:TPC],
                                        consts[:, E1 + c:E1 + c + 1],
                                        consts[:, E2E + c:E2E + c + 1], ALU.mult, ALU.add)
                        e.drain()
                        e.reciprocal(dde[:, :], dde[:, :])
                        e.drain()
                        e.tensor_tensor(num[:, :], num[:, :], dde[:, :], ALU.mult)
                        e.drain()
                        e.tensor_tensor(rwT[:, c * TPC:(c + 1) * TPC], num[:, :],
                                        rwT[:, c * TPC:(c + 1) * TPC], ALU.mult)
        with nc.Block() as blk:
            @blk.gpsimd
            def _(e):
                for c in range(NC):
                    e.dma_start(out=wbuf[:, c * D:(c + 1) * D],
                                in_=woT_in[c * 128:(c + 1) * 128, :]).then_inc(sg, 16)
                    gs[0] += 16
                e.wait_ge(sg, gs[0])
        with nc.Block() as blk:
            @blk.tensor
            def _(e):
                for t in range(NT):
                    if t >= 2:
                        e.wait_ge(sev, 2 * (t - 1))
                    for c in range(NC):
                        lhsT = rwT[:, c * TPC + t * 128:c * TPC + (t + 1) * 128]
                        for n in range(2):
                            ins = e.matmul(ps[2 * (t % 2) + n][:, :], lhsT,
                                           wbuf[:, c * D + n * 512:c * D + (n + 1) * 512],
                                           start=(c == 0), stop=(c == NC - 1))
                    ins.then_inc(smm, 1)
            @blk.scalar
            def _(e):
                for t in range(NT):
                    e.wait_ge(smm, t + 1)
                    for n in range(2):
                        e.copy(y_sb[:, t * D + n * 512:t * D + (n + 1) * 512],
                               ps[2 * (t % 2) + n][:, :]).then_inc(sev, 1)
        emit_ln_resid_streamx(nc, ctx, y_sb, x_in, x3_out, NT, "E")



def build_G(debug=False):
    nc = bass.Bass("TRN2", target_bir_lowering=False)
    io = {"x": nc.dram_tensor("x", [TPC, D], F32, kind="ExternalInput").ap(),
          "k0": nc.dram_tensor("k0", [2 * D, 4096], U8, kind="ExternalInput").ap(),
          "k1": nc.dram_tensor("k1", [4096, 4096], U8, kind="ExternalInput").ap(),
          "k2": nc.dram_tensor("k2", [4096, D], U8, kind="ExternalInput").ap(),
          "sv": nc.dram_tensor("sv", [1, 3], F32, kind="ExternalInput").ap(),
          "out": nc.dram_tensor("out", [TPC, D], F32, kind="ExternalOutput").ap()}
    emit_G(nc, io, debug=debug)
    return nc


def emit_G(nc, io, debug=False):
    x_in = io["x"]
    k0_in, k1_in, k2_in, s_in = io["k0"], io["k1"], io["k2"], io["sv"]
    out_d = io["out"]
    if debug:
        dbg_h1 = nc.dram_tensor("dbg_h1", [32 * 128, 1024], BF16, kind="ExternalOutput").ap()
        dbg_h2 = nc.dram_tensor("dbg_h2", [32 * 128, 1024], BF16, kind="ExternalOutput").ap()
        dbg_y = nc.dram_tensor("dbg_y", [D, TPC], BF16, kind="ExternalOutput").ap()
        dbg_xn = nc.dram_tensor("dbg_xn", [D, TPC], BF16, kind="ExternalOutput").ap()
    xn_d = nc.dram_tensor("xn_d", [TPC, D], BF16)
    w0_d = nc.dram_tensor("w0_d", [D, 4096], BF16)
    w1_d = nc.dram_tensor("w1_d", [32 * 32 * 128, 128], BF16)
    w2_d = nc.dram_tensor("w2_d", [8 * 32 * 128, 128], BF16)
    y_d = nc.dram_tensor("y_d", [D, TPC], BF16)
    N0 = 4096
    TB = 1024
    NB = TPC // TB

    with ExitStack() as ctx:
        ds, gs = [0], [0]
        sd = ctx.enter_context(nc.semaphore("sd"))
        sg = ctx.enter_context(nc.semaphore("sg"))
        smm = ctx.enter_context(nc.semaphore("smm"))
        sev = ctx.enter_context(nc.semaphore("sev"))
        scv = ctx.enter_context(nc.semaphore("scv"))
        syp = [ctx.enter_context(nc.semaphore(f"syp{j}")) for j in range(2)]
        sypn = [0, 0]
        slt = [ctx.enter_context(nc.semaphore(f"slt{j}")) for j in range(4)]
        sltn = [0, 0, 0, 0]
        scl = [ctx.enter_context(nc.semaphore(f"scl{j}")) for j in range(2)]
        scln = [0, 0]
        sgs = [ctx.enter_context(nc.semaphore(f"sgs{j}")) for j in range(2)]
        sgsn = [0, 0]
        cnt = {"mm": 0, "ev": 0, "cv": 0}
        scol = ctx.enter_context(nc.sbuf_tensor("scol", [128, 4], F32))
        hashpad = ctx.enter_context(nc.sbuf_tensor("hashpad", [1, 8], F32))
        srow = ctx.enter_context(nc.sbuf_tensor("srow", [1, 3], F32))
        ones32 = ctx.enter_context(nc.sbuf_tensor("ones32", [1, 128], F32))
        ybf = ctx.enter_context(nc.sbuf_tensor("ybf", [128, NT * D], BF16))
        with ExitStack() as ph1:
            x_sb = ph1.enter_context(nc.sbuf_tensor("x_sb", [128, NT * D], F32))
            xbf = ph1.enter_context(nc.sbuf_tensor("xbf", [128, NT * D], BF16))
            with nc.Block() as blk:
                @blk.sync
                def _(e):
                    emit_load_tiles(e, sd, x_sb, x_in, NT, ds)
                    e.dma_start(out=srow[:, :], in_=s_in).then_inc(sd, 16)
                    ds[0] += 16
                    e.wait_ge(sd, ds[0])
                @blk.vector
                def _(e):
                    e.memset(ones32[:, :], 1.0)
                    e.memset(hashpad[:, :], 3.0)
            emit_ln(nc, ph1, x_sb, NT, xbf, "G")
            with nc.Block() as blk:
                @blk.sync
                def _(e):
                    emit_store_tiles(e, sd, xn_d, xbf, NT, ds)
                    e.wait_ge(sd, ds[0])
        psb = ctx.enter_context(nc.psum_tensor("psb", [128, 4], F32))
        with nc.Block() as blk:
            @blk.tensor
            def _(e):
                e.matmul(psb[:, 0:3], ones32[:1, :], srow[:1, :],
                         start=True, stop=True).then_inc(smm, 1)
            @blk.vector
            def _(e):
                e.wait_ge(smm, 1)
                e.tensor_scalar(scol[:, 0:3], psb[:, 0:3], 2.0, 0.0, ALU.mult, ALU.add)
        cnt["mm"] = 1
        with ExitStack() as cvc:
            u8a = cvc.enter_context(nc.sbuf_tensor("u8a", [128, 2 * 4096], U8))
            u8b = cvc.enter_context(nc.sbuf_tensor("u8b", [128, 2 * 4096], U8))
            wcv = cvc.enter_context(nc.sbuf_tensor("wcv", [128, 2 * 4096], BF16))
            b_cv = cnt["cv"]
            b_scl, b_sgs = list(scln), list(sgsn)
            with nc.Block() as blk:
                @blk.sync
                def _(e):
                    for c in range(NC):
                        if c >= 2:
                            e.wait_ge(scv, b_cv + c - 1)
                        e.dma_start(out=u8a[:, (c % 2) * 4096:(c % 2 + 1) * 4096],
                                    in_=k0_in[c * 128:(c + 1) * 128, :]
                                    ).then_inc(scl[c % 2], 16)
                        e.dma_start(out=u8b[:, (c % 2) * 4096:(c % 2 + 1) * 4096],
                                    in_=k0_in[D + c * 128:D + (c + 1) * 128, :]
                                    ).then_inc(scl[c % 2], 16)
                        scln[c % 2] += 32
                @blk.vector
                def _(e):
                    for c in range(NC):
                        e.wait_ge(scl[c % 2], b_scl[c % 2] + 32 * (c // 2 + 1))
                        if c >= 2:
                            e.wait_ge(sgs[c % 2], b_sgs[c % 2] + 16 * (c // 2))
                        e.tensor_tensor(wcv[:, (c % 2) * 4096:(c % 2 + 1) * 4096],
                                        u8a[:, (c % 2) * 4096:(c % 2 + 1) * 4096],
                                        u8b[:, (c % 2) * 4096:(c % 2 + 1) * 4096],
                                        ALU.subtract).then_inc(scv, 1)
                @blk.gpsimd
                def _(e):
                    for c in range(NC):
                        e.wait_ge(scv, b_cv + c + 1)
                        e.dma_start(out=w0_d[c * 128:(c + 1) * 128, :],
                                    in_=wcv[:, (c % 2) * 4096:(c % 2 + 1) * 4096]
                                    ).then_inc(sgs[c % 2], 16)
                        sgsn[c % 2] += 16
            cnt["cv"] = b_cv + NC
            for (ksrc, wdst, nchunks, ntiles) in ((k1_in, w1_d, 32, 32),
                                                  (k2_in, w2_d, 32, 8)):
                b_cv = cnt["cv"]
                b_scl, b_sgs = list(scln), list(sgsn)
                width = ntiles * 128
                with nc.Block() as blk:
                    @blk.sync
                    def _(e):
                        for c in range(nchunks):
                            if c >= 2:
                                e.wait_ge(scv, b_cv + c - 1)
                            e.dma_start(out=u8a[:, (c % 2) * width:(c % 2) * width + width],
                                        in_=ksrc[c * 128:(c + 1) * 128, :]
                                        ).then_inc(scl[c % 2], 16)
                            scln[c % 2] += 16
                    @blk.vector
                    def _(e):
                        for c in range(nchunks):
                            e.wait_ge(scl[c % 2], b_scl[c % 2] + 16 * (c // 2 + 1))
                            if c >= 2:
                                e.wait_ge(sgs[c % 2], b_sgs[c % 2] + 16 * ntiles * (c // 2))
                            e.tensor_scalar(wcv[:, (c % 2) * width:(c % 2) * width + width],
                                            u8a[:, (c % 2) * width:(c % 2) * width + width],
                                            2.0, 1.0, ALU.mult,
                                            ALU.subtract).then_inc(scv, 1)
                    @blk.gpsimd
                    def _(e):
                        for c in range(nchunks):
                            e.wait_ge(scv, b_cv + c + 1)
                            for nt_ in range(ntiles):
                                e.dma_start(
                                    out=wdst[(nt_ * nchunks + c) * 128:(nt_ * nchunks + c + 1) * 128, :],
                                    in_=wcv[:, (c % 2) * width + nt_ * 128:(c % 2) * width + (nt_ + 1) * 128]
                                    ).then_inc(sgs[c % 2], 16)
                                sgsn[c % 2] += 16
                cnt["cv"] = b_cv + nchunks
        xT = ctx.enter_context(nc.sbuf_tensor("xT", [128, NC * TPC], BF16))
        with nc.Block() as blk:
            @blk.gpsimd
            def _(e):
                e.wait_ge(sgs[0], sgsn[0])
                e.wait_ge(sgs[1], sgsn[1])
            @blk.sync
            def _(e):
                for c in range(NC):
                    e.dma_start(out=xT[:, c * TPC:(c + 1) * TPC],
                                in_=xn_d[:, c * 128:(c + 1) * 128],
                                transpose=True).then_inc(sd, 16)
                    ds[0] += 16
                e.wait_ge(sd, ds[0])
        ps4 = [ctx.enter_context(nc.psum_tensor(f"ps4{n}", [128, 512], F32))
               for n in range(4)]
        for tb in range(NB):
            t0 = tb * TB
            with ExitStack() as tbs:
                h1fm = tbs.enter_context(nc.sbuf_tensor(f"h1fm{tb}", [128, 32 * TB], BF16))
                with ExitStack() as w0s:
                    w0b = w0s.enter_context(nc.sbuf_tensor(f"w0b{tb}", [128, NC * 4096], BF16))
                    with nc.Block() as blk:
                        @blk.sync
                        def _(e):
                            for c in range(NC):
                                e.dma_start(out=w0b[:, c * 4096:(c + 1) * 4096],
                                            in_=w0_d[c * 128:(c + 1) * 128, :]).then_inc(sd, 16)
                                ds[0] += 16
                            e.wait_ge(sd, ds[0])
                    b_mm, b_ev = cnt["mm"], cnt["ev"]
                    with nc.Block() as blk:
                        @blk.tensor
                        def _(e):
                            for n0 in range(32):
                                if n0 >= 2:
                                    e.wait_ge(sev, b_ev + n0 - 1)
                                for c in range(NC):
                                    lhsT = w0b[:, c * 4096 + n0 * 128:c * 4096 + (n0 + 1) * 128]
                                    for q in range(2):
                                        ins = e.matmul(
                                            ps4[2 * (n0 % 2) + q][:, :], lhsT,
                                            xT[:, c * TPC + t0 + q * 512:c * TPC + t0 + (q + 1) * 512],
                                            start=(c == 0), stop=(c == NC - 1))
                                ins.then_inc(smm, 1)
                        @blk.scalar
                        def _(e):
                            for n0 in range(32):
                                e.wait_ge(smm, b_mm + n0 + 1)
                                for q in range(2):
                                    ins = e.activation(
                                        h1fm[:, n0 * TB + q * 512:n0 * TB + (q + 1) * 512],
                                        ps4[2 * (n0 % 2) + q][:, :], AF.Relu,
                                        scale=scol[:, 0:1])
                                ins.then_inc(sev, 1)
                    cnt["mm"] = b_mm + 32
                    cnt["ev"] = b_ev + 32
                h2fm = tbs.enter_context(nc.sbuf_tensor(f"h2fm{tb}", [128, 32 * TB], BF16))
                w1st = tbs.enter_context(nc.sbuf_tensor(f"w1st{tb}", [128, 4 * 128], BF16))
                b_mm, b_ev, b_sd = cnt["mm"], cnt["ev"], ds[0]
                b_slt = list(sltn)
                with nc.Block() as blk:
                    @blk.sync
                    def _(e):
                        for i in range(32 * 32):
                            if i >= 4:
                                e.wait_ge(smm, b_mm + i - 3)
                            e.dma_start(out=w1st[:, (i % 4) * 128:(i % 4 + 1) * 128],
                                        in_=w1_d[i * 128:(i + 1) * 128, :]).then_inc(slt[i % 4], 16)
                            sltn[i % 4] += 16
                        for j in range(4):
                            e.wait_ge(slt[j], sltn[j])
                    @blk.tensor
                    def _(e):
                        for i in range(32 * 32):
                            n1, c = divmod(i, 32)
                            e.wait_ge(slt[i % 4], b_slt[i % 4] + 16 * (i // 4 + 1))
                            if c == 0 and n1 >= 2:
                                e.wait_ge(sev, b_ev + n1 - 1)
                            lhsT = w1st[:, (i % 4) * 128:(i % 4 + 1) * 128]
                            for q in range(2):
                                ins = e.matmul(
                                    ps4[2 * (n1 % 2) + q][:, :], lhsT,
                                    h1fm[:, c * TB + q * 512:c * TB + (q + 1) * 512],
                                    start=(c == 0), stop=(c == 31))
                            ins.then_inc(smm, 1)
                    @blk.scalar
                    def _(e):
                        for n1 in range(32):
                            e.wait_ge(smm, b_mm + 32 * (n1 + 1))
                            for q in range(2):
                                ins = e.activation(
                                    h2fm[:, n1 * TB + q * 512:n1 * TB + (q + 1) * 512],
                                    ps4[2 * (n1 % 2) + q][:, :], AF.Relu,
                                    scale=scol[:, 1:2])
                            ins.then_inc(sev, 1)
                cnt["mm"] = b_mm + 32 * 32
                cnt["ev"] = b_ev + 32
                if debug and tb == 0:
                    with nc.Block() as blk:
                        @blk.sync
                        def _(e):
                            for n0 in range(32):
                                e.dma_start(out=dbg_h1[n0 * 128:(n0 + 1) * 128, :],
                                            in_=h1fm[:, n0 * TB:(n0 + 1) * TB]).then_inc(sd, 16)
                                e.dma_start(out=dbg_h2[n0 * 128:(n0 + 1) * 128, :],
                                            in_=h2fm[:, n0 * TB:(n0 + 1) * TB]).then_inc(sd, 16)
                                ds[0] += 32
                            e.wait_ge(sd, ds[0])
                ystage = tbs.enter_context(nc.sbuf_tensor(f"yst{tb}", [128, 2 * TB], BF16))
                w2st = tbs.enter_context(nc.sbuf_tensor(f"w2st{tb}", [128, 4 * 128], BF16))
                b_mm, b_ev = cnt["mm"], cnt["ev"]
                b_slt = list(sltn)
                b_syp = list(sypn)
                with nc.Block() as blk:
                    @blk.sync
                    def _(e):
                        for i in range(8 * 32):
                            if i >= 4:
                                e.wait_ge(smm, b_mm + i - 3)
                            e.dma_start(out=w2st[:, (i % 4) * 128:(i % 4 + 1) * 128],
                                        in_=w2_d[i * 128:(i + 1) * 128, :]).then_inc(slt[i % 4], 16)
                            sltn[i % 4] += 16
                            if i % 32 == 31:
                                dt = i // 32
                                e.wait_ge(sev, b_ev + dt + 1)
                                e.dma_start(out=y_d[dt * 128:(dt + 1) * 128, t0:t0 + TB],
                                            in_=ystage[:, (dt % 2) * TB:(dt % 2 + 1) * TB]
                                            ).then_inc(syp[dt % 2], 16)
                                sypn[dt % 2] += 16
                        for j in range(4):
                            e.wait_ge(slt[j], sltn[j])
                        e.wait_ge(syp[0], sypn[0])
                        e.wait_ge(syp[1], sypn[1])
                    @blk.tensor
                    def _(e):
                        for i in range(8 * 32):
                            dt, c = divmod(i, 32)
                            e.wait_ge(slt[i % 4], b_slt[i % 4] + 16 * (i // 4 + 1))
                            if c == 0 and dt >= 2:
                                e.wait_ge(sev, b_ev + dt - 1)
                            lhsT = w2st[:, (i % 4) * 128:(i % 4 + 1) * 128]
                            for q in range(2):
                                ins = e.matmul(
                                    ps4[2 * (dt % 2) + q][:, :], lhsT,
                                    h2fm[:, c * TB + q * 512:c * TB + (q + 1) * 512],
                                    start=(c == 0), stop=(c == 31))
                            ins.then_inc(smm, 1)
                    @blk.scalar
                    def _(e):
                        for dt in range(8):
                            e.wait_ge(smm, b_mm + 32 * (dt + 1))
                            if dt >= 2:
                                e.wait_ge(syp[dt % 2], b_syp[dt % 2] + 16 * (dt // 2))
                            for q in range(2):
                                ins = e.activation(
                                    ystage[:, (dt % 2) * TB + q * 512:(dt % 2) * TB + (q + 1) * 512],
                                    ps4[2 * (dt % 2) + q][:, :], AF.Copy,
                                    scale=scol[:, 2:3])
                            ins.then_inc(sev, 1)
                cnt["mm"] = b_mm + 8 * 32
                cnt["ev"] = b_ev + 8
        if debug:
            with nc.Block() as blk:
                @blk.sync
                def _(e):
                    for c in range(NC):
                        e.dma_start(out=dbg_y[c * 128:(c + 1) * 128, :],
                                    in_=y_d[c * 128:(c + 1) * 128, :]).then_inc(sd, 16)
                        e.dma_start(out=dbg_xn[c * 128:(c + 1) * 128, :],
                                    in_=xT[:, c * TPC:(c + 1) * TPC]).then_inc(sd, 16)
                        ds[0] += 32
                    e.wait_ge(sd, ds[0])
        with nc.Block() as blk:
            @blk.sync
            def _(e):
                for t in range(NT):
                    e.dma_start(out=ybf[:, t * D:(t + 1) * D],
                                in_=y_d[:, t * 128:(t + 1) * 128],
                                transpose=True).then_inc(sd, 16)
                    ds[0] += 16
                e.wait_ge(sd, ds[0])
        emit_ln_resid_streamx(nc, ctx, ybf, x_in, out_d, NT, "G2")



def build_full(dbg=False):
    nc = bass.Bass("TRN2", target_bir_lowering=False)
    I32 = mybir.dt.int32
    ein = lambda n, sh, dt: nc.dram_tensor(n, sh, dt, kind="ExternalInput").ap()
    x_in = ein("x", [TPC, D], F32)
    wq_in = ein("wq", [D, 3 * D], F32)
    bq_in = ein("bq", [1, 3 * D], F32)
    wo_in = ein("wo", [D, D], F32)
    bo_in = ein("bo", [1, D], F32)
    wrT_in = ein("wrT", [D, D], F32)
    wkT_in = ein("wkT", [D, D], F32)
    wvT_in = ein("wvT", [D, D], F32)
    mur_in = ein("mur", [1, D], F32)
    muk_in = ein("muk", [1, D], F32)
    wv2_in = ein("wv2T", [D, D], F32)
    wr2_in = ein("wr2T", [D, D], F32)
    td_in = ein("td", [1, D], F32)
    tf_in = ein("tf", [1, D], F32)
    wo2_in = ein("wo2T", [D, D], F32)
    k0_in = ein("k0", [2 * D, 4096], U8)
    k1_in = ein("k1", [4096, 4096], U8)
    k2_in = ein("k2", [4096, D], U8)
    sv_in = ein("sv", [1, 3], F32)
    offs_in = ein("offs", [1, 32], I32)
    out_d = nc.dram_tensor("out", [TPC, D], F32, kind="ExternalOutput").ap()

    kindh = "ExternalOutput" if dbg else "Internal"
    qkv_d = nc.dram_tensor("qkv_d", [TPC, 3 * D], BF16).ap()
    qkv_g = nc.dram_tensor("qkv_g", [NCORES * TPC, 3 * D], BF16).ap()
    ctxn_d = nc.dram_tensor("ctxn_d", [4 * 64, S], BF16).ap()
    ctxn_g = nc.dram_tensor("ctxn_g", [NCORES * 4 * 64, S], BF16).ap()
    x1_d = nc.dram_tensor("x1_d", [TPC, D], F32, kind=kindh).ap()
    lr_in = nc.dram_tensor("lr_in", [2, D], F32).ap()
    lr_g = nc.dram_tensor("lr_g", [2 * NCORES, D], F32).ap()
    x2_d = nc.dram_tensor("x2_d", [TPC, D], F32, kind=kindh).ap()
    vT_d = nc.dram_tensor("vT_d", [D, TPC], BF16).ap()
    rT_d = nc.dram_tensor("rT_d", [D, TPC], BF16).ap()
    PB_in = nc.dram_tensor("PB_in", [256, 2 * NC], F32).ap()
    PB_g = nc.dram_tensor("PB_g", [256 * NCORES, 2 * NC], F32).ap()
    x3_d = nc.dram_tensor("x3_d", [TPC, D], F32, kind=kindh).ap()

    grp = [list(range(NCORES))]
    with ExitStack() as prog:
        set_pfx(nc, "F_")
        nc._sem_stack = prog
        osb = prog.enter_context(nc.sbuf_tensor("osb", [1, 32], I32))
        soff = prog.enter_context(nc.semaphore("soff"))
        ccs = prog.enter_context(nc.semaphore("ccs"))
        with nc.Block() as blk:
            @blk.sync
            def _(e):
                e.dma_start(out=osb[:, :], in_=offs_in).then_inc(soff, 16)
                e.wait_ge(soff, 16)

        set_pfx(nc, "A_")
        emit_A(nc, {"x": x_in, "wq": wq_in, "bq": bq_in, "qkv": qkv_d})
        set_pfx(nc, "c1_")
        with nc.Block() as blk:
            @blk.gpsimd
            def _(e):
                e.collective_compute("AllGather", ALU.bypass, replica_groups=grp,
                                     ins=[qkv_d], outs=[qkv_g]).then_inc(ccs, 1)
                e.wait_ge(ccs, 1)

        set_pfx(nc, "B_")
        emit_B(nc, {"qkv_g": qkv_g, "offsb": osb, "ctxn": ctxn_d}, SS=S)
        set_pfx(nc, "c2_")
        with nc.Block() as blk:
            @blk.gpsimd
            def _(e):
                e.collective_compute("AllGather", ALU.bypass, replica_groups=grp,
                                     ins=[ctxn_d], outs=[ctxn_g]).then_inc(ccs, 1)
                e.wait_ge(ccs, 2)

        set_pfx(nc, "C_")
        emit_C(nc, {"wo": wo_in, "bo": bo_in, "x": x_in, "x1": x1_d,
                    "ctxn_g": ctxn_g, "offsb": osb})
        set_pfx(nc, "c3_")
        with ExitStack() as hctx:
            z1 = hctx.enter_context(nc.sbuf_tensor("z1d", [1, D], F32))
            slr = prog.enter_context(nc.semaphore("slr"))
            with nc.Block() as blk:
                @blk.vector
                def _(e):
                    e.memset(z1[:, :], 0.0)
            with nc.Block() as blk:
                @blk.sync
                def _(e):
                    e.dma_start(out=lr_in[0:1, :],
                                in_=x1_d[TPC - 1:TPC, :]).then_inc(slr, 16)
                    e.dma_start(out=lr_in[1:2, :], in_=z1[:, :]).then_inc(slr, 16)
                    e.wait_ge(slr, 32)
            with nc.Block() as blk:
                @blk.gpsimd
                def _(e):
                    e.collective_compute("AllGather", ALU.bypass, replica_groups=grp,
                                         ins=[lr_in], outs=[lr_g]).then_inc(ccs, 1)
                    e.wait_ge(ccs, 3)

        set_pfx(nc, "D_")
        emit_D(nc, {"x": x1_d, "wrT": wrT_in, "wkT": wkT_in, "wvT": wvT_in,
                    "mur": mur_in, "muk": muk_in, "x2": x2_d,
                    "lr_g": lr_g, "offsb": osb})
        set_pfx(nc, "E0_")
        emit_E0(nc, {"x": x2_d, "wvT": wv2_in, "wrT": wr2_in, "td": td_in,
                     "vT": vT_d, "rT": rT_d, "PBfull": PB_in})
        set_pfx(nc, "c4_")
        with nc.Block() as blk:
            @blk.gpsimd
            def _(e):
                e.collective_compute("AllGather", ALU.bypass, replica_groups=grp,
                                     ins=[PB_in], outs=[PB_g]).then_inc(ccs, 1)
                e.wait_ge(ccs, 4)

        set_pfx(nc, "E_")
        emit_E(nc, {"x": x2_d, "vT": vT_d, "rT": rT_d, "td": td_in, "tf": tf_in,
                    "woT": wo2_in, "x3": x3_d, "PB_g": PB_g, "offsb": osb})
        set_pfx(nc, "G_")
        emit_G(nc, {"x": x3_d, "k0": k0_in, "k1": k1_in, "k2": k2_in, "sv": sv_in,
                    "out": out_d})
        if dbg:
            set_pfx(nc, "")
            qg_s = nc.dram_tensor("qg_s", [128, 3 * D], BF16,
                                  kind="ExternalOutput").ap()
            cg_s = nc.dram_tensor("cg_s", [128, S], BF16,
                                  kind="ExternalOutput").ap()
            set_pfx(nc, "dbg_")
            with ExitStack() as dctx:
                t1s = dctx.enter_context(nc.sbuf_tensor("smp1", [128, 3 * D], BF16))
                t2s = dctx.enter_context(nc.sbuf_tensor("smp2", [128, S], BF16))
                sdd = prog.enter_context(nc.semaphore("sdd"))
                with nc.Block() as blk:
                    @blk.sync
                    def _(e):
                        with e.register("dqs") as off:
                            e.reg_load(off, osb[0:1, 21:22])
                            e.dma_start(out=t1s[:, :],
                                        in_=bass.AP(qkv_g.tensor, off,
                                                    [[64 * 3 * D, 128], [1, 3 * D]])
                                        ).then_inc(sdd, 16)
                        e.dma_start(out=t2s[:, :],
                                    in_=bass.AP(ctxn_g.tensor, 0,
                                                [[16 * S, 128], [1, S]])
                                    ).then_inc(sdd, 16)
                        e.wait_ge(sdd, 32)
                        e.dma_start(out=qg_s, in_=t1s[:, :]).then_inc(sdd, 16)
                        e.dma_start(out=cg_s, in_=t2s[:, :]).then_inc(sdd, 16)
                        e.wait_ge(sdd, 64)
        nc._sem_stack = None
    return nc


HEADPERM = [h for g in range(NG) for h in (g, NG + 3 * g, NG + 3 * g + 1,
                                           NG + 3 * g + 2)]
COLPERM = np.concatenate([np.arange(HD * h, HD * (h + 1)) for h in HEADPERM])


def core_offsets(c):
    bb, g = c // 4, c % 4
    off = np.zeros(32, np.int32)
    qbase = S * bb * 3 * D
    for p in range(2):
        off[0 + p] = qbase + D + 256 * g + 128 * p
        off[2 + p] = qbase + 0 + 256 * g + 128 * p
        off[4 + p] = qbase + 2 * D + 256 * g + 128 * p
    for cc in range(NC):
        off[6 + cc] = (256 * (4 * bb + cc // 2) + 128 * (cc % 2)) * S + g * TPC
    off[14] = D if g == 0 else 2 * (c - 1) * D
    off[21] = S * bb * 3 * D
    for k in (1, 2, 3):
        if g >= k:
            off[14 + k] = 256 * (c - k) * 2 * NC
            off[17 + k] = 256 * (c - k) * 2 * NC + NC
        else:
            off[14 + k] = 128 * 2 * NC
            off[17 + k] = 128 * 2 * NC + NC
    return off.reshape(1, 32)



_CACHE = {}


def _run(name, builder, in_maps):
    if name not in _CACHE:
        _CACHE[name] = builder()
    res = run_bass_kernel_spmd(_CACHE[name], in_maps, core_ids=list(range(NCORES)))
    return res.results


def kernel_unfused(L0_attn_x, L0_attn_w_qkv, L0_attn_b_qkv, L0_attn_w_out,
           L0_attn_b_out,
           L0_ffn_W_r, L0_ffn_W_k, L0_ffn_W_v, L0_ffn_mu_r, L0_ffn_mu_k,
           L1_attn_time_decay, L1_attn_time_first, L1_attn_key_w, L1_attn_value_w,
           L1_attn_recep_w, L1_attn_out_w, L1_ffn_k0, L1_ffn_k1, L1_ffn_k2,
           L1_ffn_s0, L1_ffn_s1, L1_ffn_s2):
    C = np.ascontiguousarray
    xflat = np.asarray(L0_attn_x, np.float32).reshape(B * S, D)
    xsl = [C(xflat[c * TPC:(c + 1) * TPC]) for c in range(NCORES)]
    wq = C(np.asarray(L0_attn_w_qkv, np.float32))
    bq = C(np.asarray(L0_attn_b_qkv, np.float32).reshape(1, 3 * D))

    resA = _run("A", build_A, [{"x": xsl[c], "wq": wq, "bq": bq} for c in range(NCORES)])
    qkv = np.concatenate([resA[c]["qkv"] for c in range(NCORES)], 0).reshape(B, S, 3 * D)

    inB = []
    for c in range(NCORES):
        m = {}
        for p in range(2):
            for nm, off in (("kt", D), ("qt", 0), ("vt", 2 * D)):
                arr = np.empty((128, S), BF)
                for s2 in range(2):
                    bb, hh = CORE_UNITS[c][2 * p + s2]
                    arr[64 * s2:64 * s2 + 64] = qkv[bb, :, off + HD * hh:off + HD * (hh + 1)].T
                m[f"{nm}{p}"] = C(arr)
        inB.append(m)
    resB = _run("B", build_B, inB)
    ctx_fm = np.empty((B, D, S), BF)
    for c in range(NCORES):
        for ui, (bb, hh) in enumerate(CORE_UNITS[c]):
            ctx_fm[bb, HD * hh:HD * (hh + 1), :] = resB[c]["ctxn"][64 * ui:64 * ui + 64]

    wo = C(np.asarray(L0_attn_w_out, np.float32))
    bo = C(np.asarray(L0_attn_b_out, np.float32).reshape(1, D))
    inC = [{"ctxf": C(ctx_fm[c // 4][:, (c % 4) * TPC:(c % 4 + 1) * TPC]),
            "wo": wo, "bo": bo, "x": xsl[c]} for c in range(NCORES)]
    resC = _run("C", build_C, inC)
    x1 = [np.asarray(resC[c]["x1"]) for c in range(NCORES)]

    wrT = C(np.asarray(L0_ffn_W_r, np.float32).T)
    wkT = C(np.asarray(L0_ffn_W_k, np.float32).T)
    wvT = C(np.asarray(L0_ffn_W_v, np.float32).T)
    mur = C(np.asarray(L0_ffn_mu_r, np.float32).reshape(1, D))
    muk = C(np.asarray(L0_ffn_mu_k, np.float32).reshape(1, D))
    zrow = np.zeros((1, D), np.float32)
    inD = []
    for c in range(NCORES):
        halo = zrow if c % 4 == 0 else C(x1[c - 1][TPC - 1:TPC, :])
        inD.append({"x": C(x1[c]), "halo": halo, "wrT": wrT, "wkT": wkT, "wvT": wvT,
                    "mur": mur, "muk": muk})
    resD = _run("D", build_D, inD)
    x2 = [np.asarray(resD[c]["x2"]) for c in range(NCORES)]

    wvT2 = C(np.asarray(L1_attn_value_w, np.float32).T)
    wrT2 = C(np.asarray(L1_attn_recep_w, np.float32).T)
    td = C(np.asarray(L1_attn_time_decay, np.float32).reshape(1, D))
    tf = C(np.asarray(L1_attn_time_first, np.float32).reshape(1, D))
    inE0 = [{"x": C(x2[c]), "wvT": wvT2, "wrT": wrT2, "td": td} for c in range(NCORES)]
    resE0 = _run("E0", build_E0, inE0)

    woT2 = C(np.asarray(L1_attn_out_w, np.float32).T)
    z128 = np.zeros((128, NC), np.float32)
    inE = []
    for c in range(NCORES):
        Pp = np.concatenate([np.asarray(resE0[c - k]["P"]) if c % 4 >= k else z128
                             for k in (1, 2, 3)], 1)
        Bp = np.concatenate([np.asarray(resE0[c - k]["Bb"]) if c % 4 >= k else z128
                             for k in (1, 2, 3)], 1)
        inE.append({"x": C(x2[c]), "vT": C(np.asarray(resE0[c]["vT"])),
                    "rT": C(np.asarray(resE0[c]["rT"])),
                    "Pp": C(Pp), "Bp": C(Bp), "td": td, "tf": tf, "woT": woT2})
    resE = _run("E", build_E, inE)
    x3 = [np.asarray(resE[c]["x3"]) for c in range(NCORES)]

    k0 = np.asarray(L1_ffn_k0).view(np.uint8)
    k1 = np.asarray(L1_ffn_k1).view(np.uint8)
    k2 = np.asarray(L1_ffn_k2).view(np.uint8)
    sv = C(np.stack([np.float32(L1_ffn_s0), np.float32(L1_ffn_s1),
                     np.float32(L1_ffn_s2)]).reshape(1, 3))
    inG = [{"x": C(x3[c]), "k0": C(k0), "k1": C(k1), "k2": C(k2), "sv": sv}
           for c in range(NCORES)]
    resG = _run("G", build_G, inG)
    out = np.concatenate([np.asarray(resG[c]["out"], np.float32)
                          for c in range(NCORES)], 0)
    return out.reshape(B, S, D)


def kernel(L0_attn_x, L0_attn_w_qkv, L0_attn_b_qkv, L0_attn_w_out, L0_attn_b_out,
           L0_ffn_W_r, L0_ffn_W_k, L0_ffn_W_v, L0_ffn_mu_r, L0_ffn_mu_k,
           L1_attn_time_decay, L1_attn_time_first, L1_attn_key_w, L1_attn_value_w,
           L1_attn_recep_w, L1_attn_out_w, L1_ffn_k0, L1_ffn_k1, L1_ffn_k2,
           L1_ffn_s0, L1_ffn_s1, L1_ffn_s2):
    C = np.ascontiguousarray
    xflat = np.asarray(L0_attn_x, np.float32).reshape(B * S, D)
    perm3 = np.concatenate([sec * D + COLPERM for sec in range(3)])
    wq = C(np.asarray(L0_attn_w_qkv, np.float32)[:, perm3])
    bq = C(np.asarray(L0_attn_b_qkv, np.float32).reshape(1, 3 * D)[:, perm3])
    wo = C(np.asarray(L0_attn_w_out, np.float32)[COLPERM, :])
    bo = C(np.asarray(L0_attn_b_out, np.float32).reshape(1, D))
    wrT = C(np.asarray(L0_ffn_W_r, np.float32).T)
    wkT = C(np.asarray(L0_ffn_W_k, np.float32).T)
    wvT = C(np.asarray(L0_ffn_W_v, np.float32).T)
    mur = C(np.asarray(L0_ffn_mu_r, np.float32).reshape(1, D))
    muk = C(np.asarray(L0_ffn_mu_k, np.float32).reshape(1, D))
    wv2T = C(np.asarray(L1_attn_value_w, np.float32).T)
    wr2T = C(np.asarray(L1_attn_recep_w, np.float32).T)
    td = C(np.asarray(L1_attn_time_decay, np.float32).reshape(1, D))
    tf = C(np.asarray(L1_attn_time_first, np.float32).reshape(1, D))
    wo2T = C(np.asarray(L1_attn_out_w, np.float32).T)
    k0 = C(np.asarray(L1_ffn_k0).view(np.uint8))
    k1 = C(np.asarray(L1_ffn_k1).view(np.uint8))
    k2 = C(np.asarray(L1_ffn_k2).view(np.uint8))
    sv = C(np.stack([np.float32(L1_ffn_s0), np.float32(L1_ffn_s1),
                     np.float32(L1_ffn_s2)]).reshape(1, 3))
    inF = []
    for c in range(NCORES):
        inF.append({"x": C(xflat[c * TPC:(c + 1) * TPC]), "wq": wq, "bq": bq,
                    "wo": wo, "bo": bo, "wrT": wrT, "wkT": wkT, "wvT": wvT,
                    "mur": mur, "muk": muk, "wv2T": wv2T, "wr2T": wr2T,
                    "td": td, "tf": tf, "wo2T": wo2T,
                    "k0": k0, "k1": k1, "k2": k2, "sv": sv,
                    "offs": core_offsets(c)})
    res = _run("F", build_full, inF)
    out = np.concatenate([np.asarray(res[c]["out"], np.float32)
                          for c in range(NCORES)], 0)
    return out.reshape(B, S, D)


# version 03
# speedup vs baseline: 1.3085x; 1.3085x over previous
import sys
sys.path.insert(0, "/opt/trn_rl_repo")
import numpy as np
import ml_dtypes
from contextlib import ExitStack, nullcontext

import concourse.bass as bass
import concourse.mybir as mybir
from concourse.bass_utils import run_bass_kernel_spmd

F32 = mybir.dt.float32
BF16 = mybir.dt.bfloat16
U8 = mybir.dt.uint8
AF = mybir.ActivationFunctionType
ALU = mybir.AluOpType
BF = ml_dtypes.bfloat16

NCORES = 8
B, S, D = 2, 8192, 1024
TPC = S * B // NCORES
NT = TPC // 128
NC = D // 128
NKT = S // 128
H, HD, NG, WIN = 16, 64, 4, 256
EPS = 1e-5
B0 = 40.0

GLOBAL_UNITS = [(b, h) for b in range(B) for h in range(NG)]
LOCAL_UNITS = [(b, h) for b in range(B) for h in range(NG, H)]
CORE_UNITS = [[GLOBAL_UNITS[c]] + LOCAL_UNITS[3 * c:3 * c + 3] for c in range(NCORES)]



def emit_stats(nc, ctx, src_sb, nt, stats, tag):
    scrA = ctx.enter_context(nc.sbuf_tensor(f"stA{tag}", [128, D], BF16))
    scrB = ctx.enter_context(nc.sbuf_tensor(f"stB{tag}", [128, D], BF16))
    SU, SQ, MU, RS = 0, nt, 2 * nt, 3 * nt
    with nc.Block() as blk:
        @blk.vector
        def _(e):
            for t in range(nt):
                if t:
                    e.drain()
                e.tensor_scalar(scrA[:, :], src_sb[:, t * D:(t + 1) * D], 1.0, 0.0,
                                ALU.mult, ALU.add, accum_out=stats[:, SU + t:SU + t + 1])
        @blk.scalar
        def _(e):
            for t in range(nt):
                if t:
                    e.drain()
                e.activation(scrB[:, :], src_sb[:, t * D:(t + 1) * D], AF.Square,
                             accum_out=stats[:, SQ + t:SQ + t + 1])
    with nc.Block() as blk:
        @blk.vector
        def _(e):
            e.tensor_scalar(stats[:, MU:MU + nt], stats[:, SU:SU + nt], 1.0 / D, 0.0,
                            ALU.mult, ALU.add)
            e.drain()
            e.tensor_tensor(stats[:, SU:SU + nt], stats[:, MU:MU + nt],
                            stats[:, MU:MU + nt], ALU.mult)
            e.tensor_scalar(stats[:, SQ:SQ + nt], stats[:, SQ:SQ + nt], 1.0 / D, EPS,
                            ALU.mult, ALU.add)
            e.drain()
            e.scalar_tensor_tensor(stats[:, SQ:SQ + nt], stats[:, SQ:SQ + nt], 1.0,
                                   stats[:, SU:SU + nt], ALU.mult, ALU.subtract)
    with nc.Block() as blk:
        @blk.scalar
        def _(e):
            e.activation(stats[:, SQ:SQ + nt], stats[:, SQ:SQ + nt], AF.Sqrt)
    with nc.Block() as blk:
        @blk.vector
        def _(e):
            e.reciprocal(stats[:, RS:RS + nt], stats[:, SQ:SQ + nt])
            e.drain()


def emit_ln(nc, ctx, x_sb, nt, out_bf, tag):
    stats = ctx.enter_context(nc.sbuf_tensor(f"lnst{tag}", [128, 4 * nt], F32))
    emit_stats(nc, ctx, x_sb, nt, stats, tag)
    with nc.Block() as blk:
        @blk.vector
        def _(e):
            for t in range(nt):
                e.tensor_scalar(out_bf[:, t * D:(t + 1) * D], x_sb[:, t * D:(t + 1) * D],
                                stats[:, 2 * nt + t:2 * nt + t + 1],
                                stats[:, 3 * nt + t:3 * nt + t + 1],
                                ALU.subtract, ALU.mult)


def emit_ln_resid_streamx(nc, ctx, y_sb, x_dram, out_dram, nt, tag):
    stats = ctx.enter_context(nc.sbuf_tensor(f"lrst{tag}", [128, 4 * nt], F32))
    emit_stats(nc, ctx, y_sb, nt, stats, tag + "r")
    xb = [ctx.enter_context(nc.sbuf_tensor(f"lrx{tag}{i}", [128, D], F32)) for i in range(2)]
    ob = [ctx.enter_context(nc.sbuf_tensor(f"lro{tag}{i}", [128, D], F32)) for i in range(2)]
    tmp = ctx.enter_context(nc.sbuf_tensor(f"lrt{tag}", [128, D], F32))
    sld = [ctx.enter_context(nc.semaphore(f"lrl{tag}{i}")) for i in range(2)]
    scp = ctx.enter_context(nc.semaphore(f"lrc{tag}"))
    sst = [ctx.enter_context(nc.semaphore(f"lrs{tag}{i}")) for i in range(2)]
    with nc.Block() as blk:
        @blk.sync
        def _(e):
            for t in range(nt):
                if t >= 2:
                    e.wait_ge(scp, t - 1)
                    e.dma_start(out=out_dram[(t - 2) * 128:(t - 1) * 128, :],
                                in_=ob[t % 2][:, :]).then_inc(sst[t % 2], 16)
                e.dma_start(out=xb[t % 2][:, :],
                            in_=x_dram[t * 128:(t + 1) * 128, :]).then_inc(sld[t % 2], 16)
            for t in (nt - 2, nt - 1):
                e.wait_ge(scp, t + 1)
                e.dma_start(out=out_dram[t * 128:(t + 1) * 128, :],
                            in_=ob[t % 2][:, :]).then_inc(sst[t % 2], 16)
            e.wait_ge(sst[0], 16 * (nt - nt // 2))
            e.wait_ge(sst[1], 16 * (nt // 2))
        @blk.vector
        def _(e):
            for t in range(nt):
                e.wait_ge(sld[t % 2], 16 * (t // 2 + 1))
                if t >= 2:
                    e.wait_ge(sst[t % 2], 16 * (t // 2))
                e.tensor_scalar(tmp[:, :], y_sb[:, t * D:(t + 1) * D],
                                stats[:, 2 * nt + t:2 * nt + t + 1],
                                stats[:, 3 * nt + t:3 * nt + t + 1],
                                ALU.subtract, ALU.mult)
                e.drain()
                e.scalar_tensor_tensor(ob[t % 2][:, :], tmp[:, :], 1.0,
                                       xb[t % 2][:, :], ALU.mult,
                                       ALU.add).then_inc(scp, 1)
                e.drain()


def emit_load_tiles(e, sem, x_sb, src, nt, ds):
    for t in range(nt):
        e.dma_start(out=x_sb[:, t * D:(t + 1) * D],
                    in_=src[t * 128:(t + 1) * 128, :]).then_inc(sem, 16)
        ds[0] += 16


def emit_store_tiles(e, sem, dst, x_sb, nt, ds):
    for t in range(nt):
        e.dma_start(out=dst[t * 128:(t + 1) * 128, :],
                    in_=x_sb[:, t * D:(t + 1) * D]).then_inc(sem, 16)
        ds[0] += 16


def col_load(e, sem, dst_col, src_row, ds):
    e.dma_start(out=dst_col, in_=src_row.rearrange("o p -> p o")).then_inc(sem, 16)
    ds[0] += 16


def wrap_names(nc):
    if getattr(nc, "_name_wrapped", False):
        return
    nc._name_wrapped = True
    nc._cur_pfx = ""
    for meth in ("sbuf_tensor", "psum_tensor", "dram_tensor"):
        orig = getattr(nc, meth)
        def mk(orig):
            def f(name, *a, **k):
                return orig(nc._cur_pfx + name, *a, **k)
            return f
        setattr(nc, meth, mk(orig))
    orig_sem = nc.semaphore
    def sem_f(name, *a, **k):
        cm = orig_sem(nc._cur_pfx + name, *a, **k)
        st = getattr(nc, "_sem_stack", None)
        if st is None:
            return cm
        return nullcontext(st.enter_context(cm))
    nc.semaphore = sem_f


def set_pfx(nc, pfx):
    wrap_names(nc)
    nc._cur_pfx = pfx



def build_A():
    nc = bass.Bass("TRN2", target_bir_lowering=False)
    io = {"x": nc.dram_tensor("x", [TPC, D], F32, kind="ExternalInput").ap(),
          "wq": nc.dram_tensor("wq", [D, 3 * D], F32, kind="ExternalInput").ap(),
          "bq": nc.dram_tensor("bq", [1, 3 * D], F32, kind="ExternalInput").ap(),
          "qkv": nc.dram_tensor("qkv", [TPC, 3 * D], BF16, kind="ExternalOutput").ap()}
    emit_A(nc, io)
    return nc


def emit_A(nc, io):
    x_in, wq_in, bq_in, qkv_out = io["x"], io["wq"], io["bq"], io["qkv"]
    xn_d = nc.dram_tensor("xn_d", [TPC, D], BF16)

    with ExitStack() as ctx:
        ds, gs = [0], [0]
        sd = ctx.enter_context(nc.semaphore("sd"))
        sg = ctx.enter_context(nc.semaphore("sg"))
        smm = ctx.enter_context(nc.semaphore("smm"))
        sev = ctx.enter_context(nc.semaphore("sev"))
        ssa = [ctx.enter_context(nc.semaphore(f"ssa{i}")) for i in range(2)]
        with ExitStack() as ph1:
            xs = ph1.enter_context(nc.sbuf_tensor("xs", [128, NT * D], F32))
            xbf = ph1.enter_context(nc.sbuf_tensor("xbf", [128, NT * D], BF16))
            with nc.Block() as blk:
                @blk.sync
                def _(e):
                    emit_load_tiles(e, sd, xs, x_in, NT, ds)
                    e.wait_ge(sd, ds[0])
            emit_ln(nc, ph1, xs, NT, xbf, "A")
            with nc.Block() as blk:
                @blk.sync
                def _(e):
                    emit_store_tiles(e, sd, xn_d, xbf, NT, ds)
                    e.wait_ge(sd, ds[0])
        xT = ctx.enter_context(nc.sbuf_tensor("xT", [128, NC * TPC], BF16))
        wqb = ctx.enter_context(nc.sbuf_tensor("wqb", [128, NC * 3 * D], BF16))
        bias = ctx.enter_context(nc.sbuf_tensor("bias", [128, 3 * D], F32))
        brow = ctx.enter_context(nc.sbuf_tensor("brow", [1, 3 * D], F32))
        ones32 = ctx.enter_context(nc.sbuf_tensor("ones32", [1, 128], F32))
        stage = ctx.enter_context(nc.sbuf_tensor("stage", [128, 2 * 3 * D], BF16))
        psb = [ctx.enter_context(nc.psum_tensor(f"psb{i}", [128, 512], F32)) for i in range(2)]
        with nc.Block() as blk:
            @blk.sync
            def _(e):
                for c in range(NC):
                    e.dma_start(out=xT[:, c * TPC:(c + 1) * TPC],
                                in_=xn_d[:, c * 128:(c + 1) * 128],
                                transpose=True).then_inc(sd, 16)
                    ds[0] += 16
                e.dma_start(out=brow[:, :], in_=bq_in).then_inc(sd, 16)
                ds[0] += 16
                e.wait_ge(sd, ds[0])
            @blk.gpsimd
            def _(e):
                for c in range(NC):
                    e.dma_start(out=wqb[:, c * 3 * D:(c + 1) * 3 * D],
                                in_=wq_in[c * 128:(c + 1) * 128, :]).then_inc(sg, 16)
                    gs[0] += 16
                e.wait_ge(sg, gs[0])
            @blk.vector
            def _(e):
                e.memset(ones32[:, :], 1.0)
        with nc.Block() as blk:
            @blk.tensor
            def _(e):
                for i in range(6):
                    if i >= 2:
                        e.wait_ge(sev, i - 1)
                    e.matmul(psb[i % 2][:, :], ones32[:1, :],
                             brow[:1, i * 512:(i + 1) * 512],
                             start=True, stop=True).then_inc(smm, 1)
            @blk.vector
            def _(e):
                for i in range(6):
                    e.wait_ge(smm, i + 1)
                    e.tensor_copy(bias[:, i * 512:(i + 1) * 512],
                                  psb[i % 2][:, :]).then_inc(sev, 1)
        ps = [ctx.enter_context(nc.psum_tensor(f"ps{n}", [128, 512], F32)) for n in range(6)]
        mm0, ev0 = 6, 6
        with nc.Block() as blk:
            @blk.tensor
            def _(e):
                for t in range(NT):
                    if t > 0:
                        e.wait_ge(sev, ev0 + 6 * t)
                    for c in range(NC):
                        lhsT = xT[:, c * TPC + t * 128:c * TPC + (t + 1) * 128]
                        for n in range(6):
                            ins = e.matmul(
                                ps[n][:, :], lhsT,
                                wqb[:, c * 3 * D + n * 512:c * 3 * D + (n + 1) * 512],
                                start=(c == 0), stop=(c == NC - 1))
                    ins.then_inc(smm, 1)
            @blk.vector
            def _(e):
                for t in range(NT):
                    e.wait_ge(smm, mm0 + t + 1)
                    if t >= 2:
                        e.wait_ge(ssa[t % 2], 16 * (t // 2))
                    for n in range(6):
                        off = (t % 2) * 3 * D + n * 512
                        e.scalar_tensor_tensor(stage[:, off:off + 512], ps[n][:, :],
                                               1.0, bias[:, n * 512:(n + 1) * 512],
                                               ALU.mult, ALU.add).then_inc(sev, 1)
            @blk.sync
            def _(e):
                for t in range(NT):
                    e.wait_ge(sev, ev0 + 6 * (t + 1))
                    e.dma_start(out=qkv_out[t * 128:(t + 1) * 128, :],
                                in_=stage[:, (t % 2) * 3 * D:(t % 2 + 1) * 3 * D]
                                ).then_inc(ssa[t % 2], 16)
                e.wait_ge(ssa[0], 16 * (NT - NT // 2))
                e.wait_ge(ssa[1], 16 * (NT // 2))



def build_B(SS=S):
    nc = bass.Bass("TRN2", target_bir_lowering=False)
    io = {"ctxn": nc.dram_tensor("ctxn", [4 * 64, SS], BF16, kind="ExternalOutput").ap()}
    for p in range(2):
        io[f"kt{p}"] = nc.dram_tensor(f"kt{p}", [128, SS], BF16, kind="ExternalInput").ap()
        io[f"qt{p}"] = nc.dram_tensor(f"qt{p}", [128, SS], BF16, kind="ExternalInput").ap()
        io[f"vt{p}"] = nc.dram_tensor(f"vt{p}", [128, SS], BF16, kind="ExternalInput").ap()
    emit_B(nc, io, SS=SS)
    return nc


def emit_B(nc, io, SS=S):
    NKT_ = SS // 128
    nsb = SS // 1024
    nblocks = SS // 512
    fused = "qkv_g" in io
    if not fused:
        kt = [io[f"kt{p}"] for p in range(2)]
        qt = [io[f"qt{p}"] for p in range(2)]
        vt = [io[f"vt{p}"] for p in range(2)]
    ctx_out = io["ctxn"]

    with ExitStack() as ctx:
        ds = [0]
        sd = ctx.enter_context(nc.semaphore("sd"))
        s_mm = ctx.enter_context(nc.semaphore("s_mm"))
        s_exp = ctx.enter_context(nc.semaphore("s_exp"))
        s_msk = ctx.enter_context(nc.semaphore("s_msk"))
        s_av = ctx.enter_context(nc.semaphore("s_av"))
        s_ev = ctx.enter_context(nc.semaphore("s_ev"))
        s_tr = ctx.enter_context(nc.semaphore("s_tr"))

        ktb = [ctx.enter_context(nc.sbuf_tensor(f"ktb{p}", [128, SS], BF16)) for p in range(2)]
        qtb = [ctx.enter_context(nc.sbuf_tensor(f"qtb{p}", [128, SS], BF16)) for p in range(2)]
        vpt = [ctx.enter_context(nc.sbuf_tensor(f"vpt{u}", [128, 65 * NKT_], BF16))
               for u in range(4)]
        identb = ctx.enter_context(nc.sbuf_tensor("identb", [128, 128], BF16))
        m_lo = ctx.enter_context(nc.sbuf_tensor("m_lo", [128, 128], BF16))
        m_hi = ctx.enter_context(nc.sbuf_tensor("m_hi", [128, 128], BF16))
        rowi = ctx.enter_context(nc.sbuf_tensor("rowi", [128, 128], F32))
        coli = ctx.enter_context(nc.sbuf_tensor("coli", [128, 128], F32))
        pbuf = [ctx.enter_context(nc.sbuf_tensor(f"pbuf{i}", [128, 1024], BF16))
                for i in range(2)]
        den = ctx.enter_context(nc.sbuf_tensor("den", [1, SS], F32))
        rbc = ctx.enter_context(nc.sbuf_tensor("rbc", [64, SS], BF16))
        ctxn = ctx.enter_context(nc.sbuf_tensor("ctxn_sb", [64, SS], BF16))
        ctxraw = ctxn
        ones32 = ctx.enter_context(nc.sbuf_tensor("ones32", [1, 64], F32))
        b0col = ctx.enter_context(nc.sbuf_tensor("b0col", [128, 1], F32))

        with nc.Block() as blk:
            @blk.gpsimd
            def _(e):
                e.iota(rowi[:, :], [[0, 128]], channel_multiplier=1,
                       allow_small_or_imprecise_dtypes=True)
                e.iota(coli[:, :], [[1, 128]], channel_multiplier=0,
                       allow_small_or_imprecise_dtypes=True)
            @blk.sync
            def _(e):
                if fused:
                    offsb = io["offsb"]
                    qkv_g = io["qkv_g"]
                    with e.register("dynoff") as off:
                        for p in range(2):
                            for j, dst in ((0 + p, ktb[p]), (2 + p, qtb[p])):
                                e.reg_load(off, offsb[0:1, j:j + 1])
                                e.dma_start(out=dst[:, :],
                                            in_=bass.AP(qkv_g.tensor, off,
                                                        [[3 * D, SS], [1, 128]]),
                                            transpose=True).then_inc(sd, 16)
                                ds[0] += 16
                else:
                    for p in range(2):
                        e.dma_start(out=ktb[p][:, :], in_=kt[p]).then_inc(sd, 16)
                        e.dma_start(out=qtb[p][:, :], in_=qt[p]).then_inc(sd, 16)
                        ds[0] += 32
                e.wait_ge(sd, ds[0])
            @blk.vector
            def _(e):
                e.memset(ones32[:, :], 1.0)
                e.memset(b0col[:, :], -B0)
                for u in range(4):
                    e.memset(vpt[u][:, :], 1.0)
        with nc.Block() as blk:
            @blk.vector
            def _(e):
                e.tensor_tensor(m_lo[:, :], coli[:, :], rowi[:, :], ALU.is_ge)
                e.tensor_tensor(m_hi[:, :], coli[:, :], rowi[:, :], ALU.is_le)
                e.tensor_tensor(identb[:, :], coli[:, :], rowi[:, :], ALU.is_equal)

        with nc.named_scope("vtr"), ExitStack() as trctx:
            vts = trctx.enter_context(nc.sbuf_tensor("vts", [128, SS], BF16))
            pstr = [trctx.enter_context(nc.psum_tensor(f"pstr{i}", [128, 128], BF16))
                    for i in range(2)]
            ntr = 0
            for p in range(2):
                with nc.Block() as blk:
                    @blk.sync
                    def _(e):
                        if fused:
                            with e.register(f"dynoffv{p}") as off:
                                e.reg_load(off, io["offsb"][0:1, 4 + p:5 + p])
                                e.dma_start(out=vts[:, :],
                                            in_=bass.AP(io["qkv_g"].tensor, off,
                                                        [[3 * D, SS], [1, 128]]),
                                            transpose=True).then_inc(sd, 16)
                        else:
                            e.dma_start(out=vts[:, :], in_=vt[p]).then_inc(sd, 16)
                        ds[0] += 16
                        e.wait_ge(sd, ds[0])
                base_tr = ntr
                with nc.Block() as blk:
                    @blk.tensor
                    def _(e):
                        for i in range(NKT_):
                            if base_tr + i >= 2:
                                e.wait_ge(s_tr, base_tr + i - 1)
                            e.transpose(pstr[(base_tr + i) % 2][:, :],
                                        vts[:, 128 * i:128 * (i + 1)],
                                        identb[:, :]).then_inc(s_mm, 1)
                    @blk.vector
                    def _(e):
                        for i in range(NKT_):
                            e.wait_ge(s_mm, base_tr + i + 1)
                            e.tensor_copy(vpt[2 * p][:, 65 * i:65 * i + 64],
                                          pstr[(base_tr + i) % 2][:, 0:64])
                            e.tensor_copy(vpt[2 * p + 1][:, 65 * i:65 * i + 64],
                                          pstr[(base_tr + i) % 2][:, 64:128]).then_inc(s_tr, 1)
                ntr += NKT_

        ps_s = [ctx.enter_context(nc.psum_tensor(f"ps_s{i}", [128, 1024], F32))
                for i in range(2)]
        ps_c = [ctx.enter_context(nc.psum_tensor(f"ps_c{i}", [128, 512], F32))
                for i in range(4)]
        cnt = {"mm": ntr, "exp": 0, "msk": 0, "av": 0, "ev": 0}

        def unit_global(base, K2, Q2, VT):
            plan = []
            for m in range(nsb):
                for j in range(8 * (m + 1)):
                    plan.append((len(plan), 128 * j, 1024 * m))
            msk_tgt, c = {}, cnt["msk"]
            for (i, tk0, sb0) in plan:
                if tk0 >= sb0:
                    c += 1
                msk_tgt[i] = c
            sb_last = {}
            idx = 0
            for m in range(nsb):
                idx += 8 * (m + 1)
                sb_last[m] = idx
            b_mm, b_exp, b_av, b_ev = cnt["mm"], cnt["exp"], cnt["av"], cnt["ev"]
            with nc.Block() as blk:
                @blk.tensor
                def _(e):
                    def do_av(i, tk0, sb0):
                        buf, m = i % 2, sb0 // 1024
                        first = (plan[i - 1][2] != sb0) if i > 0 else True
                        last = (plan[i + 1][2] != sb0) if i + 1 < len(plan) else True
                        if first and m >= 2:
                            e.wait_ge(s_ev, b_ev + 2 * (m - 1))
                        if tk0 >= sb0:
                            e.wait_ge(s_msk, msk_tgt[i])
                        else:
                            e.wait_ge(s_exp, b_exp + i + 1)
                        j = tk0 // 128
                        for hh in range(2):
                            ins = e.matmul(ps_c[(2 * m + hh) % 4][0:65, :],
                                           VT[:, 65 * j:65 * j + 65],
                                           pbuf[buf][:, 512 * hh:512 * (hh + 1)],
                                           start=first, stop=last)
                        ins.then_inc(s_av, 1)
                    for (i, tk0, sb0) in plan:
                        buf = i % 2
                        if i >= 2:
                            e.wait_ge(s_exp, b_exp + i - 1)
                        lhsT = K2[base:base + 64, tk0:tk0 + 128]
                        for hh in range(2):
                            lo = sb0 + 512 * hh
                            ins = e.matmul(ps_s[buf][:, 512 * hh:512 * (hh + 1)],
                                           lhsT, Q2[base:base + 64, lo:lo + 512],
                                           start=True, stop=True)
                        ins.then_inc(s_mm, 1)
                        if i > 0:
                            do_av(*plan[i - 1])
                    do_av(*plan[-1])
                @blk.scalar
                def _(e):
                    for (i, tk0, sb0) in plan:
                        buf = i % 2
                        v0 = max(0, tk0 - sb0)
                        e.wait_ge(s_mm, b_mm + i + 1)
                        if i >= 2:
                            e.wait_ge(s_av, b_av + i - 1)
                        e.activation(pbuf[buf][:, v0:1024], ps_s[buf][:, v0:1024],
                                     AF.Exp, bias=b0col[:, 0:1], scale=0.125).then_inc(s_exp, 1)
                @blk.gpsimd
                def _(e):
                    for (i, tk0, sb0) in plan:
                        if tk0 < sb0:
                            continue
                        buf = i % 2
                        e.wait_ge(s_exp, b_exp + i + 1)
                        jj = (tk0 - sb0) // 128
                        if jj > 0:
                            e.memset(pbuf[buf][:, 0:128 * jj], 0.0)
                            e.drain()
                        e.tensor_tensor(pbuf[buf][:, 128 * jj:128 * (jj + 1)],
                                        pbuf[buf][:, 128 * jj:128 * (jj + 1)],
                                        m_lo[:, :], ALU.mult).then_inc(s_msk, 1)
                @blk.vector
                def _(e):
                    for m in range(nsb):
                        e.wait_ge(s_av, b_av + sb_last[m])
                        for hh in range(2):
                            bank = (2 * m + hh) % 4
                            lo = 1024 * m + 512 * hh
                            e.tensor_copy(ctxraw[:, lo:lo + 512], ps_c[bank][0:64, :])
                            e.tensor_copy(den[:, lo:lo + 512],
                                          ps_c[bank][64:65, :]).then_inc(s_ev, 1)
            cnt["mm"] = b_mm + len(plan)
            cnt["exp"] = b_exp + len(plan)
            cnt["msk"] = msk_tgt[len(plan) - 1]
            cnt["av"] = b_av + len(plan)
            cnt["ev"] = b_ev + 2 * nsb

        def unit_local(base, K2, Q2, VT):
            plan = [(j, 128 * j, min(512, SS - 128 * j)) for j in range(NKT_)]
            blk_last = {m: min(NKT_ - 1, 4 * m + 3) for m in range(nblocks)}
            navs = {}
            nav = 0
            for (j, tk0, width) in plan:
                off = tk0 % 512
                w1 = min(width, 512 - off)
                nav += 1 + (1 if width > w1 else 0)
                navs[j] = nav
            b_mm, b_exp, b_av, b_ev = cnt["mm"], cnt["exp"], cnt["av"], cnt["ev"]
            b_msk = cnt["msk"]
            with nc.Block() as blk:
                @blk.tensor
                def _(e):
                    def do_av(i, tk0, width):
                        buf = i % 2
                        e.wait_ge(s_msk, b_msk + i + 1)
                        A = tk0 // 512
                        off = tk0 % 512
                        w1 = min(width, 512 - off)
                        firstA = (i == max(0, 4 * A - 3))
                        lastA = (i == blk_last[A])
                        if firstA and A >= 4:
                            e.wait_ge(s_ev, b_ev + (A - 3))
                        e.matmul(ps_c[A % 4][0:65, off:off + w1],
                                 VT[:, 65 * i:65 * i + 65], pbuf[buf][:, 0:w1],
                                 start=firstA, stop=lastA).then_inc(s_av, 1)
                        if width > w1:
                            A2 = A + 1
                            firstA2 = (i == max(0, 4 * A2 - 3))
                            lastA2 = (i == blk_last[A2])
                            if firstA2 and A2 >= 4:
                                e.wait_ge(s_ev, b_ev + (A2 - 3))
                            e.matmul(ps_c[A2 % 4][0:65, 0:width - w1],
                                     VT[:, 65 * i:65 * i + 65], pbuf[buf][:, w1:width],
                                     start=firstA2, stop=lastA2).then_inc(s_av, 1)
                    for (i, tk0, width) in plan:
                        buf = i % 2
                        if i >= 2:
                            e.wait_ge(s_exp, b_exp + i - 1)
                        lhsT = K2[base:base + 64, tk0:tk0 + 128]
                        e.matmul(ps_s[buf][:, 0:width], lhsT,
                                 Q2[base:base + 64, tk0:tk0 + width],
                                 start=True, stop=True).then_inc(s_mm, 1)
                        if i > 0:
                            do_av(*plan[i - 1])
                    do_av(*plan[-1])
                @blk.scalar
                def _(e):
                    for (i, tk0, width) in plan:
                        buf = i % 2
                        e.wait_ge(s_mm, b_mm + i + 1)
                        if i >= 2:
                            e.wait_ge(s_av, b_av + navs[i - 2])
                        e.activation(pbuf[buf][:, 0:width], ps_s[buf][:, 0:width],
                                     AF.Exp, bias=b0col[:, 0:1], scale=0.125).then_inc(s_exp, 1)
                @blk.gpsimd
                def _(e):
                    for (i, tk0, width) in plan:
                        buf = i % 2
                        e.wait_ge(s_exp, b_exp + i + 1)
                        ins = e.tensor_tensor(pbuf[buf][:, 0:128], pbuf[buf][:, 0:128],
                                              m_lo[:, :], ALU.mult)
                        if width > 256:
                            ins = e.tensor_tensor(pbuf[buf][:, 256:384],
                                                  pbuf[buf][:, 256:384],
                                                  m_hi[:, :], ALU.mult)
                        if width > 384:
                            ins = e.memset(pbuf[buf][:, 384:width], 0.0)
                        ins.then_inc(s_msk, 1)
                @blk.vector
                def _(e):
                    for (i, tk0, width) in plan:
                        for m in range(nblocks):
                            if blk_last[m] == i:
                                e.wait_ge(s_av, b_av + navs[i])
                                e.tensor_copy(ctxraw[:, 512 * m:512 * (m + 1)],
                                              ps_c[m % 4][0:64, :])
                                e.tensor_copy(den[:, 512 * m:512 * (m + 1)],
                                              ps_c[m % 4][64:65, :]).then_inc(s_ev, 1)
            cnt["mm"] = b_mm + len(plan)
            cnt["exp"] = b_exp + len(plan)
            cnt["msk"] = b_msk + len(plan)
            cnt["av"] = b_av + nav
            cnt["ev"] = b_ev + nblocks

        for u in range(4):
            p, base = u // 2, 64 * (u % 2)
            with nc.named_scope(f"u{u}"):
                if u == 0:
                    unit_global(base, ktb[p], qtb[p], vpt[u])
                else:
                    unit_local(base, ktb[p], qtb[p], vpt[u])
            with nc.Block() as blk:
                @blk.vector
                def _(e):
                    e.reciprocal(den[:, :], den[:, :])
                    e.drain()
            b_mm, b_ev = cnt["mm"], cnt["ev"]
            with nc.Block() as blk:
                @blk.tensor
                def _(e):
                    for i in range(SS // 512):
                        if i >= 2:
                            e.wait_ge(s_ev, b_ev + i - 1)
                        e.matmul(ps_c[i % 2][0:64, :], ones32[:1, :],
                                 den[:1, 512 * i:512 * (i + 1)],
                                 start=True, stop=True).then_inc(s_mm, 1)
                @blk.scalar
                def _(e):
                    for i in range(SS // 512):
                        e.wait_ge(s_mm, b_mm + i + 1)
                        e.copy(rbc[:, 512 * i:512 * (i + 1)],
                               ps_c[i % 2][0:64, :]).then_inc(s_ev, 1)
            cnt["mm"] = b_mm + SS // 512
            cnt["ev"] = b_ev + SS // 512
            with nc.Block() as blk:
                @blk.vector
                def _(e):
                    e.tensor_tensor(ctxn[:, :], ctxn[:, :], rbc[:, :], ALU.mult)
            with nc.Block() as blk:
                @blk.sync
                def _(e):
                    e.dma_start(out=ctx_out[64 * u:64 * (u + 1), :],
                                in_=ctxn[:, :]).then_inc(sd, 16)
                    ds[0] += 16
                    e.wait_ge(sd, ds[0])



def build_C():
    nc = bass.Bass("TRN2", target_bir_lowering=False)
    io = {"ctxf": nc.dram_tensor("ctxf", [D, TPC], BF16, kind="ExternalInput").ap(),
          "wo": nc.dram_tensor("wo", [D, D], F32, kind="ExternalInput").ap(),
          "bo": nc.dram_tensor("bo", [1, D], F32, kind="ExternalInput").ap(),
          "x": nc.dram_tensor("x", [TPC, D], F32, kind="ExternalInput").ap(),
          "x1": nc.dram_tensor("x1", [TPC, D], F32, kind="ExternalOutput").ap()}
    emit_C(nc, io)
    return nc


def emit_C(nc, io):
    wo_in, bo_in, x_in, x1_out = io["wo"], io["bo"], io["x"], io["x1"]
    fused = "ctxn_g" in io
    if not fused:
        ctxf_in = io["ctxf"]

    with ExitStack() as ctx:
        ds, gs = [0], [0]
        sd = ctx.enter_context(nc.semaphore("sd"))
        sg = ctx.enter_context(nc.semaphore("sg"))
        smm = ctx.enter_context(nc.semaphore("smm"))
        sev = ctx.enter_context(nc.semaphore("sev"))
        ctxf = ctx.enter_context(nc.sbuf_tensor("ctxf_sb", [128, NC * TPC], BF16))
        wob = ctx.enter_context(nc.sbuf_tensor("wob", [128, NC * D], BF16))
        bias = ctx.enter_context(nc.sbuf_tensor("bias", [128, D], F32))
        brow = ctx.enter_context(nc.sbuf_tensor("brow", [1, D], F32))
        ones32 = ctx.enter_context(nc.sbuf_tensor("ones32", [1, 128], F32))
        y_sb = ctx.enter_context(nc.sbuf_tensor("y_sb", [128, NT * D], F32))
        ps = [ctx.enter_context(nc.psum_tensor(f"ps{n}", [128, 512], F32)) for n in range(4)]
        with nc.Block() as blk:
            @blk.sync
            def _(e):
                if fused:
                    with e.register("dynoffc") as off:
                        for c in range(NC):
                            e.reg_load(off, io["offsb"][0:1, 6 + c:7 + c])
                            e.dma_start(out=ctxf[:, c * TPC:(c + 1) * TPC],
                                        in_=bass.AP(io["ctxn_g"].tensor, off,
                                                    [[S, 128], [1, TPC]])).then_inc(sd, 16)
                            ds[0] += 16
                else:
                    for c in range(NC):
                        e.dma_start(out=ctxf[:, c * TPC:(c + 1) * TPC],
                                    in_=ctxf_in[c * 128:(c + 1) * 128, :]).then_inc(sd, 16)
                        ds[0] += 16
                e.dma_start(out=brow[:, :], in_=bo_in).then_inc(sd, 16)
                ds[0] += 16
                e.wait_ge(sd, ds[0])
            @blk.gpsimd
            def _(e):
                for c in range(NC):
                    e.dma_start(out=wob[:, c * D:(c + 1) * D],
                                in_=wo_in[c * 128:(c + 1) * 128, :]).then_inc(sg, 16)
                    gs[0] += 16
                e.wait_ge(sg, gs[0])
            @blk.vector
            def _(e):
                e.memset(ones32[:, :], 1.0)
        with nc.Block() as blk:
            @blk.tensor
            def _(e):
                for i in range(2):
                    e.matmul(ps[2 + i][:, :], ones32[:1, :],
                             brow[:1, i * 512:(i + 1) * 512],
                             start=True, stop=True).then_inc(smm, 1)
            @blk.vector
            def _(e):
                for i in range(2):
                    e.wait_ge(smm, i + 1)
                    e.tensor_copy(bias[:, i * 512:(i + 1) * 512], ps[2 + i][:, :])
        with nc.Block() as blk:
            @blk.tensor
            def _(e):
                for t in range(NT):
                    if t >= 2:
                        e.wait_ge(sev, 2 * (t - 1))
                    for c in range(NC):
                        lhsT = ctxf[:, c * TPC + t * 128:c * TPC + (t + 1) * 128]
                        for n in range(2):
                            ins = e.matmul(ps[2 * (t % 2) + n][:, :], lhsT,
                                           wob[:, c * D + n * 512:c * D + (n + 1) * 512],
                                           start=(c == 0), stop=(c == NC - 1))
                    ins.then_inc(smm, 1)
            @blk.vector
            def _(e):
                for t in range(NT):
                    e.wait_ge(smm, 2 + t + 1)
                    for n in range(2):
                        e.scalar_tensor_tensor(
                            y_sb[:, t * D + n * 512:t * D + (n + 1) * 512],
                            ps[2 * (t % 2) + n][:, :], 1.0,
                            bias[:, n * 512:(n + 1) * 512], ALU.mult,
                            ALU.add).then_inc(sev, 1)
        emit_ln_resid_streamx(nc, ctx, y_sb, x_in, x1_out, NT, "C")



def build_D():
    nc = bass.Bass("TRN2", target_bir_lowering=False)
    io = {"x": nc.dram_tensor("x", [TPC, D], F32, kind="ExternalInput").ap(),
          "halo": nc.dram_tensor("halo", [1, D], F32, kind="ExternalInput").ap(),
          "wrT": nc.dram_tensor("wrT", [D, D], F32, kind="ExternalInput").ap(),
          "wkT": nc.dram_tensor("wkT", [D, D], F32, kind="ExternalInput").ap(),
          "wvT": nc.dram_tensor("wvT", [D, D], F32, kind="ExternalInput").ap(),
          "mur": nc.dram_tensor("mur", [1, D], F32, kind="ExternalInput").ap(),
          "muk": nc.dram_tensor("muk", [1, D], F32, kind="ExternalInput").ap(),
          "x2": nc.dram_tensor("x2", [TPC, D], F32, kind="ExternalOutput").ap()}
    emit_D(nc, io)
    return nc


def emit_D(nc, io):
    x_in, x2_out = io["x"], io["x2"]
    wrT_in, wkT_in, wvT_in = io["wrT"], io["wkT"], io["wvT"]
    mur_in, muk_in = io["mur"], io["muk"]
    fused = "lr_g" in io
    if not fused:
        halo_in = io["halo"]
    xn_d = nc.dram_tensor("xn_d", [TPC, D], BF16)
    xp_d = nc.dram_tensor("xp_d", [TPC, D], BF16)

    with ExitStack() as ctx:
        ds, gs = [0], [0]
        sd = ctx.enter_context(nc.semaphore("sd"))
        sg = ctx.enter_context(nc.semaphore("sg"))
        smm = ctx.enter_context(nc.semaphore("smm"))
        sev = ctx.enter_context(nc.semaphore("sev"))
        cnt = {"mm": 0, "ev": 0}
        r_sb = ctx.enter_context(nc.sbuf_tensor("r_sb", [128, NT * D], BF16))
        k2T = ctx.enter_context(nc.sbuf_tensor("k2T", [128, NC * TPC], BF16))
        wbuf = ctx.enter_context(nc.sbuf_tensor("wbuf", [128, NC * D], BF16))
        ps = [ctx.enter_context(nc.psum_tensor(f"ps{n}", [128, 512], F32)) for n in range(4)]

        with ExitStack() as ph1:
            x_sb = ph1.enter_context(nc.sbuf_tensor("x_sb", [128, NT * D], F32))
            xbf = ph1.enter_context(nc.sbuf_tensor("xbf", [128, NT * D], BF16))
            h_sb = ph1.enter_context(nc.sbuf_tensor("h_sb", [1, D], F32))
            hst = ph1.enter_context(nc.sbuf_tensor("hst", [1, 4], F32))
            hbf = ph1.enter_context(nc.sbuf_tensor("hbf", [1, D], BF16))
            hscr = ph1.enter_context(nc.sbuf_tensor("hscr", [1, D], BF16))
            with nc.Block() as blk:
                @blk.sync
                def _(e):
                    emit_load_tiles(e, sd, x_sb, x_in, NT, ds)
                    if fused:
                        with e.register("dynoffh") as off:
                            e.reg_load(off, io["offsb"][0:1, 14:15])
                            e.dma_start(out=h_sb[:, :],
                                        in_=bass.AP(io["lr_g"].tensor, off,
                                                    [[D, 1], [1, D]])).then_inc(sd, 16)
                    else:
                        e.dma_start(out=h_sb[:, :], in_=halo_in).then_inc(sd, 16)
                    ds[0] += 16
                    e.wait_ge(sd, ds[0])
            emit_ln(nc, ph1, x_sb, NT, xbf, "D")
            with nc.Block() as blk:
                @blk.vector
                def _(e):
                    e.tensor_scalar(hbf[:, :], h_sb[:, :], 1.0, 0.0, ALU.mult, ALU.add,
                                    accum_out=hst[:, 0:1])
                @blk.scalar
                def _(e):
                    e.activation(hscr[:, :], h_sb[:, :], AF.Square, accum_out=hst[:, 1:2])
            with nc.Block() as blk:
                @blk.vector
                def _(e):
                    e.tensor_scalar(hst[:, 2:3], hst[:, 0:1], 1.0 / D, 0.0, ALU.mult, ALU.add)
                    e.drain()
                    e.tensor_tensor(hst[:, 0:1], hst[:, 2:3], hst[:, 2:3], ALU.mult)
                    e.tensor_scalar(hst[:, 1:2], hst[:, 1:2], 1.0 / D, EPS, ALU.mult, ALU.add)
                    e.drain()
                    e.scalar_tensor_tensor(hst[:, 1:2], hst[:, 1:2], 1.0, hst[:, 0:1],
                                           ALU.mult, ALU.subtract)
            with nc.Block() as blk:
                @blk.scalar
                def _(e):
                    e.activation(hst[:, 1:2], hst[:, 1:2], AF.Sqrt)
            with nc.Block() as blk:
                @blk.vector
                def _(e):
                    e.reciprocal(hst[:, 3:4], hst[:, 1:2])
                    e.drain()
                    e.tensor_scalar(hbf[:, :], h_sb[:, :], hst[:, 2:3], hst[:, 3:4],
                                    ALU.subtract, ALU.mult)
            with nc.Block() as blk:
                @blk.sync
                def _(e):
                    emit_store_tiles(e, sd, xn_d, xbf, NT, ds)
                    e.dma_start(out=xp_d[0:1, :], in_=hbf[:, :]).then_inc(sd, 16)
                    ds[0] += 16
                    for t in range(NT):
                        nrows = 128 if t < NT - 1 else 127
                        e.dma_start(out=xp_d[t * 128 + 1:t * 128 + 1 + nrows, :],
                                    in_=xbf[0:nrows, t * D:(t + 1) * D]).then_inc(sd, 16)
                        ds[0] += 16
                    e.wait_ge(sd, ds[0])

        with ExitStack() as ph2:
            xT = ph2.enter_context(nc.sbuf_tensor("xT", [128, NC * TPC], BF16))
            xP = ph2.enter_context(nc.sbuf_tensor("xP", [128, NC * TPC], BF16))
            xm = ph2.enter_context(nc.sbuf_tensor("xm", [128, NC * TPC], BF16))
            dmix = ph2.enter_context(nc.sbuf_tensor("dmix", [128, TPC], BF16))
            mucol = ph2.enter_context(nc.sbuf_tensor("mucol", [128, 2 * NC], F32))
            krelu = ph2.enter_context(nc.sbuf_tensor("krelu", [128, 4 * 512], F32))
            with nc.Block() as blk:
                @blk.sync
                def _(e):
                    for c in range(NC):
                        e.dma_start(out=xT[:, c * TPC:(c + 1) * TPC],
                                    in_=xn_d[:, c * 128:(c + 1) * 128],
                                    transpose=True).then_inc(sd, 16)
                        ds[0] += 16
                    for c in range(NC):
                        e.dma_start(out=xP[:, c * TPC:(c + 1) * TPC],
                                    in_=xp_d[:, c * 128:(c + 1) * 128],
                                    transpose=True).then_inc(sd, 16)
                        ds[0] += 16
                    e.wait_ge(sd, ds[0])
                @blk.gpsimd
                def _(e):
                    for c in range(NC):
                        col_load(e, sg, mucol[:, c:c + 1],
                                 mur_in[0:1, c * 128:(c + 1) * 128], gs)
                        col_load(e, sg, mucol[:, NC + c:NC + c + 1],
                                 muk_in[0:1, c * 128:(c + 1) * 128], gs)
                    for c in range(NC):
                        e.dma_start(out=wbuf[:, c * D:(c + 1) * D],
                                    in_=wrT_in[c * 128:(c + 1) * 128, :]).then_inc(sg, 16)
                        gs[0] += 16
                    e.wait_ge(sg, gs[0])

            def emit_mix(mucol_off):
                with nc.Block() as blk:
                    @blk.vector
                    def _(e):
                        for c in range(NC):
                            cur = xT[:, c * TPC:(c + 1) * TPC]
                            prev = xP[:, c * TPC:(c + 1) * TPC]
                            e.tensor_tensor(dmix[:, :], cur, prev, ALU.subtract)
                            e.drain()
                            e.scalar_tensor_tensor(xm[:, c * TPC:(c + 1) * TPC],
                                                   dmix[:, :],
                                                   mucol[:, mucol_off + c:mucol_off + c + 1],
                                                   prev, ALU.mult, ALU.add)
                            e.drain()

            emit_mix(0)
            b_mm, b_ev = cnt["mm"], cnt["ev"]
            with nc.Block() as blk:
                @blk.tensor
                def _(e):
                    for t in range(NT):
                        if t >= 2:
                            e.wait_ge(sev, b_ev + 2 * (t - 1))
                        for c in range(NC):
                            lhsT = xm[:, c * TPC + t * 128:c * TPC + (t + 1) * 128]
                            for n in range(2):
                                ins = e.matmul(ps[2 * (t % 2) + n][:, :], lhsT,
                                               wbuf[:, c * D + n * 512:c * D + (n + 1) * 512],
                                               start=(c == 0), stop=(c == NC - 1))
                        ins.then_inc(smm, 1)
                @blk.scalar
                def _(e):
                    for t in range(NT):
                        e.wait_ge(smm, b_mm + t + 1)
                        for n in range(2):
                            e.activation(r_sb[:, t * D + n * 512:t * D + (n + 1) * 512],
                                         ps[2 * (t % 2) + n][:, :],
                                         AF.Sigmoid).then_inc(sev, 1)
            cnt["mm"] = b_mm + NT
            cnt["ev"] = b_ev + 2 * NT
            with nc.Block() as blk:
                @blk.gpsimd
                def _(e):
                    for c in range(NC):
                        e.dma_start(out=wbuf[:, c * D:(c + 1) * D],
                                    in_=wkT_in[c * 128:(c + 1) * 128, :]).then_inc(sg, 16)
                        gs[0] += 16
                    e.wait_ge(sg, gs[0])
            emit_mix(NC)
            b_mm, b_ev = cnt["mm"], cnt["ev"]
            with nc.Block() as blk:
                @blk.tensor
                def _(e):
                    for dt in range(NC):
                        if dt >= 1:
                            e.wait_ge(sev, b_ev + dt)
                        for c in range(NC):
                            lhsT = wbuf[:, c * D + dt * 128:c * D + (dt + 1) * 128]
                            for q in range(4):
                                ins = e.matmul(ps[q][:, :], lhsT,
                                               xm[:, c * TPC + q * 512:c * TPC + (q + 1) * 512],
                                               start=(c == 0), stop=(c == NC - 1))
                        ins.then_inc(smm, 1)
                @blk.scalar
                def _(e):
                    for dt in range(NC):
                        e.wait_ge(smm, b_mm + dt + 1)
                        for q in range(4):
                            e.activation(krelu[:, q * 512:(q + 1) * 512], ps[q][:, :],
                                         AF.Relu)
                        e.drain()
                        e.activation(k2T[:, dt * TPC:(dt + 1) * TPC], krelu[:, :],
                                     AF.Square).then_inc(sev, 1)
            cnt["mm"] = b_mm + NC
            cnt["ev"] = b_ev + NC
        y_sb = ctx.enter_context(nc.sbuf_tensor("y_sb", [128, NT * D], F32))
        with nc.Block() as blk:
            @blk.gpsimd
            def _(e):
                for c in range(NC):
                    e.dma_start(out=wbuf[:, c * D:(c + 1) * D],
                                in_=wvT_in[c * 128:(c + 1) * 128, :]).then_inc(sg, 16)
                    gs[0] += 16
                e.wait_ge(sg, gs[0])
        b_mm, b_ev = cnt["mm"], cnt["ev"]
        with nc.Block() as blk:
            @blk.tensor
            def _(e):
                for t in range(NT):
                    if t >= 2:
                        e.wait_ge(sev, b_ev + 2 * (t - 1))
                    for c in range(NC):
                        lhsT = k2T[:, c * TPC + t * 128:c * TPC + (t + 1) * 128]
                        for n in range(2):
                            ins = e.matmul(ps[2 * (t % 2) + n][:, :], lhsT,
                                           wbuf[:, c * D + n * 512:c * D + (n + 1) * 512],
                                           start=(c == 0), stop=(c == NC - 1))
                    ins.then_inc(smm, 1)
            @blk.vector
            def _(e):
                for t in range(NT):
                    e.wait_ge(smm, b_mm + t + 1)
                    for n in range(2):
                        sl = slice(t * D + n * 512, t * D + (n + 1) * 512)
                        e.tensor_tensor(y_sb[:, sl], ps[2 * (t % 2) + n][:, :],
                                        r_sb[:, sl], ALU.mult).then_inc(sev, 1)
        cnt["mm"] = b_mm + NT
        cnt["ev"] = b_ev + 2 * NT
        emit_ln_resid_streamx(nc, ctx, y_sb, x_in, x2_out, NT, "D")



def build_E0():
    nc = bass.Bass("TRN2", target_bir_lowering=False)
    io = {"x": nc.dram_tensor("x", [TPC, D], F32, kind="ExternalInput").ap(),
          "wvT": nc.dram_tensor("wvT", [D, D], F32, kind="ExternalInput").ap(),
          "wrT": nc.dram_tensor("wrT", [D, D], F32, kind="ExternalInput").ap(),
          "td": nc.dram_tensor("td", [1, D], F32, kind="ExternalInput").ap(),
          "vT": nc.dram_tensor("vT", [D, TPC], BF16, kind="ExternalOutput").ap(),
          "rT": nc.dram_tensor("rT", [D, TPC], BF16, kind="ExternalOutput").ap(),
          "P": nc.dram_tensor("P", [128, NC], F32, kind="ExternalOutput").ap(),
          "Bb": nc.dram_tensor("Bb", [128, NC], F32, kind="ExternalOutput").ap()}
    emit_E0(nc, io)
    return nc


def emit_E0(nc, io):
    x_in, wvT_in, wrT_in, td_in = io["x"], io["wvT"], io["wrT"], io["td"]
    vT_out, rT_out = io["vT"], io["rT"]
    fused = "PBfull" in io
    if not fused:
        P_out, Bb_out = io["P"], io["Bb"]
    xn_d = nc.dram_tensor("xn_d", [TPC, D], BF16)

    with ExitStack() as ctx:
        ds, gs = [0], [0]
        sd = ctx.enter_context(nc.semaphore("sd"))
        sg = ctx.enter_context(nc.semaphore("sg"))
        smm = ctx.enter_context(nc.semaphore("smm"))
        sev = ctx.enter_context(nc.semaphore("sev"))
        vT_sb = ctx.enter_context(nc.sbuf_tensor("vT_sb", [128, NC * TPC], BF16))
        rT_sb = ctx.enter_context(nc.sbuf_tensor("rT_sb", [128, NC * TPC], BF16))
        wbuf = ctx.enter_context(nc.sbuf_tensor("wbuf", [128, NC * D], BF16))
        ps = [ctx.enter_context(nc.psum_tensor(f"ps{n}", [128, 512], F32)) for n in range(4)]
        with ExitStack() as ph1:
            x_sb = ph1.enter_context(nc.sbuf_tensor("x_sb", [128, NT * D], F32))
            xbf = ph1.enter_context(nc.sbuf_tensor("xbf", [128, NT * D], BF16))
            with nc.Block() as blk:
                @blk.sync
                def _(e):
                    emit_load_tiles(e, sd, x_sb, x_in, NT, ds)
                    e.wait_ge(sd, ds[0])
            emit_ln(nc, ph1, x_sb, NT, xbf, "E0")
            with nc.Block() as blk:
                @blk.sync
                def _(e):
                    emit_store_tiles(e, sd, xn_d, xbf, NT, ds)
                    e.wait_ge(sd, ds[0])
        xT = ctx.enter_context(nc.sbuf_tensor("xT", [128, NC * TPC], BF16))
        with nc.Block() as blk:
            @blk.sync
            def _(e):
                for c in range(NC):
                    e.dma_start(out=xT[:, c * TPC:(c + 1) * TPC],
                                in_=xn_d[:, c * 128:(c + 1) * 128],
                                transpose=True).then_inc(sd, 16)
                    ds[0] += 16
                e.wait_ge(sd, ds[0])
            @blk.gpsimd
            def _(e):
                for c in range(NC):
                    e.dma_start(out=wbuf[:, c * D:(c + 1) * D],
                                in_=wvT_in[c * 128:(c + 1) * 128, :]).then_inc(sg, 16)
                    gs[0] += 16
                e.wait_ge(sg, gs[0])
        cnt = {"mm": 0, "ev": 0}

        def fm_matmul(dst_sb, act):
            b_mm, b_ev = cnt["mm"], cnt["ev"]
            with nc.Block() as blk:
                @blk.tensor
                def _(e):
                    for dt in range(NC):
                        if dt >= 1:
                            e.wait_ge(sev, b_ev + dt)
                        for c in range(NC):
                            lhsT = wbuf[:, c * D + dt * 128:c * D + (dt + 1) * 128]
                            for q in range(4):
                                ins = e.matmul(ps[q][:, :], lhsT,
                                               xT[:, c * TPC + q * 512:c * TPC + (q + 1) * 512],
                                               start=(c == 0), stop=(c == NC - 1))
                        ins.then_inc(smm, 1)
                @blk.scalar
                def _(e):
                    for dt in range(NC):
                        e.wait_ge(smm, b_mm + dt + 1)
                        for q in range(4):
                            ins = e.activation(
                                dst_sb[:, dt * TPC + q * 512:dt * TPC + (q + 1) * 512],
                                ps[q][:, :], act)
                        ins.then_inc(sev, 1)
            cnt["mm"] = b_mm + NC
            cnt["ev"] = b_ev + NC

        fm_matmul(vT_sb, AF.Copy)
        with nc.Block() as blk:
            @blk.gpsimd
            def _(e):
                for c in range(NC):
                    e.dma_start(out=wbuf[:, c * D:(c + 1) * D],
                                in_=wrT_in[c * 128:(c + 1) * 128, :]).then_inc(sg, 16)
                    gs[0] += 16
                e.wait_ge(sg, gs[0])
        fm_matmul(rT_sb, AF.Sigmoid)
        tdc = ctx.enter_context(nc.sbuf_tensor("tdc", [128, NC], F32))
        cc = ctx.enter_context(nc.sbuf_tensor("cc", [128, 2 * NC], F32))
        onesT = ctx.enter_context(nc.sbuf_tensor("onesT", [128, TPC], F32))
        c1bc = ctx.enter_context(nc.sbuf_tensor("c1bc", [128, TPC], F32))
        cv = ctx.enter_context(nc.sbuf_tensor("cv", [128, TPC], F32))
        Sa = ctx.enter_context(nc.sbuf_tensor("Sa", [128, TPC], F32))
        PB = ctx.enter_context(nc.sbuf_tensor("PB", [128, 2 * NC], F32))
        zz = ctx.enter_context(nc.sbuf_tensor("zzpb", [128, 2 * NC], F32))
        with nc.Block() as blk:
            @blk.gpsimd
            def _(e):
                for c in range(NC):
                    col_load(e, sg, tdc[:, c:c + 1], td_in[0:1, c * 128:(c + 1) * 128], gs)
                e.wait_ge(sg, gs[0])
            @blk.vector
            def _(e):
                e.memset(onesT[:, :], 1.0)
                e.memset(zz[:, :], 0.0)
        with nc.Block() as blk:
            @blk.vector
            def _(e):
                e.tensor_scalar_max(cc[:, 0:NC], tdc[:, :], 0.0)
                e.tensor_scalar_min(cc[:, NC:2 * NC], tdc[:, :], 0.0)
        with nc.Block() as blk:
            @blk.scalar
            def _(e):
                e.activation(cc[:, 0:NC], cc[:, 0:NC], AF.Exp, scale=-1.0)
                e.drain()
                e.activation(cc[:, NC:2 * NC], cc[:, NC:2 * NC], AF.Exp)
        for c in range(NC):
            with nc.Block() as blk:
                @blk.vector
                def _(e):
                    e.tensor_scalar(c1bc[:, :], onesT[:, :], cc[:, c:c + 1], 0.0,
                                    ALU.mult, ALU.add)
                    e.tensor_scalar(cv[:, :], vT_sb[:, c * TPC:(c + 1) * TPC],
                                    cc[:, NC + c:NC + c + 1], 0.0, ALU.mult, ALU.add)
                    e.drain()
                    e.tensor_tensor_scan(Sa[:, :], c1bc[:, :], cv[:, :], 0.0,
                                         ALU.mult, ALU.add)
                    e.drain()
                    e.tensor_copy(PB[:, c:c + 1], Sa[:, TPC - 1:TPC])
                    e.tensor_scalar(cv[:, :], onesT[:, :], cc[:, NC + c:NC + c + 1], 0.0,
                                    ALU.mult, ALU.add)
                    e.drain()
                    e.tensor_tensor_scan(Sa[:, :], c1bc[:, :], cv[:, :], 0.0,
                                         ALU.mult, ALU.add)
                    e.drain()
                    e.tensor_copy(PB[:, NC + c:NC + c + 1], Sa[:, TPC - 1:TPC])
        with nc.Block() as blk:
            @blk.sync
            def _(e):
                for c in range(NC):
                    e.dma_start(out=vT_out[c * 128:(c + 1) * 128, :],
                                in_=vT_sb[:, c * TPC:(c + 1) * TPC]).then_inc(sd, 16)
                    e.dma_start(out=rT_out[c * 128:(c + 1) * 128, :],
                                in_=rT_sb[:, c * TPC:(c + 1) * TPC]).then_inc(sd, 16)
                    ds[0] += 32
                if fused:
                    e.dma_start(out=io["PBfull"][0:128, :], in_=PB[:, :]).then_inc(sd, 16)
                    e.dma_start(out=io["PBfull"][128:256, :],
                                in_=zz[:, :]).then_inc(sd, 16)
                else:
                    e.dma_start(out=P_out, in_=PB[:, 0:NC]).then_inc(sd, 16)
                    e.dma_start(out=Bb_out, in_=PB[:, NC:2 * NC]).then_inc(sd, 16)
                ds[0] += 32
                e.wait_ge(sd, ds[0])



def build_E():
    nc = bass.Bass("TRN2", target_bir_lowering=False)
    io = {"x": nc.dram_tensor("x", [TPC, D], F32, kind="ExternalInput").ap(),
          "vT": nc.dram_tensor("vT", [D, TPC], BF16, kind="ExternalInput").ap(),
          "rT": nc.dram_tensor("rT", [D, TPC], BF16, kind="ExternalInput").ap(),
          "Pp": nc.dram_tensor("Pp", [128, 3 * NC], F32, kind="ExternalInput").ap(),
          "Bp": nc.dram_tensor("Bp", [128, 3 * NC], F32, kind="ExternalInput").ap(),
          "td": nc.dram_tensor("td", [1, D], F32, kind="ExternalInput").ap(),
          "tf": nc.dram_tensor("tf", [1, D], F32, kind="ExternalInput").ap(),
          "woT": nc.dram_tensor("woT", [D, D], F32, kind="ExternalInput").ap(),
          "x3": nc.dram_tensor("x3", [TPC, D], F32, kind="ExternalOutput").ap()}
    emit_E(nc, io)
    return nc


def emit_E(nc, io):
    x_in, vT_in, rT_in = io["x"], io["vT"], io["rT"]
    td_in, tf_in, woT_in, x3_out = io["td"], io["tf"], io["woT"], io["x3"]
    fused = "PB_g" in io
    if not fused:
        Pp_in, Bp_in = io["Pp"], io["Bp"]

    with ExitStack() as ctx:
        ds, gs = [0], [0]
        sd = ctx.enter_context(nc.semaphore("sd"))
        sg = ctx.enter_context(nc.semaphore("sg"))
        smm = ctx.enter_context(nc.semaphore("smm"))
        sev = ctx.enter_context(nc.semaphore("sev"))
        y_sb = ctx.enter_context(nc.sbuf_tensor("y_sb", [128, NT * D], F32))
        rwT = ctx.enter_context(nc.sbuf_tensor("rwT", [128, NC * TPC], BF16))
        wbuf = ctx.enter_context(nc.sbuf_tensor("wbuf", [128, NC * D], BF16))
        ps = [ctx.enter_context(nc.psum_tensor(f"ps{n}", [128, 512], F32)) for n in range(4)]
        with ExitStack() as ph1:
            vT_sb = ph1.enter_context(nc.sbuf_tensor("vT_sb", [128, NC * TPC], BF16))
            consts = ph1.enter_context(nc.sbuf_tensor("consts", [128, 8 * NC], F32))
            Pp = ph1.enter_context(nc.sbuf_tensor("Pp_sb", [128, 3 * NC], F32))
            Bp = ph1.enter_context(nc.sbuf_tensor("Bp_sb", [128, 3 * NC], F32))
            tdc = ph1.enter_context(nc.sbuf_tensor("tdc", [128, 2 * NC], F32))
            onesT = ph1.enter_context(nc.sbuf_tensor("onesT", [128, TPC], F32))
            c1bc = ph1.enter_context(nc.sbuf_tensor("c1bc", [128, TPC], F32))
            cv = ph1.enter_context(nc.sbuf_tensor("cv", [128, TPC], F32))
            Sa = ph1.enter_context(nc.sbuf_tensor("Sa", [128, TPC + 1], F32))
            Sb = ph1.enter_context(nc.sbuf_tensor("Sb", [128, TPC + 1], F32))
            num = ph1.enter_context(nc.sbuf_tensor("num", [128, TPC], F32))
            dde = ph1.enter_context(nc.sbuf_tensor("dde", [128, TPC], F32))
            C1, C2, E1, E2E, AAIN, BBIN, D2048, E2R = (0, NC, 2 * NC, 3 * NC, 4 * NC,
                                                       5 * NC, 6 * NC, 7 * NC)
            with nc.Block() as blk:
                @blk.sync
                def _(e):
                    for c in range(NC):
                        e.dma_start(out=vT_sb[:, c * TPC:(c + 1) * TPC],
                                    in_=vT_in[c * 128:(c + 1) * 128, :]).then_inc(sd, 16)
                        e.dma_start(out=rwT[:, c * TPC:(c + 1) * TPC],
                                    in_=rT_in[c * 128:(c + 1) * 128, :]).then_inc(sd, 16)
                        ds[0] += 32
                    if fused:
                        with e.register("dynoffp") as off:
                            for k in range(3):
                                e.reg_load(off, io["offsb"][0:1, 15 + k:16 + k])
                                e.dma_start(out=Pp[:, k * NC:(k + 1) * NC],
                                            in_=bass.AP(io["PB_g"].tensor, off,
                                                        [[2 * NC, 128], [1, NC]])
                                            ).then_inc(sd, 16)
                                e.reg_load(off, io["offsb"][0:1, 18 + k:19 + k])
                                e.dma_start(out=Bp[:, k * NC:(k + 1) * NC],
                                            in_=bass.AP(io["PB_g"].tensor, off,
                                                        [[2 * NC, 128], [1, NC]])
                                            ).then_inc(sd, 16)
                                ds[0] += 32
                    else:
                        e.dma_start(out=Pp[:, :], in_=Pp_in).then_inc(sd, 16)
                        e.dma_start(out=Bp[:, :], in_=Bp_in).then_inc(sd, 16)
                        ds[0] += 32
                    e.wait_ge(sd, ds[0])
                @blk.gpsimd
                def _(e):
                    for c in range(NC):
                        col_load(e, sg, tdc[:, c:c + 1], td_in[0:1, c * 128:(c + 1) * 128], gs)
                        col_load(e, sg, tdc[:, NC + c:NC + c + 1],
                                 tf_in[0:1, c * 128:(c + 1) * 128], gs)
                    e.wait_ge(sg, gs[0])
                @blk.vector
                def _(e):
                    e.memset(onesT[:, :], 1.0)
            with nc.Block() as blk:
                @blk.vector
                def _(e):
                    e.tensor_scalar_max(consts[:, C1:C1 + NC], tdc[:, 0:NC], 0.0)
                    e.tensor_scalar_min(consts[:, C2:C2 + NC], tdc[:, 0:NC], 0.0)
                    e.tensor_scalar_max(consts[:, E1:E1 + NC], tdc[:, NC:2 * NC], 0.0)
                    e.tensor_scalar_min(consts[:, E2E:E2E + NC], tdc[:, NC:2 * NC], 0.0)
                    e.drain()
                    e.tensor_scalar(consts[:, D2048:D2048 + NC], consts[:, C1:C1 + NC],
                                    -float(TPC), 0.0, ALU.mult, ALU.add)
            with nc.Block() as blk:
                @blk.scalar
                def _(e):
                    e.activation(consts[:, C1:C1 + NC], consts[:, C1:C1 + NC],
                                 AF.Exp, scale=-1.0)
                    e.drain()
                    e.activation(consts[:, C2:C2 + NC], consts[:, C2:C2 + NC], AF.Exp)
                    e.drain()
                    e.activation(consts[:, E1:E1 + NC], consts[:, E1:E1 + NC],
                                 AF.Exp, scale=-1.0)
                    e.drain()
                    e.activation(consts[:, E2E:E2E + NC], consts[:, E2E:E2E + NC], AF.Exp)
                    e.drain()
                    e.activation(consts[:, D2048:D2048 + NC], consts[:, D2048:D2048 + NC],
                                 AF.Exp)
            with nc.Block() as blk:
                @blk.vector
                def _(e):
                    e.tensor_copy(consts[:, E2R:E2R + NC], consts[:, E2E:E2E + NC])
                    e.drain()
                    e.tensor_scalar(consts[:, E2E:E2E + NC], consts[:, E2E:E2E + NC],
                                    1.0, 1e-6, ALU.mult, ALU.add)
                    e.drain()
                    for (dst, Psrc) in ((AAIN, Pp), (BBIN, Bp)):
                        e.tensor_tensor(consts[:, dst:dst + NC], Psrc[:, 2 * NC:3 * NC],
                                        consts[:, D2048:D2048 + NC], ALU.mult)
                        e.drain()
                        e.tensor_tensor(consts[:, dst:dst + NC], consts[:, dst:dst + NC],
                                        Psrc[:, NC:2 * NC], ALU.add)
                        e.drain()
                        e.tensor_tensor(consts[:, dst:dst + NC], consts[:, dst:dst + NC],
                                        consts[:, D2048:D2048 + NC], ALU.mult)
                        e.drain()
                        e.tensor_tensor(consts[:, dst:dst + NC], consts[:, dst:dst + NC],
                                        Psrc[:, 0:NC], ALU.add)
                        e.drain()
            for c in range(NC):
                with nc.Block() as blk:
                    @blk.vector
                    def _(e):
                        e.tensor_scalar(c1bc[:, :], onesT[:, :],
                                        consts[:, C1 + c:C1 + c + 1], 0.0, ALU.mult, ALU.add)
                        e.tensor_scalar(cv[:, :], vT_sb[:, c * TPC:(c + 1) * TPC],
                                        consts[:, C2 + c:C2 + c + 1], 0.0, ALU.mult, ALU.add)
                        e.tensor_copy(Sa[:, 0:1], consts[:, AAIN + c:AAIN + c + 1])
                        e.drain()
                        e.tensor_tensor_scan(Sa[:, 1:TPC + 1], c1bc[:, :], cv[:, :],
                                             consts[:, AAIN + c:AAIN + c + 1],
                                             ALU.mult, ALU.add)
                        e.tensor_scalar(cv[:, :], onesT[:, :],
                                        consts[:, C2 + c:C2 + c + 1], 0.0, ALU.mult, ALU.add)
                        e.tensor_copy(Sb[:, 0:1], consts[:, BBIN + c:BBIN + c + 1])
                        e.drain()
                        e.tensor_tensor_scan(Sb[:, 1:TPC + 1], c1bc[:, :], cv[:, :],
                                             consts[:, BBIN + c:BBIN + c + 1],
                                             ALU.mult, ALU.add)
                        e.drain()
                        e.tensor_scalar(num[:, :], Sa[:, 0:TPC],
                                        consts[:, E1 + c:E1 + c + 1], 0.0, ALU.mult, ALU.add)
                        e.drain()
                        e.scalar_tensor_tensor(num[:, :], vT_sb[:, c * TPC:(c + 1) * TPC],
                                               consts[:, E2R + c:E2R + c + 1],
                                               num[:, :], ALU.mult, ALU.add)
                        e.tensor_scalar(dde[:, :], Sb[:, 0:TPC],
                                        consts[:, E1 + c:E1 + c + 1],
                                        consts[:, E2E + c:E2E + c + 1], ALU.mult, ALU.add)
                        e.drain()
                        e.reciprocal(dde[:, :], dde[:, :])
                        e.drain()
                        e.tensor_tensor(num[:, :], num[:, :], dde[:, :], ALU.mult)
                        e.drain()
                        e.tensor_tensor(rwT[:, c * TPC:(c + 1) * TPC], num[:, :],
                                        rwT[:, c * TPC:(c + 1) * TPC], ALU.mult)
        with nc.Block() as blk:
            @blk.gpsimd
            def _(e):
                for c in range(NC):
                    e.dma_start(out=wbuf[:, c * D:(c + 1) * D],
                                in_=woT_in[c * 128:(c + 1) * 128, :]).then_inc(sg, 16)
                    gs[0] += 16
                e.wait_ge(sg, gs[0])
        with nc.Block() as blk:
            @blk.tensor
            def _(e):
                for t in range(NT):
                    if t >= 2:
                        e.wait_ge(sev, 2 * (t - 1))
                    for c in range(NC):
                        lhsT = rwT[:, c * TPC + t * 128:c * TPC + (t + 1) * 128]
                        for n in range(2):
                            ins = e.matmul(ps[2 * (t % 2) + n][:, :], lhsT,
                                           wbuf[:, c * D + n * 512:c * D + (n + 1) * 512],
                                           start=(c == 0), stop=(c == NC - 1))
                    ins.then_inc(smm, 1)
            @blk.scalar
            def _(e):
                for t in range(NT):
                    e.wait_ge(smm, t + 1)
                    for n in range(2):
                        e.copy(y_sb[:, t * D + n * 512:t * D + (n + 1) * 512],
                               ps[2 * (t % 2) + n][:, :]).then_inc(sev, 1)
        emit_ln_resid_streamx(nc, ctx, y_sb, x_in, x3_out, NT, "E")



def build_G(debug=False):
    nc = bass.Bass("TRN2", target_bir_lowering=False)
    io = {"x": nc.dram_tensor("x", [TPC, D], F32, kind="ExternalInput").ap(),
          "k0": nc.dram_tensor("k0", [2 * D, 4096], U8, kind="ExternalInput").ap(),
          "k1": nc.dram_tensor("k1", [4096, 4096], U8, kind="ExternalInput").ap(),
          "k2": nc.dram_tensor("k2", [4096, D], U8, kind="ExternalInput").ap(),
          "sv": nc.dram_tensor("sv", [1, 3], F32, kind="ExternalInput").ap(),
          "out": nc.dram_tensor("out", [TPC, D], F32, kind="ExternalOutput").ap()}
    emit_G(nc, io, debug=debug)
    return nc


def emit_G(nc, io, debug=False):
    x_in = io["x"]
    k0_in, k1_in, k2_in, s_in = io["k0"], io["k1"], io["k2"], io["sv"]
    out_d = io["out"]
    if debug:
        dbg_h1 = nc.dram_tensor("dbg_h1", [32 * 128, 1024], BF16, kind="ExternalOutput").ap()
        dbg_h2 = nc.dram_tensor("dbg_h2", [32 * 128, 1024], BF16, kind="ExternalOutput").ap()
        dbg_y = nc.dram_tensor("dbg_y", [D, TPC], BF16, kind="ExternalOutput").ap()
        dbg_xn = nc.dram_tensor("dbg_xn", [D, TPC], BF16, kind="ExternalOutput").ap()
    xn_d = nc.dram_tensor("xn_d", [TPC, D], BF16)
    w0_d = nc.dram_tensor("w0_d", [D, 4096], BF16)
    w1_d = nc.dram_tensor("w1_d", [32 * 32 * 128, 128], BF16)
    w2_d = nc.dram_tensor("w2_d", [8 * 32 * 128, 128], BF16)
    y_d = nc.dram_tensor("y_d", [D, TPC], BF16)
    N0 = 4096
    TB = 1024
    NB = TPC // TB

    with ExitStack() as ctx:
        ds, gs = [0], [0]
        sd = ctx.enter_context(nc.semaphore("sd"))
        sg = ctx.enter_context(nc.semaphore("sg"))
        smm = ctx.enter_context(nc.semaphore("smm"))
        sev = ctx.enter_context(nc.semaphore("sev"))
        scv = ctx.enter_context(nc.semaphore("scv"))
        syp = [ctx.enter_context(nc.semaphore(f"syp{j}")) for j in range(2)]
        sypn = [0, 0]
        slt = [ctx.enter_context(nc.semaphore(f"slt{j}")) for j in range(4)]
        sltn = [0, 0, 0, 0]
        scl = [ctx.enter_context(nc.semaphore(f"scl{j}")) for j in range(2)]
        scln = [0, 0]
        sgs = [ctx.enter_context(nc.semaphore(f"sgs{j}")) for j in range(2)]
        sgsn = [0, 0]
        cnt = {"mm": 0, "ev": 0, "cv": 0}
        scol = ctx.enter_context(nc.sbuf_tensor("scol", [128, 4], F32))
        hashpad = ctx.enter_context(nc.sbuf_tensor("hashpad", [1, 8], F32))
        srow = ctx.enter_context(nc.sbuf_tensor("srow", [1, 3], F32))
        ones32 = ctx.enter_context(nc.sbuf_tensor("ones32", [1, 128], F32))
        ybf = ctx.enter_context(nc.sbuf_tensor("ybf", [128, NT * D], BF16))
        with ExitStack() as ph1:
            x_sb = ph1.enter_context(nc.sbuf_tensor("x_sb", [128, NT * D], F32))
            xbf = ph1.enter_context(nc.sbuf_tensor("xbf", [128, NT * D], BF16))
            with nc.Block() as blk:
                @blk.sync
                def _(e):
                    emit_load_tiles(e, sd, x_sb, x_in, NT, ds)
                    e.dma_start(out=srow[:, :], in_=s_in).then_inc(sd, 16)
                    ds[0] += 16
                    e.wait_ge(sd, ds[0])
                @blk.vector
                def _(e):
                    e.memset(ones32[:, :], 1.0)
                    e.memset(hashpad[:, :], 3.0)
            emit_ln(nc, ph1, x_sb, NT, xbf, "G")
            with nc.Block() as blk:
                @blk.sync
                def _(e):
                    emit_store_tiles(e, sd, xn_d, xbf, NT, ds)
                    e.wait_ge(sd, ds[0])
        psb = ctx.enter_context(nc.psum_tensor("psb", [128, 4], F32))
        with nc.Block() as blk:
            @blk.tensor
            def _(e):
                e.matmul(psb[:, 0:3], ones32[:1, :], srow[:1, :],
                         start=True, stop=True).then_inc(smm, 1)
            @blk.vector
            def _(e):
                e.wait_ge(smm, 1)
                e.tensor_scalar(scol[:, 0:3], psb[:, 0:3], 2.0, 0.0, ALU.mult, ALU.add)
        cnt["mm"] = 1
        with nc.named_scope("conv"), ExitStack() as cvc:
            u8a = cvc.enter_context(nc.sbuf_tensor("u8a", [128, 2 * 4096], U8))
            u8b = cvc.enter_context(nc.sbuf_tensor("u8b", [128, 2 * 4096], U8))
            wcv = cvc.enter_context(nc.sbuf_tensor("wcv", [128, 2 * 4096], BF16))
            b_cv = cnt["cv"]
            b_scl, b_sgs = list(scln), list(sgsn)
            with nc.Block() as blk:
                @blk.sync
                def _(e):
                    for c in range(NC):
                        if c >= 2:
                            e.wait_ge(scv, b_cv + c - 1)
                        e.dma_start(out=u8a[:, (c % 2) * 4096:(c % 2 + 1) * 4096],
                                    in_=k0_in[c * 128:(c + 1) * 128, :]
                                    ).then_inc(scl[c % 2], 16)
                        e.dma_start(out=u8b[:, (c % 2) * 4096:(c % 2 + 1) * 4096],
                                    in_=k0_in[D + c * 128:D + (c + 1) * 128, :]
                                    ).then_inc(scl[c % 2], 16)
                        scln[c % 2] += 32
                @blk.vector
                def _(e):
                    for c in range(NC):
                        e.wait_ge(scl[c % 2], b_scl[c % 2] + 32 * (c // 2 + 1))
                        if c >= 2:
                            e.wait_ge(sgs[c % 2], b_sgs[c % 2] + 16 * (c // 2))
                        e.tensor_tensor(wcv[:, (c % 2) * 4096:(c % 2 + 1) * 4096],
                                        u8a[:, (c % 2) * 4096:(c % 2 + 1) * 4096],
                                        u8b[:, (c % 2) * 4096:(c % 2 + 1) * 4096],
                                        ALU.subtract).then_inc(scv, 1)
                @blk.gpsimd
                def _(e):
                    for c in range(NC):
                        e.wait_ge(scv, b_cv + c + 1)
                        e.dma_start(out=w0_d[c * 128:(c + 1) * 128, :],
                                    in_=wcv[:, (c % 2) * 4096:(c % 2 + 1) * 4096]
                                    ).then_inc(sgs[c % 2], 16)
                        sgsn[c % 2] += 16
            cnt["cv"] = b_cv + NC
            for (ksrc, wdst, nchunks, ntiles) in ((k1_in, w1_d, 32, 32),
                                                  (k2_in, w2_d, 32, 8)):
                b_cv = cnt["cv"]
                b_scl, b_sgs = list(scln), list(sgsn)
                width = ntiles * 128
                with nc.Block() as blk:
                    @blk.sync
                    def _(e):
                        for c in range(nchunks):
                            if c >= 2:
                                e.wait_ge(scv, b_cv + c - 1)
                            e.dma_start(out=u8a[:, (c % 2) * width:(c % 2) * width + width],
                                        in_=ksrc[c * 128:(c + 1) * 128, :]
                                        ).then_inc(scl[c % 2], 16)
                            scln[c % 2] += 16
                    @blk.vector
                    def _(e):
                        for c in range(nchunks):
                            e.wait_ge(scl[c % 2], b_scl[c % 2] + 16 * (c // 2 + 1))
                            if c >= 2:
                                e.wait_ge(sgs[c % 2], b_sgs[c % 2] + 16 * (c // 2))
                            e.tensor_scalar(wcv[:, (c % 2) * width:(c % 2) * width + width],
                                            u8a[:, (c % 2) * width:(c % 2) * width + width],
                                            2.0, 1.0, ALU.mult,
                                            ALU.subtract).then_inc(scv, 1)
                    @blk.gpsimd
                    def _(e):
                        for c in range(nchunks):
                            e.wait_ge(scv, b_cv + c + 1)
                            e.dma_start(
                                out=bass.AP(wdst.ap().tensor, c * 128 * 128,
                                            [[128, 128],
                                             [nchunks * 128 * 128, ntiles],
                                             [1, 128]]),
                                in_=wcv[:, (c % 2) * width:(c % 2) * width + width]
                                ).then_inc(sgs[c % 2], 16)
                            sgsn[c % 2] += 16
                cnt["cv"] = b_cv + nchunks
        xT = ctx.enter_context(nc.sbuf_tensor("xT", [128, NC * TPC], BF16))
        with nc.Block() as blk:
            @blk.gpsimd
            def _(e):
                e.wait_ge(sgs[0], sgsn[0])
                e.wait_ge(sgs[1], sgsn[1])
            @blk.sync
            def _(e):
                for c in range(NC):
                    e.dma_start(out=xT[:, c * TPC:(c + 1) * TPC],
                                in_=xn_d[:, c * 128:(c + 1) * 128],
                                transpose=True).then_inc(sd, 16)
                    ds[0] += 16
                e.wait_ge(sd, ds[0])
        ps4 = [ctx.enter_context(nc.psum_tensor(f"ps4{n}", [128, 512], F32))
               for n in range(4)]
        for tb in range(NB):
            t0 = tb * TB
            with nc.named_scope(f"tb{tb}"), ExitStack() as tbs:
                h1fm = tbs.enter_context(nc.sbuf_tensor(f"h1fm{tb}", [128, 32 * TB], BF16))
                with nc.named_scope("mm0"), ExitStack() as w0s:
                    w0b = w0s.enter_context(nc.sbuf_tensor(f"w0b{tb}", [128, NC * 4096], BF16))
                    with nc.Block() as blk:
                        @blk.sync
                        def _(e):
                            for c in range(NC):
                                e.dma_start(out=w0b[:, c * 4096:(c + 1) * 4096],
                                            in_=w0_d[c * 128:(c + 1) * 128, :]).then_inc(sd, 16)
                                ds[0] += 16
                            e.wait_ge(sd, ds[0])
                    b_mm, b_ev = cnt["mm"], cnt["ev"]
                    with nc.Block() as blk:
                        @blk.tensor
                        def _(e):
                            for n0 in range(32):
                                if n0 >= 2:
                                    e.wait_ge(sev, b_ev + n0 - 1)
                                for c in range(NC):
                                    lhsT = w0b[:, c * 4096 + n0 * 128:c * 4096 + (n0 + 1) * 128]
                                    for q in range(2):
                                        ins = e.matmul(
                                            ps4[2 * (n0 % 2) + q][:, :], lhsT,
                                            xT[:, c * TPC + t0 + q * 512:c * TPC + t0 + (q + 1) * 512],
                                            start=(c == 0), stop=(c == NC - 1))
                                ins.then_inc(smm, 1)
                        @blk.scalar
                        def _(e):
                            for n0 in range(32):
                                e.wait_ge(smm, b_mm + n0 + 1)
                                for q in range(2):
                                    ins = e.activation(
                                        h1fm[:, n0 * TB + q * 512:n0 * TB + (q + 1) * 512],
                                        ps4[2 * (n0 % 2) + q][:, :], AF.Relu,
                                        scale=scol[:, 0:1])
                                ins.then_inc(sev, 1)
                    cnt["mm"] = b_mm + 32
                    cnt["ev"] = b_ev + 32
                h2fm = tbs.enter_context(nc.sbuf_tensor(f"h2fm{tb}", [128, 32 * TB], BF16))
                w1st = tbs.enter_context(nc.sbuf_tensor(f"w1st{tb}", [128, 4 * 128], BF16))
                b_mm, b_ev, b_sd = cnt["mm"], cnt["ev"], ds[0]
                b_slt = list(sltn)
                with nc.named_scope("mm1"), nc.Block() as blk:
                    @blk.sync
                    def _(e):
                        for i in range(32 * 32):
                            if i >= 4:
                                e.wait_ge(smm, b_mm + i - 3)
                            e.dma_start(out=w1st[:, (i % 4) * 128:(i % 4 + 1) * 128],
                                        in_=w1_d[i * 128:(i + 1) * 128, :]).then_inc(slt[i % 4], 16)
                            sltn[i % 4] += 16
                        for j in range(4):
                            e.wait_ge(slt[j], sltn[j])
                    @blk.tensor
                    def _(e):
                        for i in range(32 * 32):
                            n1, c = divmod(i, 32)
                            e.wait_ge(slt[i % 4], b_slt[i % 4] + 16 * (i // 4 + 1))
                            if c == 0 and n1 >= 2:
                                e.wait_ge(sev, b_ev + n1 - 1)
                            lhsT = w1st[:, (i % 4) * 128:(i % 4 + 1) * 128]
                            for q in range(2):
                                ins = e.matmul(
                                    ps4[2 * (n1 % 2) + q][:, :], lhsT,
                                    h1fm[:, c * TB + q * 512:c * TB + (q + 1) * 512],
                                    start=(c == 0), stop=(c == 31))
                            ins.then_inc(smm, 1)
                    @blk.scalar
                    def _(e):
                        for n1 in range(32):
                            e.wait_ge(smm, b_mm + 32 * (n1 + 1))
                            for q in range(2):
                                ins = e.activation(
                                    h2fm[:, n1 * TB + q * 512:n1 * TB + (q + 1) * 512],
                                    ps4[2 * (n1 % 2) + q][:, :], AF.Relu,
                                    scale=scol[:, 1:2])
                            ins.then_inc(sev, 1)
                cnt["mm"] = b_mm + 32 * 32
                cnt["ev"] = b_ev + 32
                if debug and tb == 0:
                    with nc.Block() as blk:
                        @blk.sync
                        def _(e):
                            for n0 in range(32):
                                e.dma_start(out=dbg_h1[n0 * 128:(n0 + 1) * 128, :],
                                            in_=h1fm[:, n0 * TB:(n0 + 1) * TB]).then_inc(sd, 16)
                                e.dma_start(out=dbg_h2[n0 * 128:(n0 + 1) * 128, :],
                                            in_=h2fm[:, n0 * TB:(n0 + 1) * TB]).then_inc(sd, 16)
                                ds[0] += 32
                            e.wait_ge(sd, ds[0])
                ystage = tbs.enter_context(nc.sbuf_tensor(f"yst{tb}", [128, 2 * TB], BF16))
                w2st = tbs.enter_context(nc.sbuf_tensor(f"w2st{tb}", [128, 4 * 128], BF16))
                b_mm, b_ev = cnt["mm"], cnt["ev"]
                b_slt = list(sltn)
                b_syp = list(sypn)
                with nc.named_scope("mm2"), nc.Block() as blk:
                    @blk.sync
                    def _(e):
                        for i in range(8 * 32):
                            if i >= 4:
                                e.wait_ge(smm, b_mm + i - 3)
                            e.dma_start(out=w2st[:, (i % 4) * 128:(i % 4 + 1) * 128],
                                        in_=w2_d[i * 128:(i + 1) * 128, :]).then_inc(slt[i % 4], 16)
                            sltn[i % 4] += 16
                            if i % 32 == 31:
                                dt = i // 32
                                e.wait_ge(sev, b_ev + dt + 1)
                                e.dma_start(out=y_d[dt * 128:(dt + 1) * 128, t0:t0 + TB],
                                            in_=ystage[:, (dt % 2) * TB:(dt % 2 + 1) * TB]
                                            ).then_inc(syp[dt % 2], 16)
                                sypn[dt % 2] += 16
                        for j in range(4):
                            e.wait_ge(slt[j], sltn[j])
                        e.wait_ge(syp[0], sypn[0])
                        e.wait_ge(syp[1], sypn[1])
                    @blk.tensor
                    def _(e):
                        for i in range(8 * 32):
                            dt, c = divmod(i, 32)
                            e.wait_ge(slt[i % 4], b_slt[i % 4] + 16 * (i // 4 + 1))
                            if c == 0 and dt >= 2:
                                e.wait_ge(sev, b_ev + dt - 1)
                            lhsT = w2st[:, (i % 4) * 128:(i % 4 + 1) * 128]
                            for q in range(2):
                                ins = e.matmul(
                                    ps4[2 * (dt % 2) + q][:, :], lhsT,
                                    h2fm[:, c * TB + q * 512:c * TB + (q + 1) * 512],
                                    start=(c == 0), stop=(c == 31))
                            ins.then_inc(smm, 1)
                    @blk.scalar
                    def _(e):
                        for dt in range(8):
                            e.wait_ge(smm, b_mm + 32 * (dt + 1))
                            if dt >= 2:
                                e.wait_ge(syp[dt % 2], b_syp[dt % 2] + 16 * (dt // 2))
                            for q in range(2):
                                ins = e.activation(
                                    ystage[:, (dt % 2) * TB + q * 512:(dt % 2) * TB + (q + 1) * 512],
                                    ps4[2 * (dt % 2) + q][:, :], AF.Copy,
                                    scale=scol[:, 2:3])
                            ins.then_inc(sev, 1)
                cnt["mm"] = b_mm + 8 * 32
                cnt["ev"] = b_ev + 8
        if debug:
            with nc.Block() as blk:
                @blk.sync
                def _(e):
                    for c in range(NC):
                        e.dma_start(out=dbg_y[c * 128:(c + 1) * 128, :],
                                    in_=y_d[c * 128:(c + 1) * 128, :]).then_inc(sd, 16)
                        e.dma_start(out=dbg_xn[c * 128:(c + 1) * 128, :],
                                    in_=xT[:, c * TPC:(c + 1) * TPC]).then_inc(sd, 16)
                        ds[0] += 32
                    e.wait_ge(sd, ds[0])
        with nc.named_scope("tail"):
            with nc.Block() as blk:
                @blk.sync
                def _(e):
                    for t in range(NT):
                        e.dma_start(out=ybf[:, t * D:(t + 1) * D],
                                    in_=y_d[:, t * 128:(t + 1) * 128],
                                    transpose=True).then_inc(sd, 16)
                        ds[0] += 16
                    e.wait_ge(sd, ds[0])
            emit_ln_resid_streamx(nc, ctx, ybf, x_in, out_d, NT, "G2")



def build_full(dbg=False):
    nc = bass.Bass("TRN2", target_bir_lowering=False)
    I32 = mybir.dt.int32
    ein = lambda n, sh, dt: nc.dram_tensor(n, sh, dt, kind="ExternalInput").ap()
    x_in = ein("x", [TPC, D], F32)
    wq_in = ein("wq", [D, 3 * D], F32)
    bq_in = ein("bq", [1, 3 * D], F32)
    wo_in = ein("wo", [D, D], F32)
    bo_in = ein("bo", [1, D], F32)
    wrT_in = ein("wrT", [D, D], F32)
    wkT_in = ein("wkT", [D, D], F32)
    wvT_in = ein("wvT", [D, D], F32)
    mur_in = ein("mur", [1, D], F32)
    muk_in = ein("muk", [1, D], F32)
    wv2_in = ein("wv2T", [D, D], F32)
    wr2_in = ein("wr2T", [D, D], F32)
    td_in = ein("td", [1, D], F32)
    tf_in = ein("tf", [1, D], F32)
    wo2_in = ein("wo2T", [D, D], F32)
    k0_in = ein("k0", [2 * D, 4096], U8)
    k1_in = ein("k1", [4096, 4096], U8)
    k2_in = ein("k2", [4096, D], U8)
    sv_in = ein("sv", [1, 3], F32)
    offs_in = ein("offs", [1, 32], I32)
    out_d = nc.dram_tensor("out", [TPC, D], F32, kind="ExternalOutput").ap()

    kindh = "ExternalOutput" if dbg else "Internal"
    qkv_d = nc.dram_tensor("qkv_d", [TPC, 3 * D], BF16).ap()
    qkv_g = nc.dram_tensor("qkv_g", [NCORES * TPC, 3 * D], BF16).ap()
    ctxn_d = nc.dram_tensor("ctxn_d", [4 * 64, S], BF16).ap()
    ctxn_g = nc.dram_tensor("ctxn_g", [NCORES * 4 * 64, S], BF16).ap()
    x1_d = nc.dram_tensor("x1_d", [TPC, D], F32, kind=kindh).ap()
    lr_in = nc.dram_tensor("lr_in", [2, D], F32).ap()
    lr_g = nc.dram_tensor("lr_g", [2 * NCORES, D], F32).ap()
    x2_d = nc.dram_tensor("x2_d", [TPC, D], F32, kind=kindh).ap()
    vT_d = nc.dram_tensor("vT_d", [D, TPC], BF16).ap()
    rT_d = nc.dram_tensor("rT_d", [D, TPC], BF16).ap()
    PB_in = nc.dram_tensor("PB_in", [256, 2 * NC], F32).ap()
    PB_g = nc.dram_tensor("PB_g", [256 * NCORES, 2 * NC], F32).ap()
    x3_d = nc.dram_tensor("x3_d", [TPC, D], F32, kind=kindh).ap()

    grp = [list(range(NCORES))]
    with ExitStack() as prog:
        set_pfx(nc, "F_")
        nc._sem_stack = prog
        osb = prog.enter_context(nc.sbuf_tensor("osb", [1, 32], I32))
        soff = prog.enter_context(nc.semaphore("soff"))
        ccs = prog.enter_context(nc.semaphore("ccs"))
        with nc.Block() as blk:
            @blk.sync
            def _(e):
                e.dma_start(out=osb[:, :], in_=offs_in).then_inc(soff, 16)
                e.wait_ge(soff, 16)

        set_pfx(nc, "A_")
        with nc.named_scope("phA"):
            emit_A(nc, {"x": x_in, "wq": wq_in, "bq": bq_in, "qkv": qkv_d})
        set_pfx(nc, "c1_")
        with nc.named_scope("cc1"), nc.Block() as blk:
            @blk.gpsimd
            def _(e):
                e.collective_compute("AllGather", ALU.bypass, replica_groups=grp,
                                     ins=[qkv_d], outs=[qkv_g]).then_inc(ccs, 1)
                e.wait_ge(ccs, 1)

        set_pfx(nc, "B_")
        with nc.named_scope("phB"):
            emit_B(nc, {"qkv_g": qkv_g, "offsb": osb, "ctxn": ctxn_d}, SS=S)
        set_pfx(nc, "c2_")
        with nc.named_scope("cc2"), nc.Block() as blk:
            @blk.gpsimd
            def _(e):
                e.collective_compute("AllGather", ALU.bypass, replica_groups=grp,
                                     ins=[ctxn_d], outs=[ctxn_g]).then_inc(ccs, 1)
                e.wait_ge(ccs, 2)

        set_pfx(nc, "C_")
        with nc.named_scope("phC"):
            emit_C(nc, {"wo": wo_in, "bo": bo_in, "x": x_in, "x1": x1_d,
                        "ctxn_g": ctxn_g, "offsb": osb})
        set_pfx(nc, "c3_")
        with nc.named_scope("cc3"), ExitStack() as hctx:
            z1 = hctx.enter_context(nc.sbuf_tensor("z1d", [1, D], F32))
            slr = prog.enter_context(nc.semaphore("slr"))
            with nc.Block() as blk:
                @blk.vector
                def _(e):
                    e.memset(z1[:, :], 0.0)
            with nc.Block() as blk:
                @blk.sync
                def _(e):
                    e.dma_start(out=lr_in[0:1, :],
                                in_=x1_d[TPC - 1:TPC, :]).then_inc(slr, 16)
                    e.dma_start(out=lr_in[1:2, :], in_=z1[:, :]).then_inc(slr, 16)
                    e.wait_ge(slr, 32)
            with nc.Block() as blk:
                @blk.gpsimd
                def _(e):
                    e.collective_compute("AllGather", ALU.bypass, replica_groups=grp,
                                         ins=[lr_in], outs=[lr_g]).then_inc(ccs, 1)
                    e.wait_ge(ccs, 3)

        set_pfx(nc, "D_")
        with nc.named_scope("phD"):
            emit_D(nc, {"x": x1_d, "wrT": wrT_in, "wkT": wkT_in, "wvT": wvT_in,
                        "mur": mur_in, "muk": muk_in, "x2": x2_d,
                        "lr_g": lr_g, "offsb": osb})
        set_pfx(nc, "E0_")
        with nc.named_scope("phE0"):
            emit_E0(nc, {"x": x2_d, "wvT": wv2_in, "wrT": wr2_in, "td": td_in,
                         "vT": vT_d, "rT": rT_d, "PBfull": PB_in})
        set_pfx(nc, "c4_")
        with nc.named_scope("cc4"), nc.Block() as blk:
            @blk.gpsimd
            def _(e):
                e.collective_compute("AllGather", ALU.bypass, replica_groups=grp,
                                     ins=[PB_in], outs=[PB_g]).then_inc(ccs, 1)
                e.wait_ge(ccs, 4)

        set_pfx(nc, "E_")
        with nc.named_scope("phE"):
            emit_E(nc, {"x": x2_d, "vT": vT_d, "rT": rT_d, "td": td_in, "tf": tf_in,
                        "woT": wo2_in, "x3": x3_d, "PB_g": PB_g, "offsb": osb})
        set_pfx(nc, "G_")
        with nc.named_scope("phG"):
            emit_G(nc, {"x": x3_d, "k0": k0_in, "k1": k1_in, "k2": k2_in, "sv": sv_in,
                        "out": out_d})
        if dbg:
            set_pfx(nc, "")
            qg_s = nc.dram_tensor("qg_s", [128, 3 * D], BF16,
                                  kind="ExternalOutput").ap()
            cg_s = nc.dram_tensor("cg_s", [128, S], BF16,
                                  kind="ExternalOutput").ap()
            set_pfx(nc, "dbg_")
            with ExitStack() as dctx:
                t1s = dctx.enter_context(nc.sbuf_tensor("smp1", [128, 3 * D], BF16))
                t2s = dctx.enter_context(nc.sbuf_tensor("smp2", [128, S], BF16))
                sdd = prog.enter_context(nc.semaphore("sdd"))
                with nc.Block() as blk:
                    @blk.sync
                    def _(e):
                        with e.register("dqs") as off:
                            e.reg_load(off, osb[0:1, 21:22])
                            e.dma_start(out=t1s[:, :],
                                        in_=bass.AP(qkv_g.tensor, off,
                                                    [[64 * 3 * D, 128], [1, 3 * D]])
                                        ).then_inc(sdd, 16)
                        e.dma_start(out=t2s[:, :],
                                    in_=bass.AP(ctxn_g.tensor, 0,
                                                [[16 * S, 128], [1, S]])
                                    ).then_inc(sdd, 16)
                        e.wait_ge(sdd, 32)
                        e.dma_start(out=qg_s, in_=t1s[:, :]).then_inc(sdd, 16)
                        e.dma_start(out=cg_s, in_=t2s[:, :]).then_inc(sdd, 16)
                        e.wait_ge(sdd, 64)
        nc._sem_stack = None
    return nc


HEADPERM = [h for g in range(NG) for h in (g, NG + 3 * g, NG + 3 * g + 1,
                                           NG + 3 * g + 2)]
COLPERM = np.concatenate([np.arange(HD * h, HD * (h + 1)) for h in HEADPERM])


def core_offsets(c):
    bb, g = c // 4, c % 4
    off = np.zeros(32, np.int32)
    qbase = S * bb * 3 * D
    for p in range(2):
        off[0 + p] = qbase + D + 256 * g + 128 * p
        off[2 + p] = qbase + 0 + 256 * g + 128 * p
        off[4 + p] = qbase + 2 * D + 256 * g + 128 * p
    for cc in range(NC):
        off[6 + cc] = (256 * (4 * bb + cc // 2) + 128 * (cc % 2)) * S + g * TPC
    off[14] = D if g == 0 else 2 * (c - 1) * D
    off[21] = S * bb * 3 * D
    for k in (1, 2, 3):
        if g >= k:
            off[14 + k] = 256 * (c - k) * 2 * NC
            off[17 + k] = 256 * (c - k) * 2 * NC + NC
        else:
            off[14 + k] = 128 * 2 * NC
            off[17 + k] = 128 * 2 * NC + NC
    return off.reshape(1, 32)



_CACHE = {}


def _run(name, builder, in_maps):
    if name not in _CACHE:
        _CACHE[name] = builder()
    res = run_bass_kernel_spmd(_CACHE[name], in_maps, core_ids=list(range(NCORES)))
    return res.results


def kernel_unfused(L0_attn_x, L0_attn_w_qkv, L0_attn_b_qkv, L0_attn_w_out,
           L0_attn_b_out,
           L0_ffn_W_r, L0_ffn_W_k, L0_ffn_W_v, L0_ffn_mu_r, L0_ffn_mu_k,
           L1_attn_time_decay, L1_attn_time_first, L1_attn_key_w, L1_attn_value_w,
           L1_attn_recep_w, L1_attn_out_w, L1_ffn_k0, L1_ffn_k1, L1_ffn_k2,
           L1_ffn_s0, L1_ffn_s1, L1_ffn_s2):
    C = np.ascontiguousarray
    xflat = np.asarray(L0_attn_x, np.float32).reshape(B * S, D)
    xsl = [C(xflat[c * TPC:(c + 1) * TPC]) for c in range(NCORES)]
    wq = C(np.asarray(L0_attn_w_qkv, np.float32))
    bq = C(np.asarray(L0_attn_b_qkv, np.float32).reshape(1, 3 * D))

    resA = _run("A", build_A, [{"x": xsl[c], "wq": wq, "bq": bq} for c in range(NCORES)])
    qkv = np.concatenate([resA[c]["qkv"] for c in range(NCORES)], 0).reshape(B, S, 3 * D)

    inB = []
    for c in range(NCORES):
        m = {}
        for p in range(2):
            for nm, off in (("kt", D), ("qt", 0), ("vt", 2 * D)):
                arr = np.empty((128, S), BF)
                for s2 in range(2):
                    bb, hh = CORE_UNITS[c][2 * p + s2]
                    arr[64 * s2:64 * s2 + 64] = qkv[bb, :, off + HD * hh:off + HD * (hh + 1)].T
                m[f"{nm}{p}"] = C(arr)
        inB.append(m)
    resB = _run("B", build_B, inB)
    ctx_fm = np.empty((B, D, S), BF)
    for c in range(NCORES):
        for ui, (bb, hh) in enumerate(CORE_UNITS[c]):
            ctx_fm[bb, HD * hh:HD * (hh + 1), :] = resB[c]["ctxn"][64 * ui:64 * ui + 64]

    wo = C(np.asarray(L0_attn_w_out, np.float32))
    bo = C(np.asarray(L0_attn_b_out, np.float32).reshape(1, D))
    inC = [{"ctxf": C(ctx_fm[c // 4][:, (c % 4) * TPC:(c % 4 + 1) * TPC]),
            "wo": wo, "bo": bo, "x": xsl[c]} for c in range(NCORES)]
    resC = _run("C", build_C, inC)
    x1 = [np.asarray(resC[c]["x1"]) for c in range(NCORES)]

    wrT = C(np.asarray(L0_ffn_W_r, np.float32).T)
    wkT = C(np.asarray(L0_ffn_W_k, np.float32).T)
    wvT = C(np.asarray(L0_ffn_W_v, np.float32).T)
    mur = C(np.asarray(L0_ffn_mu_r, np.float32).reshape(1, D))
    muk = C(np.asarray(L0_ffn_mu_k, np.float32).reshape(1, D))
    zrow = np.zeros((1, D), np.float32)
    inD = []
    for c in range(NCORES):
        halo = zrow if c % 4 == 0 else C(x1[c - 1][TPC - 1:TPC, :])
        inD.append({"x": C(x1[c]), "halo": halo, "wrT": wrT, "wkT": wkT, "wvT": wvT,
                    "mur": mur, "muk": muk})
    resD = _run("D", build_D, inD)
    x2 = [np.asarray(resD[c]["x2"]) for c in range(NCORES)]

    wvT2 = C(np.asarray(L1_attn_value_w, np.float32).T)
    wrT2 = C(np.asarray(L1_attn_recep_w, np.float32).T)
    td = C(np.asarray(L1_attn_time_decay, np.float32).reshape(1, D))
    tf = C(np.asarray(L1_attn_time_first, np.float32).reshape(1, D))
    inE0 = [{"x": C(x2[c]), "wvT": wvT2, "wrT": wrT2, "td": td} for c in range(NCORES)]
    resE0 = _run("E0", build_E0, inE0)

    woT2 = C(np.asarray(L1_attn_out_w, np.float32).T)
    z128 = np.zeros((128, NC), np.float32)
    inE = []
    for c in range(NCORES):
        Pp = np.concatenate([np.asarray(resE0[c - k]["P"]) if c % 4 >= k else z128
                             for k in (1, 2, 3)], 1)
        Bp = np.concatenate([np.asarray(resE0[c - k]["Bb"]) if c % 4 >= k else z128
                             for k in (1, 2, 3)], 1)
        inE.append({"x": C(x2[c]), "vT": C(np.asarray(resE0[c]["vT"])),
                    "rT": C(np.asarray(resE0[c]["rT"])),
                    "Pp": C(Pp), "Bp": C(Bp), "td": td, "tf": tf, "woT": woT2})
    resE = _run("E", build_E, inE)
    x3 = [np.asarray(resE[c]["x3"]) for c in range(NCORES)]

    k0 = np.asarray(L1_ffn_k0).view(np.uint8)
    k1 = np.asarray(L1_ffn_k1).view(np.uint8)
    k2 = np.asarray(L1_ffn_k2).view(np.uint8)
    sv = C(np.stack([np.float32(L1_ffn_s0), np.float32(L1_ffn_s1),
                     np.float32(L1_ffn_s2)]).reshape(1, 3))
    inG = [{"x": C(x3[c]), "k0": C(k0), "k1": C(k1), "k2": C(k2), "sv": sv}
           for c in range(NCORES)]
    resG = _run("G", build_G, inG)
    out = np.concatenate([np.asarray(resG[c]["out"], np.float32)
                          for c in range(NCORES)], 0)
    return out.reshape(B, S, D)


def kernel(L0_attn_x, L0_attn_w_qkv, L0_attn_b_qkv, L0_attn_w_out, L0_attn_b_out,
           L0_ffn_W_r, L0_ffn_W_k, L0_ffn_W_v, L0_ffn_mu_r, L0_ffn_mu_k,
           L1_attn_time_decay, L1_attn_time_first, L1_attn_key_w, L1_attn_value_w,
           L1_attn_recep_w, L1_attn_out_w, L1_ffn_k0, L1_ffn_k1, L1_ffn_k2,
           L1_ffn_s0, L1_ffn_s1, L1_ffn_s2):
    C = np.ascontiguousarray
    xflat = np.asarray(L0_attn_x, np.float32).reshape(B * S, D)
    perm3 = np.concatenate([sec * D + COLPERM for sec in range(3)])
    wq = C(np.asarray(L0_attn_w_qkv, np.float32)[:, perm3])
    bq = C(np.asarray(L0_attn_b_qkv, np.float32).reshape(1, 3 * D)[:, perm3])
    wo = C(np.asarray(L0_attn_w_out, np.float32)[COLPERM, :])
    bo = C(np.asarray(L0_attn_b_out, np.float32).reshape(1, D))
    wrT = C(np.asarray(L0_ffn_W_r, np.float32).T)
    wkT = C(np.asarray(L0_ffn_W_k, np.float32).T)
    wvT = C(np.asarray(L0_ffn_W_v, np.float32).T)
    mur = C(np.asarray(L0_ffn_mu_r, np.float32).reshape(1, D))
    muk = C(np.asarray(L0_ffn_mu_k, np.float32).reshape(1, D))
    wv2T = C(np.asarray(L1_attn_value_w, np.float32).T)
    wr2T = C(np.asarray(L1_attn_recep_w, np.float32).T)
    td = C(np.asarray(L1_attn_time_decay, np.float32).reshape(1, D))
    tf = C(np.asarray(L1_attn_time_first, np.float32).reshape(1, D))
    wo2T = C(np.asarray(L1_attn_out_w, np.float32).T)
    k0 = C(np.asarray(L1_ffn_k0).view(np.uint8))
    k1 = C(np.asarray(L1_ffn_k1).view(np.uint8))
    k2 = C(np.asarray(L1_ffn_k2).view(np.uint8))
    sv = C(np.stack([np.float32(L1_ffn_s0), np.float32(L1_ffn_s1),
                     np.float32(L1_ffn_s2)]).reshape(1, 3))
    inF = []
    for c in range(NCORES):
        inF.append({"x": C(xflat[c * TPC:(c + 1) * TPC]), "wq": wq, "bq": bq,
                    "wo": wo, "bo": bo, "wrT": wrT, "wkT": wkT, "wvT": wvT,
                    "mur": mur, "muk": muk, "wv2T": wv2T, "wr2T": wr2T,
                    "td": td, "tf": tf, "wo2T": wo2T,
                    "k0": k0, "k1": k1, "k2": k2, "sv": sv,
                    "offs": core_offsets(c)})
    res = _run("F", build_full, inF)
    out = np.concatenate([np.asarray(res[c]["out"], np.float32)
                          for c in range(NCORES)], 0)
    return out.reshape(B, S, D)


# version 04
# speedup vs baseline: 1.3167x; 1.0063x over previous
import sys
sys.path.insert(0, "/opt/trn_rl_repo")
import numpy as np
import ml_dtypes
from contextlib import ExitStack, nullcontext

import concourse.bass as bass
import concourse.mybir as mybir
from concourse.bass_utils import run_bass_kernel_spmd

F32 = mybir.dt.float32
BF16 = mybir.dt.bfloat16
U8 = mybir.dt.uint8
AF = mybir.ActivationFunctionType
ALU = mybir.AluOpType
BF = ml_dtypes.bfloat16

NCORES = 8
B, S, D = 2, 8192, 1024
TPC = S * B // NCORES
NT = TPC // 128
NC = D // 128
NKT = S // 128
H, HD, NG, WIN = 16, 64, 4, 256
EPS = 1e-5
B0 = 40.0

GLOBAL_UNITS = [(b, h) for b in range(B) for h in range(NG)]
LOCAL_UNITS = [(b, h) for b in range(B) for h in range(NG, H)]
CORE_UNITS = [[GLOBAL_UNITS[c]] + LOCAL_UNITS[3 * c:3 * c + 3] for c in range(NCORES)]



def emit_stats(nc, ctx, src_sb, nt, stats, tag):
    scrA = ctx.enter_context(nc.sbuf_tensor(f"stA{tag}", [128, D], BF16))
    scrB = ctx.enter_context(nc.sbuf_tensor(f"stB{tag}", [128, D], BF16))
    SU, SQ, MU, RS = 0, nt, 2 * nt, 3 * nt
    with nc.Block() as blk:
        @blk.vector
        def _(e):
            for t in range(nt):
                if t:
                    e.drain()
                e.tensor_scalar(scrA[:, :], src_sb[:, t * D:(t + 1) * D], 1.0, 0.0,
                                ALU.mult, ALU.add, accum_out=stats[:, SU + t:SU + t + 1])
        @blk.scalar
        def _(e):
            for t in range(nt):
                if t:
                    e.drain()
                e.activation(scrB[:, :], src_sb[:, t * D:(t + 1) * D], AF.Square,
                             accum_out=stats[:, SQ + t:SQ + t + 1])
    with nc.Block() as blk:
        @blk.vector
        def _(e):
            e.tensor_scalar(stats[:, MU:MU + nt], stats[:, SU:SU + nt], 1.0 / D, 0.0,
                            ALU.mult, ALU.add)
            e.drain()
            e.tensor_tensor(stats[:, SU:SU + nt], stats[:, MU:MU + nt],
                            stats[:, MU:MU + nt], ALU.mult)
            e.tensor_scalar(stats[:, SQ:SQ + nt], stats[:, SQ:SQ + nt], 1.0 / D, EPS,
                            ALU.mult, ALU.add)
            e.drain()
            e.scalar_tensor_tensor(stats[:, SQ:SQ + nt], stats[:, SQ:SQ + nt], 1.0,
                                   stats[:, SU:SU + nt], ALU.mult, ALU.subtract)
    with nc.Block() as blk:
        @blk.scalar
        def _(e):
            e.activation(stats[:, SQ:SQ + nt], stats[:, SQ:SQ + nt], AF.Sqrt)
    with nc.Block() as blk:
        @blk.vector
        def _(e):
            e.reciprocal(stats[:, RS:RS + nt], stats[:, SQ:SQ + nt])
            e.drain()


def emit_ln(nc, ctx, x_sb, nt, out_bf, tag):
    stats = ctx.enter_context(nc.sbuf_tensor(f"lnst{tag}", [128, 4 * nt], F32))
    emit_stats(nc, ctx, x_sb, nt, stats, tag)
    with nc.Block() as blk:
        @blk.vector
        def _(e):
            for t in range(nt):
                e.tensor_scalar(out_bf[:, t * D:(t + 1) * D], x_sb[:, t * D:(t + 1) * D],
                                stats[:, 2 * nt + t:2 * nt + t + 1],
                                stats[:, 3 * nt + t:3 * nt + t + 1],
                                ALU.subtract, ALU.mult)


def emit_ln_resid_streamx(nc, ctx, y_sb, x_dram, out_dram, nt, tag):
    stats = ctx.enter_context(nc.sbuf_tensor(f"lrst{tag}", [128, 4 * nt], F32))
    emit_stats(nc, ctx, y_sb, nt, stats, tag + "r")
    xb = [ctx.enter_context(nc.sbuf_tensor(f"lrx{tag}{i}", [128, D], F32)) for i in range(2)]
    ob = [ctx.enter_context(nc.sbuf_tensor(f"lro{tag}{i}", [128, D], F32)) for i in range(2)]
    tmp = ctx.enter_context(nc.sbuf_tensor(f"lrt{tag}", [128, D], F32))
    sld = [ctx.enter_context(nc.semaphore(f"lrl{tag}{i}")) for i in range(2)]
    scp = ctx.enter_context(nc.semaphore(f"lrc{tag}"))
    sst = [ctx.enter_context(nc.semaphore(f"lrs{tag}{i}")) for i in range(2)]
    with nc.Block() as blk:
        @blk.sync
        def _(e):
            for t in range(nt):
                if t >= 2:
                    e.wait_ge(scp, t - 1)
                    e.dma_start(out=out_dram[(t - 2) * 128:(t - 1) * 128, :],
                                in_=ob[t % 2][:, :]).then_inc(sst[t % 2], 16)
                e.dma_start(out=xb[t % 2][:, :],
                            in_=x_dram[t * 128:(t + 1) * 128, :]).then_inc(sld[t % 2], 16)
            for t in (nt - 2, nt - 1):
                e.wait_ge(scp, t + 1)
                e.dma_start(out=out_dram[t * 128:(t + 1) * 128, :],
                            in_=ob[t % 2][:, :]).then_inc(sst[t % 2], 16)
            e.wait_ge(sst[0], 16 * (nt - nt // 2))
            e.wait_ge(sst[1], 16 * (nt // 2))
        @blk.vector
        def _(e):
            for t in range(nt):
                e.wait_ge(sld[t % 2], 16 * (t // 2 + 1))
                if t >= 2:
                    e.wait_ge(sst[t % 2], 16 * (t // 2))
                e.tensor_scalar(tmp[:, :], y_sb[:, t * D:(t + 1) * D],
                                stats[:, 2 * nt + t:2 * nt + t + 1],
                                stats[:, 3 * nt + t:3 * nt + t + 1],
                                ALU.subtract, ALU.mult)
                e.drain()
                e.scalar_tensor_tensor(ob[t % 2][:, :], tmp[:, :], 1.0,
                                       xb[t % 2][:, :], ALU.mult,
                                       ALU.add).then_inc(scp, 1)
                e.drain()


def emit_load_tiles(e, sem, x_sb, src, nt, ds):
    for t in range(nt):
        e.dma_start(out=x_sb[:, t * D:(t + 1) * D],
                    in_=src[t * 128:(t + 1) * 128, :]).then_inc(sem, 16)
        ds[0] += 16


def emit_store_tiles(e, sem, dst, x_sb, nt, ds):
    for t in range(nt):
        e.dma_start(out=dst[t * 128:(t + 1) * 128, :],
                    in_=x_sb[:, t * D:(t + 1) * D]).then_inc(sem, 16)
        ds[0] += 16


def col_load(e, sem, dst_col, src_row, ds):
    e.dma_start(out=dst_col, in_=src_row.rearrange("o p -> p o")).then_inc(sem, 16)
    ds[0] += 16


def wrap_names(nc):
    if getattr(nc, "_name_wrapped", False):
        return
    nc._name_wrapped = True
    nc._cur_pfx = ""
    for meth in ("sbuf_tensor", "psum_tensor", "dram_tensor"):
        orig = getattr(nc, meth)
        def mk(orig):
            def f(name, *a, **k):
                return orig(nc._cur_pfx + name, *a, **k)
            return f
        setattr(nc, meth, mk(orig))
    orig_sem = nc.semaphore
    def sem_f(name, *a, **k):
        cm = orig_sem(nc._cur_pfx + name, *a, **k)
        st = getattr(nc, "_sem_stack", None)
        if st is None:
            return cm
        return nullcontext(st.enter_context(cm))
    nc.semaphore = sem_f


def set_pfx(nc, pfx):
    wrap_names(nc)
    nc._cur_pfx = pfx



def build_A():
    nc = bass.Bass("TRN2", target_bir_lowering=False)
    io = {"x": nc.dram_tensor("x", [TPC, D], F32, kind="ExternalInput").ap(),
          "wq": nc.dram_tensor("wq", [D, 3 * D], F32, kind="ExternalInput").ap(),
          "bq": nc.dram_tensor("bq", [1, 3 * D], F32, kind="ExternalInput").ap(),
          "qkv": nc.dram_tensor("qkv", [TPC, 3 * D], BF16, kind="ExternalOutput").ap()}
    emit_A(nc, io)
    return nc


def emit_A(nc, io):
    x_in, wq_in, bq_in, qkv_out = io["x"], io["wq"], io["bq"], io["qkv"]
    if isinstance(qkv_out, list):
        qtile = lambda t: qkv_out[t // 4][(t % 4) * 128:(t % 4 + 1) * 128, :]
    else:
        qtile = lambda t: qkv_out[t * 128:(t + 1) * 128, :]
    xn_d = nc.dram_tensor("xn_d", [TPC, D], BF16)

    with ExitStack() as ctx:
        ds, gs = [0], [0]
        sd = ctx.enter_context(nc.semaphore("sd"))
        sg = ctx.enter_context(nc.semaphore("sg"))
        smm = ctx.enter_context(nc.semaphore("smm"))
        sev = ctx.enter_context(nc.semaphore("sev"))
        ssa = [ctx.enter_context(nc.semaphore(f"ssa{i}")) for i in range(2)]
        with ExitStack() as ph1:
            xs = ph1.enter_context(nc.sbuf_tensor("xs", [128, NT * D], F32))
            xbf = ph1.enter_context(nc.sbuf_tensor("xbf", [128, NT * D], BF16))
            with nc.Block() as blk:
                @blk.sync
                def _(e):
                    emit_load_tiles(e, sd, xs, x_in, NT, ds)
                    e.wait_ge(sd, ds[0])
            emit_ln(nc, ph1, xs, NT, xbf, "A")
            with nc.Block() as blk:
                @blk.sync
                def _(e):
                    emit_store_tiles(e, sd, xn_d, xbf, NT, ds)
                    e.wait_ge(sd, ds[0])
        xT = ctx.enter_context(nc.sbuf_tensor("xT", [128, NC * TPC], BF16))
        wqb = ctx.enter_context(nc.sbuf_tensor("wqb", [128, NC * 3 * D], BF16))
        bias = ctx.enter_context(nc.sbuf_tensor("bias", [128, 3 * D], F32))
        brow = ctx.enter_context(nc.sbuf_tensor("brow", [1, 3 * D], F32))
        ones32 = ctx.enter_context(nc.sbuf_tensor("ones32", [1, 128], F32))
        stage = ctx.enter_context(nc.sbuf_tensor("stage", [128, 2 * 3 * D], BF16))
        psb = [ctx.enter_context(nc.psum_tensor(f"psb{i}", [128, 512], F32)) for i in range(2)]
        with nc.Block() as blk:
            @blk.sync
            def _(e):
                for c in range(NC):
                    e.dma_start(out=xT[:, c * TPC:(c + 1) * TPC],
                                in_=xn_d[:, c * 128:(c + 1) * 128],
                                transpose=True).then_inc(sd, 16)
                    ds[0] += 16
                e.dma_start(out=brow[:, :], in_=bq_in).then_inc(sd, 16)
                ds[0] += 16
                e.wait_ge(sd, ds[0])
            @blk.gpsimd
            def _(e):
                for c in range(NC):
                    e.dma_start(out=wqb[:, c * 3 * D:(c + 1) * 3 * D],
                                in_=wq_in[c * 128:(c + 1) * 128, :]).then_inc(sg, 16)
                    gs[0] += 16
                e.wait_ge(sg, gs[0])
            @blk.vector
            def _(e):
                e.memset(ones32[:, :], 1.0)
        with nc.Block() as blk:
            @blk.tensor
            def _(e):
                for i in range(6):
                    if i >= 2:
                        e.wait_ge(sev, i - 1)
                    e.matmul(psb[i % 2][:, :], ones32[:1, :],
                             brow[:1, i * 512:(i + 1) * 512],
                             start=True, stop=True).then_inc(smm, 1)
            @blk.vector
            def _(e):
                for i in range(6):
                    e.wait_ge(smm, i + 1)
                    e.tensor_copy(bias[:, i * 512:(i + 1) * 512],
                                  psb[i % 2][:, :]).then_inc(sev, 1)
        ps = [ctx.enter_context(nc.psum_tensor(f"ps{n}", [128, 512], F32)) for n in range(6)]
        mm0, ev0 = 6, 6
        with nc.Block() as blk:
            @blk.tensor
            def _(e):
                for t in range(NT):
                    if t > 0:
                        e.wait_ge(sev, ev0 + 6 * t)
                    for c in range(NC):
                        lhsT = xT[:, c * TPC + t * 128:c * TPC + (t + 1) * 128]
                        for n in range(6):
                            ins = e.matmul(
                                ps[n][:, :], lhsT,
                                wqb[:, c * 3 * D + n * 512:c * 3 * D + (n + 1) * 512],
                                start=(c == 0), stop=(c == NC - 1))
                    ins.then_inc(smm, 1)
            @blk.vector
            def _(e):
                for t in range(NT):
                    e.wait_ge(smm, mm0 + t + 1)
                    if t >= 2:
                        e.wait_ge(ssa[t % 2], 16 * (t // 2))
                    for n in range(6):
                        off = (t % 2) * 3 * D + n * 512
                        e.scalar_tensor_tensor(stage[:, off:off + 512], ps[n][:, :],
                                               1.0, bias[:, n * 512:(n + 1) * 512],
                                               ALU.mult, ALU.add).then_inc(sev, 1)
            @blk.sync
            def _(e):
                for t in range(NT):
                    e.wait_ge(sev, ev0 + 6 * (t + 1))
                    e.dma_start(out=qtile(t),
                                in_=stage[:, (t % 2) * 3 * D:(t % 2 + 1) * 3 * D]
                                ).then_inc(ssa[t % 2], 16)
                e.wait_ge(ssa[0], 16 * (NT - NT // 2))
                e.wait_ge(ssa[1], 16 * (NT // 2))
            if "gath" in io:
                gl, ccs = io["gath"]
                @blk.gpsimd
                def _(e):
                    for k, (gi, go) in enumerate(gl):
                        e.wait_ge(ssa[0], 16 * (2 * k + 2))
                        e.wait_ge(ssa[1], 16 * (2 * k + 2))
                        e.collective_compute(
                            "AllGather", ALU.bypass,
                            replica_groups=[list(range(NCORES))],
                            ins=[gi], outs=[go]).then_inc(ccs, 1)



def build_B(SS=S):
    nc = bass.Bass("TRN2", target_bir_lowering=False)
    io = {"ctxn": nc.dram_tensor("ctxn", [4 * 64, SS], BF16, kind="ExternalOutput").ap()}
    for p in range(2):
        io[f"kt{p}"] = nc.dram_tensor(f"kt{p}", [128, SS], BF16, kind="ExternalInput").ap()
        io[f"qt{p}"] = nc.dram_tensor(f"qt{p}", [128, SS], BF16, kind="ExternalInput").ap()
        io[f"vt{p}"] = nc.dram_tensor(f"vt{p}", [128, SS], BF16, kind="ExternalInput").ap()
    emit_B(nc, io, SS=SS)
    return nc


def emit_B(nc, io, SS=S):
    NKT_ = SS // 128
    nsb = SS // 1024
    nblocks = SS // 512
    fused = "qkv_gs" in io
    if not fused:
        kt = [io[f"kt{p}"] for p in range(2)]
        qt = [io[f"qt{p}"] for p in range(2)]
        vt = [io[f"vt{p}"] for p in range(2)]
    ctx_out = io["ctxn"]

    with ExitStack() as ctx:
        ds = [0]
        sd = ctx.enter_context(nc.semaphore("sd"))
        s_mm = ctx.enter_context(nc.semaphore("s_mm"))
        s_exp = ctx.enter_context(nc.semaphore("s_exp"))
        s_msk = ctx.enter_context(nc.semaphore("s_msk"))
        s_av = ctx.enter_context(nc.semaphore("s_av"))
        s_ev = ctx.enter_context(nc.semaphore("s_ev"))
        s_tr = ctx.enter_context(nc.semaphore("s_tr"))

        ktb = [ctx.enter_context(nc.sbuf_tensor(f"ktb{p}", [128, SS], BF16)) for p in range(2)]
        qtb = [ctx.enter_context(nc.sbuf_tensor(f"qtb{p}", [128, SS], BF16)) for p in range(2)]
        vpt = [ctx.enter_context(nc.sbuf_tensor(f"vpt{u}", [128, 65 * NKT_], BF16))
               for u in range(4)]
        identb = ctx.enter_context(nc.sbuf_tensor("identb", [128, 128], BF16))
        m_lo = ctx.enter_context(nc.sbuf_tensor("m_lo", [128, 128], BF16))
        m_hi = ctx.enter_context(nc.sbuf_tensor("m_hi", [128, 128], BF16))
        rowi = ctx.enter_context(nc.sbuf_tensor("rowi", [128, 128], F32))
        coli = ctx.enter_context(nc.sbuf_tensor("coli", [128, 128], F32))
        pbuf = [ctx.enter_context(nc.sbuf_tensor(f"pbuf{i}", [128, 1024], BF16))
                for i in range(2)]
        den = ctx.enter_context(nc.sbuf_tensor("den", [1, SS], F32))
        rbc = ctx.enter_context(nc.sbuf_tensor("rbc", [64, SS], BF16))
        ctxn = ctx.enter_context(nc.sbuf_tensor("ctxn_sb", [64, SS], BF16))
        ctxraw = ctxn
        ones32 = ctx.enter_context(nc.sbuf_tensor("ones32", [1, 64], F32))
        b0col = ctx.enter_context(nc.sbuf_tensor("b0col", [128, 1], F32))

        with nc.Block() as blk:
            @blk.gpsimd
            def _(e):
                e.iota(rowi[:, :], [[0, 128]], channel_multiplier=1,
                       allow_small_or_imprecise_dtypes=True)
                e.iota(coli[:, :], [[1, 128]], channel_multiplier=0,
                       allow_small_or_imprecise_dtypes=True)
            @blk.sync
            def _(e):
                if fused:
                    offsb = io["offsb"]
                    qgs = io["qkv_gs"]
                    qloc = io["qkv_loc"]
                    ccs, cc0 = io["ccs"], io["cc0"]
                    with e.register("dynoff") as off:
                        e.reg_load(off, offsb[0:1, 22:23])
                        for k in range(4):
                            e.wait_ge(ccs, cc0 + k + 1)
                            e.dma_start(
                                out=bass.AP(qloc.tensor, k * 512 * 768,
                                            [[2048 * 768, 4], [768, 512],
                                             [1, 768]]),
                                in_=bass.AP(qgs[k].tensor, off,
                                            [[512 * 3 * D, 4], [3 * D, 512],
                                             [1, 768]])).then_inc(sd, 16)
                            ds[0] += 16
                            e.wait_ge(sd, ds[0])
                            for j, dst in ((0, ktb[0]), (1, ktb[1]),
                                           (2, qtb[0]), (3, qtb[1])):
                                c0 = 256 + 128 * j if j < 2 else 128 * (j - 2)
                                for qq in range(4):
                                    r0 = qq * 2048 + k * 512
                                    e.dma_start(
                                        out=dst[:, r0:r0 + 512],
                                        in_=qloc[r0:r0 + 512, c0:c0 + 128],
                                        transpose=True).then_inc(sd, 16)
                                    ds[0] += 16
                else:
                    for p in range(2):
                        e.dma_start(out=ktb[p][:, :], in_=kt[p]).then_inc(sd, 16)
                        e.dma_start(out=qtb[p][:, :], in_=qt[p]).then_inc(sd, 16)
                        ds[0] += 32
                e.wait_ge(sd, ds[0])
            @blk.vector
            def _(e):
                e.memset(ones32[:, :], 1.0)
                e.memset(b0col[:, :], -B0)
                for u in range(4):
                    e.memset(vpt[u][:, :], 1.0)
        with nc.Block() as blk:
            @blk.vector
            def _(e):
                e.tensor_tensor(m_lo[:, :], coli[:, :], rowi[:, :], ALU.is_ge)
                e.tensor_tensor(m_hi[:, :], coli[:, :], rowi[:, :], ALU.is_le)
                e.tensor_tensor(identb[:, :], coli[:, :], rowi[:, :], ALU.is_equal)

        with nc.named_scope("vtr"), ExitStack() as trctx:
            vts = trctx.enter_context(nc.sbuf_tensor("vts", [128, SS], BF16))
            pstr = [trctx.enter_context(nc.psum_tensor(f"pstr{i}", [128, 128], BF16))
                    for i in range(2)]
            ntr = 0
            for p in range(2):
                with nc.Block() as blk:
                    @blk.sync
                    def _(e):
                        if fused:
                            e.dma_start(out=vts[:, :],
                                        in_=io["qkv_loc"][:, 512 + 128 * p:
                                                          640 + 128 * p],
                                        transpose=True).then_inc(sd, 16)
                        else:
                            e.dma_start(out=vts[:, :], in_=vt[p]).then_inc(sd, 16)
                        ds[0] += 16
                        e.wait_ge(sd, ds[0])
                base_tr = ntr
                with nc.Block() as blk:
                    @blk.tensor
                    def _(e):
                        for i in range(NKT_):
                            if base_tr + i >= 2:
                                e.wait_ge(s_tr, base_tr + i - 1)
                            e.transpose(pstr[(base_tr + i) % 2][:, :],
                                        vts[:, 128 * i:128 * (i + 1)],
                                        identb[:, :]).then_inc(s_mm, 1)
                    @blk.vector
                    def _(e):
                        for i in range(NKT_):
                            e.wait_ge(s_mm, base_tr + i + 1)
                            e.tensor_copy(vpt[2 * p][:, 65 * i:65 * i + 64],
                                          pstr[(base_tr + i) % 2][:, 0:64])
                            e.tensor_copy(vpt[2 * p + 1][:, 65 * i:65 * i + 64],
                                          pstr[(base_tr + i) % 2][:, 64:128]).then_inc(s_tr, 1)
                ntr += NKT_

        ps_s = [ctx.enter_context(nc.psum_tensor(f"ps_s{i}", [128, 1024], F32))
                for i in range(2)]
        ps_c = [ctx.enter_context(nc.psum_tensor(f"ps_c{i}", [128, 512], F32))
                for i in range(4)]
        cnt = {"mm": ntr, "exp": 0, "msk": 0, "av": 0, "ev": 0}

        def unit_global(base, K2, Q2, VT):
            plan = []
            for m in range(nsb):
                for j in range(8 * (m + 1)):
                    plan.append((len(plan), 128 * j, 1024 * m))
            msk_tgt, c = {}, cnt["msk"]
            for (i, tk0, sb0) in plan:
                if tk0 >= sb0:
                    c += 1
                msk_tgt[i] = c
            sb_last = {}
            idx = 0
            for m in range(nsb):
                idx += 8 * (m + 1)
                sb_last[m] = idx
            b_mm, b_exp, b_av, b_ev = cnt["mm"], cnt["exp"], cnt["av"], cnt["ev"]
            with nc.Block() as blk:
                @blk.tensor
                def _(e):
                    def do_av(i, tk0, sb0):
                        buf, m = i % 2, sb0 // 1024
                        first = (plan[i - 1][2] != sb0) if i > 0 else True
                        last = (plan[i + 1][2] != sb0) if i + 1 < len(plan) else True
                        if first and m >= 2:
                            e.wait_ge(s_ev, b_ev + 2 * (m - 1))
                        if tk0 >= sb0:
                            e.wait_ge(s_msk, msk_tgt[i])
                        else:
                            e.wait_ge(s_exp, b_exp + i + 1)
                        j = tk0 // 128
                        for hh in range(2):
                            ins = e.matmul(ps_c[(2 * m + hh) % 4][0:65, :],
                                           VT[:, 65 * j:65 * j + 65],
                                           pbuf[buf][:, 512 * hh:512 * (hh + 1)],
                                           start=first, stop=last)
                        ins.then_inc(s_av, 1)
                    for (i, tk0, sb0) in plan:
                        buf = i % 2
                        if i >= 2:
                            e.wait_ge(s_exp, b_exp + i - 1)
                        lhsT = K2[base:base + 64, tk0:tk0 + 128]
                        for hh in range(2):
                            lo = sb0 + 512 * hh
                            ins = e.matmul(ps_s[buf][:, 512 * hh:512 * (hh + 1)],
                                           lhsT, Q2[base:base + 64, lo:lo + 512],
                                           start=True, stop=True)
                        ins.then_inc(s_mm, 1)
                        if i > 0:
                            do_av(*plan[i - 1])
                    do_av(*plan[-1])
                @blk.scalar
                def _(e):
                    for (i, tk0, sb0) in plan:
                        buf = i % 2
                        v0 = max(0, tk0 - sb0)
                        e.wait_ge(s_mm, b_mm + i + 1)
                        if i >= 2:
                            e.wait_ge(s_av, b_av + i - 1)
                        e.activation(pbuf[buf][:, v0:1024], ps_s[buf][:, v0:1024],
                                     AF.Exp, bias=b0col[:, 0:1], scale=0.125).then_inc(s_exp, 1)
                @blk.gpsimd
                def _(e):
                    for (i, tk0, sb0) in plan:
                        if tk0 < sb0:
                            continue
                        buf = i % 2
                        e.wait_ge(s_exp, b_exp + i + 1)
                        jj = (tk0 - sb0) // 128
                        if jj > 0:
                            e.memset(pbuf[buf][:, 0:128 * jj], 0.0)
                            e.drain()
                        e.tensor_tensor(pbuf[buf][:, 128 * jj:128 * (jj + 1)],
                                        pbuf[buf][:, 128 * jj:128 * (jj + 1)],
                                        m_lo[:, :], ALU.mult).then_inc(s_msk, 1)
                @blk.vector
                def _(e):
                    for m in range(nsb):
                        e.wait_ge(s_av, b_av + sb_last[m])
                        for hh in range(2):
                            bank = (2 * m + hh) % 4
                            lo = 1024 * m + 512 * hh
                            e.tensor_copy(ctxraw[:, lo:lo + 512], ps_c[bank][0:64, :])
                            e.tensor_copy(den[:, lo:lo + 512],
                                          ps_c[bank][64:65, :]).then_inc(s_ev, 1)
            cnt["mm"] = b_mm + len(plan)
            cnt["exp"] = b_exp + len(plan)
            cnt["msk"] = msk_tgt[len(plan) - 1]
            cnt["av"] = b_av + len(plan)
            cnt["ev"] = b_ev + 2 * nsb

        def unit_local(base, K2, Q2, VT):
            plan = [(j, 128 * j, min(512, SS - 128 * j)) for j in range(NKT_)]
            blk_last = {m: min(NKT_ - 1, 4 * m + 3) for m in range(nblocks)}
            navs = {}
            nav = 0
            for (j, tk0, width) in plan:
                off = tk0 % 512
                w1 = min(width, 512 - off)
                nav += 1 + (1 if width > w1 else 0)
                navs[j] = nav
            b_mm, b_exp, b_av, b_ev = cnt["mm"], cnt["exp"], cnt["av"], cnt["ev"]
            b_msk = cnt["msk"]
            with nc.Block() as blk:
                @blk.tensor
                def _(e):
                    def do_av(i, tk0, width):
                        buf = i % 2
                        e.wait_ge(s_msk, b_msk + i + 1)
                        A = tk0 // 512
                        off = tk0 % 512
                        w1 = min(width, 512 - off)
                        firstA = (i == max(0, 4 * A - 3))
                        lastA = (i == blk_last[A])
                        if firstA and A >= 4:
                            e.wait_ge(s_ev, b_ev + (A - 3))
                        e.matmul(ps_c[A % 4][0:65, off:off + w1],
                                 VT[:, 65 * i:65 * i + 65], pbuf[buf][:, 0:w1],
                                 start=firstA, stop=lastA).then_inc(s_av, 1)
                        if width > w1:
                            A2 = A + 1
                            firstA2 = (i == max(0, 4 * A2 - 3))
                            lastA2 = (i == blk_last[A2])
                            if firstA2 and A2 >= 4:
                                e.wait_ge(s_ev, b_ev + (A2 - 3))
                            e.matmul(ps_c[A2 % 4][0:65, 0:width - w1],
                                     VT[:, 65 * i:65 * i + 65], pbuf[buf][:, w1:width],
                                     start=firstA2, stop=lastA2).then_inc(s_av, 1)
                    for (i, tk0, width) in plan:
                        buf = i % 2
                        if i >= 2:
                            e.wait_ge(s_exp, b_exp + i - 1)
                        lhsT = K2[base:base + 64, tk0:tk0 + 128]
                        e.matmul(ps_s[buf][:, 0:width], lhsT,
                                 Q2[base:base + 64, tk0:tk0 + width],
                                 start=True, stop=True).then_inc(s_mm, 1)
                        if i > 0:
                            do_av(*plan[i - 1])
                    do_av(*plan[-1])
                @blk.scalar
                def _(e):
                    for (i, tk0, width) in plan:
                        buf = i % 2
                        e.wait_ge(s_mm, b_mm + i + 1)
                        if i >= 2:
                            e.wait_ge(s_av, b_av + navs[i - 2])
                        e.activation(pbuf[buf][:, 0:width], ps_s[buf][:, 0:width],
                                     AF.Exp, bias=b0col[:, 0:1], scale=0.125).then_inc(s_exp, 1)
                @blk.gpsimd
                def _(e):
                    for (i, tk0, width) in plan:
                        buf = i % 2
                        e.wait_ge(s_exp, b_exp + i + 1)
                        ins = e.tensor_tensor(pbuf[buf][:, 0:128], pbuf[buf][:, 0:128],
                                              m_lo[:, :], ALU.mult)
                        if width > 256:
                            ins = e.tensor_tensor(pbuf[buf][:, 256:384],
                                                  pbuf[buf][:, 256:384],
                                                  m_hi[:, :], ALU.mult)
                        if width > 384:
                            ins = e.memset(pbuf[buf][:, 384:width], 0.0)
                        ins.then_inc(s_msk, 1)
                @blk.vector
                def _(e):
                    for (i, tk0, width) in plan:
                        for m in range(nblocks):
                            if blk_last[m] == i:
                                e.wait_ge(s_av, b_av + navs[i])
                                e.tensor_copy(ctxraw[:, 512 * m:512 * (m + 1)],
                                              ps_c[m % 4][0:64, :])
                                e.tensor_copy(den[:, 512 * m:512 * (m + 1)],
                                              ps_c[m % 4][64:65, :]).then_inc(s_ev, 1)
            cnt["mm"] = b_mm + len(plan)
            cnt["exp"] = b_exp + len(plan)
            cnt["msk"] = b_msk + len(plan)
            cnt["av"] = b_av + nav
            cnt["ev"] = b_ev + nblocks

        for u in range(4):
            p, base = u // 2, 64 * (u % 2)
            with nc.named_scope(f"u{u}"):
                if u == 0:
                    unit_global(base, ktb[p], qtb[p], vpt[u])
                else:
                    unit_local(base, ktb[p], qtb[p], vpt[u])
            with nc.Block() as blk:
                @blk.vector
                def _(e):
                    e.reciprocal(den[:, :], den[:, :])
                    e.drain()
            b_mm, b_ev = cnt["mm"], cnt["ev"]
            with nc.Block() as blk:
                @blk.tensor
                def _(e):
                    for i in range(SS // 512):
                        if i >= 2:
                            e.wait_ge(s_ev, b_ev + i - 1)
                        e.matmul(ps_c[i % 2][0:64, :], ones32[:1, :],
                                 den[:1, 512 * i:512 * (i + 1)],
                                 start=True, stop=True).then_inc(s_mm, 1)
                @blk.scalar
                def _(e):
                    for i in range(SS // 512):
                        e.wait_ge(s_mm, b_mm + i + 1)
                        e.copy(rbc[:, 512 * i:512 * (i + 1)],
                               ps_c[i % 2][0:64, :]).then_inc(s_ev, 1)
            cnt["mm"] = b_mm + SS // 512
            cnt["ev"] = b_ev + SS // 512
            with nc.Block() as blk:
                @blk.vector
                def _(e):
                    e.tensor_tensor(ctxn[:, :], ctxn[:, :], rbc[:, :], ALU.mult)
            with nc.Block() as blk:
                @blk.sync
                def _(e):
                    e.dma_start(out=ctx_out[64 * u:64 * (u + 1), :],
                                in_=ctxn[:, :]).then_inc(sd, 16)
                    ds[0] += 16
                    e.wait_ge(sd, ds[0])



def build_C():
    nc = bass.Bass("TRN2", target_bir_lowering=False)
    io = {"ctxf": nc.dram_tensor("ctxf", [D, TPC], BF16, kind="ExternalInput").ap(),
          "wo": nc.dram_tensor("wo", [D, D], F32, kind="ExternalInput").ap(),
          "bo": nc.dram_tensor("bo", [1, D], F32, kind="ExternalInput").ap(),
          "x": nc.dram_tensor("x", [TPC, D], F32, kind="ExternalInput").ap(),
          "x1": nc.dram_tensor("x1", [TPC, D], F32, kind="ExternalOutput").ap()}
    emit_C(nc, io)
    return nc


def emit_C(nc, io):
    wo_in, bo_in, x_in, x1_out = io["wo"], io["bo"], io["x"], io["x1"]
    fused = "ctxn_g" in io
    if not fused:
        ctxf_in = io["ctxf"]

    with ExitStack() as ctx:
        ds, gs = [0], [0]
        sd = ctx.enter_context(nc.semaphore("sd"))
        sg = ctx.enter_context(nc.semaphore("sg"))
        smm = ctx.enter_context(nc.semaphore("smm"))
        sev = ctx.enter_context(nc.semaphore("sev"))
        ctxf = ctx.enter_context(nc.sbuf_tensor("ctxf_sb", [128, NC * TPC], BF16))
        wob = ctx.enter_context(nc.sbuf_tensor("wob", [128, NC * D], BF16))
        bias = ctx.enter_context(nc.sbuf_tensor("bias", [128, D], F32))
        brow = ctx.enter_context(nc.sbuf_tensor("brow", [1, D], F32))
        ones32 = ctx.enter_context(nc.sbuf_tensor("ones32", [1, 128], F32))
        y_sb = ctx.enter_context(nc.sbuf_tensor("y_sb", [128, NT * D], F32))
        ps = [ctx.enter_context(nc.psum_tensor(f"ps{n}", [128, 512], F32)) for n in range(4)]
        with nc.Block() as blk:
            @blk.sync
            def _(e):
                if fused:
                    with e.register("dynoffc") as off:
                        for c in range(NC):
                            e.reg_load(off, io["offsb"][0:1, 6 + c:7 + c])
                            e.dma_start(out=ctxf[:, c * TPC:(c + 1) * TPC],
                                        in_=bass.AP(io["ctxn_g"].tensor, off,
                                                    [[S, 128], [1, TPC]])).then_inc(sd, 16)
                            ds[0] += 16
                else:
                    for c in range(NC):
                        e.dma_start(out=ctxf[:, c * TPC:(c + 1) * TPC],
                                    in_=ctxf_in[c * 128:(c + 1) * 128, :]).then_inc(sd, 16)
                        ds[0] += 16
                e.dma_start(out=brow[:, :], in_=bo_in).then_inc(sd, 16)
                ds[0] += 16
                e.wait_ge(sd, ds[0])
            @blk.gpsimd
            def _(e):
                for c in range(NC):
                    e.dma_start(out=wob[:, c * D:(c + 1) * D],
                                in_=wo_in[c * 128:(c + 1) * 128, :]).then_inc(sg, 16)
                    gs[0] += 16
                e.wait_ge(sg, gs[0])
            @blk.vector
            def _(e):
                e.memset(ones32[:, :], 1.0)
        with nc.Block() as blk:
            @blk.tensor
            def _(e):
                for i in range(2):
                    e.matmul(ps[2 + i][:, :], ones32[:1, :],
                             brow[:1, i * 512:(i + 1) * 512],
                             start=True, stop=True).then_inc(smm, 1)
            @blk.vector
            def _(e):
                for i in range(2):
                    e.wait_ge(smm, i + 1)
                    e.tensor_copy(bias[:, i * 512:(i + 1) * 512], ps[2 + i][:, :])
        with nc.Block() as blk:
            @blk.tensor
            def _(e):
                for t in range(NT):
                    if t >= 2:
                        e.wait_ge(sev, 2 * (t - 1))
                    for c in range(NC):
                        lhsT = ctxf[:, c * TPC + t * 128:c * TPC + (t + 1) * 128]
                        for n in range(2):
                            ins = e.matmul(ps[2 * (t % 2) + n][:, :], lhsT,
                                           wob[:, c * D + n * 512:c * D + (n + 1) * 512],
                                           start=(c == 0), stop=(c == NC - 1))
                    ins.then_inc(smm, 1)
            @blk.vector
            def _(e):
                for t in range(NT):
                    e.wait_ge(smm, 2 + t + 1)
                    for n in range(2):
                        e.scalar_tensor_tensor(
                            y_sb[:, t * D + n * 512:t * D + (n + 1) * 512],
                            ps[2 * (t % 2) + n][:, :], 1.0,
                            bias[:, n * 512:(n + 1) * 512], ALU.mult,
                            ALU.add).then_inc(sev, 1)
        emit_ln_resid_streamx(nc, ctx, y_sb, x_in, x1_out, NT, "C")



def build_D():
    nc = bass.Bass("TRN2", target_bir_lowering=False)
    io = {"x": nc.dram_tensor("x", [TPC, D], F32, kind="ExternalInput").ap(),
          "halo": nc.dram_tensor("halo", [1, D], F32, kind="ExternalInput").ap(),
          "wrT": nc.dram_tensor("wrT", [D, D], F32, kind="ExternalInput").ap(),
          "wkT": nc.dram_tensor("wkT", [D, D], F32, kind="ExternalInput").ap(),
          "wvT": nc.dram_tensor("wvT", [D, D], F32, kind="ExternalInput").ap(),
          "mur": nc.dram_tensor("mur", [1, D], F32, kind="ExternalInput").ap(),
          "muk": nc.dram_tensor("muk", [1, D], F32, kind="ExternalInput").ap(),
          "x2": nc.dram_tensor("x2", [TPC, D], F32, kind="ExternalOutput").ap()}
    emit_D(nc, io)
    return nc


def emit_D(nc, io):
    x_in, x2_out = io["x"], io["x2"]
    wrT_in, wkT_in, wvT_in = io["wrT"], io["wkT"], io["wvT"]
    mur_in, muk_in = io["mur"], io["muk"]
    fused = "lr_g" in io
    if not fused:
        halo_in = io["halo"]
    xn_d = nc.dram_tensor("xn_d", [TPC, D], BF16)
    xp_d = nc.dram_tensor("xp_d", [TPC, D], BF16)

    with ExitStack() as ctx:
        ds, gs = [0], [0]
        sd = ctx.enter_context(nc.semaphore("sd"))
        sg = ctx.enter_context(nc.semaphore("sg"))
        smm = ctx.enter_context(nc.semaphore("smm"))
        sev = ctx.enter_context(nc.semaphore("sev"))
        cnt = {"mm": 0, "ev": 0}
        r_sb = ctx.enter_context(nc.sbuf_tensor("r_sb", [128, NT * D], BF16))
        k2T = ctx.enter_context(nc.sbuf_tensor("k2T", [128, NC * TPC], BF16))
        wbuf = ctx.enter_context(nc.sbuf_tensor("wbuf", [128, NC * D], BF16))
        ps = [ctx.enter_context(nc.psum_tensor(f"ps{n}", [128, 512], F32)) for n in range(4)]

        with ExitStack() as ph1:
            x_sb = ph1.enter_context(nc.sbuf_tensor("x_sb", [128, NT * D], F32))
            xbf = ph1.enter_context(nc.sbuf_tensor("xbf", [128, NT * D], BF16))
            h_sb = ph1.enter_context(nc.sbuf_tensor("h_sb", [1, D], F32))
            hst = ph1.enter_context(nc.sbuf_tensor("hst", [1, 4], F32))
            hbf = ph1.enter_context(nc.sbuf_tensor("hbf", [1, D], BF16))
            hscr = ph1.enter_context(nc.sbuf_tensor("hscr", [1, D], BF16))
            with nc.Block() as blk:
                @blk.sync
                def _(e):
                    emit_load_tiles(e, sd, x_sb, x_in, NT, ds)
                    if fused:
                        with e.register("dynoffh") as off:
                            e.reg_load(off, io["offsb"][0:1, 14:15])
                            e.dma_start(out=h_sb[:, :],
                                        in_=bass.AP(io["lr_g"].tensor, off,
                                                    [[D, 1], [1, D]])).then_inc(sd, 16)
                    else:
                        e.dma_start(out=h_sb[:, :], in_=halo_in).then_inc(sd, 16)
                    ds[0] += 16
                    e.wait_ge(sd, ds[0])
            emit_ln(nc, ph1, x_sb, NT, xbf, "D")
            with nc.Block() as blk:
                @blk.vector
                def _(e):
                    e.tensor_scalar(hbf[:, :], h_sb[:, :], 1.0, 0.0, ALU.mult, ALU.add,
                                    accum_out=hst[:, 0:1])
                @blk.scalar
                def _(e):
                    e.activation(hscr[:, :], h_sb[:, :], AF.Square, accum_out=hst[:, 1:2])
            with nc.Block() as blk:
                @blk.vector
                def _(e):
                    e.tensor_scalar(hst[:, 2:3], hst[:, 0:1], 1.0 / D, 0.0, ALU.mult, ALU.add)
                    e.drain()
                    e.tensor_tensor(hst[:, 0:1], hst[:, 2:3], hst[:, 2:3], ALU.mult)
                    e.tensor_scalar(hst[:, 1:2], hst[:, 1:2], 1.0 / D, EPS, ALU.mult, ALU.add)
                    e.drain()
                    e.scalar_tensor_tensor(hst[:, 1:2], hst[:, 1:2], 1.0, hst[:, 0:1],
                                           ALU.mult, ALU.subtract)
            with nc.Block() as blk:
                @blk.scalar
                def _(e):
                    e.activation(hst[:, 1:2], hst[:, 1:2], AF.Sqrt)
            with nc.Block() as blk:
                @blk.vector
                def _(e):
                    e.reciprocal(hst[:, 3:4], hst[:, 1:2])
                    e.drain()
                    e.tensor_scalar(hbf[:, :], h_sb[:, :], hst[:, 2:3], hst[:, 3:4],
                                    ALU.subtract, ALU.mult)
            with nc.Block() as blk:
                @blk.sync
                def _(e):
                    emit_store_tiles(e, sd, xn_d, xbf, NT, ds)
                    e.dma_start(out=xp_d[0:1, :], in_=hbf[:, :]).then_inc(sd, 16)
                    ds[0] += 16
                    for t in range(NT):
                        nrows = 128 if t < NT - 1 else 127
                        e.dma_start(out=xp_d[t * 128 + 1:t * 128 + 1 + nrows, :],
                                    in_=xbf[0:nrows, t * D:(t + 1) * D]).then_inc(sd, 16)
                        ds[0] += 16
                    e.wait_ge(sd, ds[0])

        with ExitStack() as ph2:
            xT = ph2.enter_context(nc.sbuf_tensor("xT", [128, NC * TPC], BF16))
            xP = ph2.enter_context(nc.sbuf_tensor("xP", [128, NC * TPC], BF16))
            xm = ph2.enter_context(nc.sbuf_tensor("xm", [128, NC * TPC], BF16))
            dmix = ph2.enter_context(nc.sbuf_tensor("dmix", [128, TPC], BF16))
            mucol = ph2.enter_context(nc.sbuf_tensor("mucol", [128, 2 * NC], F32))
            krelu = ph2.enter_context(nc.sbuf_tensor("krelu", [128, 4 * 512], F32))
            with nc.Block() as blk:
                @blk.sync
                def _(e):
                    for c in range(NC):
                        e.dma_start(out=xT[:, c * TPC:(c + 1) * TPC],
                                    in_=xn_d[:, c * 128:(c + 1) * 128],
                                    transpose=True).then_inc(sd, 16)
                        ds[0] += 16
                    for c in range(NC):
                        e.dma_start(out=xP[:, c * TPC:(c + 1) * TPC],
                                    in_=xp_d[:, c * 128:(c + 1) * 128],
                                    transpose=True).then_inc(sd, 16)
                        ds[0] += 16
                    e.wait_ge(sd, ds[0])
                @blk.gpsimd
                def _(e):
                    for c in range(NC):
                        col_load(e, sg, mucol[:, c:c + 1],
                                 mur_in[0:1, c * 128:(c + 1) * 128], gs)
                        col_load(e, sg, mucol[:, NC + c:NC + c + 1],
                                 muk_in[0:1, c * 128:(c + 1) * 128], gs)
                    for c in range(NC):
                        e.dma_start(out=wbuf[:, c * D:(c + 1) * D],
                                    in_=wrT_in[c * 128:(c + 1) * 128, :]).then_inc(sg, 16)
                        gs[0] += 16
                    e.wait_ge(sg, gs[0])

            def emit_mix(mucol_off):
                with nc.Block() as blk:
                    @blk.vector
                    def _(e):
                        for c in range(NC):
                            cur = xT[:, c * TPC:(c + 1) * TPC]
                            prev = xP[:, c * TPC:(c + 1) * TPC]
                            e.tensor_tensor(dmix[:, :], cur, prev, ALU.subtract)
                            e.drain()
                            e.scalar_tensor_tensor(xm[:, c * TPC:(c + 1) * TPC],
                                                   dmix[:, :],
                                                   mucol[:, mucol_off + c:mucol_off + c + 1],
                                                   prev, ALU.mult, ALU.add)
                            e.drain()

            emit_mix(0)
            b_mm, b_ev = cnt["mm"], cnt["ev"]
            with nc.Block() as blk:
                @blk.tensor
                def _(e):
                    for t in range(NT):
                        if t >= 2:
                            e.wait_ge(sev, b_ev + 2 * (t - 1))
                        for c in range(NC):
                            lhsT = xm[:, c * TPC + t * 128:c * TPC + (t + 1) * 128]
                            for n in range(2):
                                ins = e.matmul(ps[2 * (t % 2) + n][:, :], lhsT,
                                               wbuf[:, c * D + n * 512:c * D + (n + 1) * 512],
                                               start=(c == 0), stop=(c == NC - 1))
                        ins.then_inc(smm, 1)
                @blk.scalar
                def _(e):
                    for t in range(NT):
                        e.wait_ge(smm, b_mm + t + 1)
                        for n in range(2):
                            e.activation(r_sb[:, t * D + n * 512:t * D + (n + 1) * 512],
                                         ps[2 * (t % 2) + n][:, :],
                                         AF.Sigmoid).then_inc(sev, 1)
            cnt["mm"] = b_mm + NT
            cnt["ev"] = b_ev + 2 * NT
            with nc.Block() as blk:
                @blk.gpsimd
                def _(e):
                    for c in range(NC):
                        e.dma_start(out=wbuf[:, c * D:(c + 1) * D],
                                    in_=wkT_in[c * 128:(c + 1) * 128, :]).then_inc(sg, 16)
                        gs[0] += 16
                    e.wait_ge(sg, gs[0])
            emit_mix(NC)
            b_mm, b_ev = cnt["mm"], cnt["ev"]
            with nc.Block() as blk:
                @blk.tensor
                def _(e):
                    for dt in range(NC):
                        if dt >= 1:
                            e.wait_ge(sev, b_ev + dt)
                        for c in range(NC):
                            lhsT = wbuf[:, c * D + dt * 128:c * D + (dt + 1) * 128]
                            for q in range(4):
                                ins = e.matmul(ps[q][:, :], lhsT,
                                               xm[:, c * TPC + q * 512:c * TPC + (q + 1) * 512],
                                               start=(c == 0), stop=(c == NC - 1))
                        ins.then_inc(smm, 1)
                @blk.scalar
                def _(e):
                    for dt in range(NC):
                        e.wait_ge(smm, b_mm + dt + 1)
                        for q in range(4):
                            e.activation(krelu[:, q * 512:(q + 1) * 512], ps[q][:, :],
                                         AF.Relu)
                        e.drain()
                        e.activation(k2T[:, dt * TPC:(dt + 1) * TPC], krelu[:, :],
                                     AF.Square).then_inc(sev, 1)
            cnt["mm"] = b_mm + NC
            cnt["ev"] = b_ev + NC
        y_sb = ctx.enter_context(nc.sbuf_tensor("y_sb", [128, NT * D], F32))
        with nc.Block() as blk:
            @blk.gpsimd
            def _(e):
                for c in range(NC):
                    e.dma_start(out=wbuf[:, c * D:(c + 1) * D],
                                in_=wvT_in[c * 128:(c + 1) * 128, :]).then_inc(sg, 16)
                    gs[0] += 16
                e.wait_ge(sg, gs[0])
        b_mm, b_ev = cnt["mm"], cnt["ev"]
        with nc.Block() as blk:
            @blk.tensor
            def _(e):
                for t in range(NT):
                    if t >= 2:
                        e.wait_ge(sev, b_ev + 2 * (t - 1))
                    for c in range(NC):
                        lhsT = k2T[:, c * TPC + t * 128:c * TPC + (t + 1) * 128]
                        for n in range(2):
                            ins = e.matmul(ps[2 * (t % 2) + n][:, :], lhsT,
                                           wbuf[:, c * D + n * 512:c * D + (n + 1) * 512],
                                           start=(c == 0), stop=(c == NC - 1))
                    ins.then_inc(smm, 1)
            @blk.vector
            def _(e):
                for t in range(NT):
                    e.wait_ge(smm, b_mm + t + 1)
                    for n in range(2):
                        sl = slice(t * D + n * 512, t * D + (n + 1) * 512)
                        e.tensor_tensor(y_sb[:, sl], ps[2 * (t % 2) + n][:, :],
                                        r_sb[:, sl], ALU.mult).then_inc(sev, 1)
        cnt["mm"] = b_mm + NT
        cnt["ev"] = b_ev + 2 * NT
        emit_ln_resid_streamx(nc, ctx, y_sb, x_in, x2_out, NT, "D")



def build_E0():
    nc = bass.Bass("TRN2", target_bir_lowering=False)
    io = {"x": nc.dram_tensor("x", [TPC, D], F32, kind="ExternalInput").ap(),
          "wvT": nc.dram_tensor("wvT", [D, D], F32, kind="ExternalInput").ap(),
          "wrT": nc.dram_tensor("wrT", [D, D], F32, kind="ExternalInput").ap(),
          "td": nc.dram_tensor("td", [1, D], F32, kind="ExternalInput").ap(),
          "vT": nc.dram_tensor("vT", [D, TPC], BF16, kind="ExternalOutput").ap(),
          "rT": nc.dram_tensor("rT", [D, TPC], BF16, kind="ExternalOutput").ap(),
          "P": nc.dram_tensor("P", [128, NC], F32, kind="ExternalOutput").ap(),
          "Bb": nc.dram_tensor("Bb", [128, NC], F32, kind="ExternalOutput").ap()}
    emit_E0(nc, io)
    return nc


def emit_E0(nc, io):
    x_in, wvT_in, wrT_in, td_in = io["x"], io["wvT"], io["wrT"], io["td"]
    vT_out, rT_out = io["vT"], io["rT"]
    fused = "PBfull" in io
    if not fused:
        P_out, Bb_out = io["P"], io["Bb"]
    xn_d = nc.dram_tensor("xn_d", [TPC, D], BF16)

    with ExitStack() as ctx:
        ds, gs = [0], [0]
        sd = ctx.enter_context(nc.semaphore("sd"))
        sg = ctx.enter_context(nc.semaphore("sg"))
        smm = ctx.enter_context(nc.semaphore("smm"))
        sev = ctx.enter_context(nc.semaphore("sev"))
        vT_sb = ctx.enter_context(nc.sbuf_tensor("vT_sb", [128, NC * TPC], BF16))
        rT_sb = ctx.enter_context(nc.sbuf_tensor("rT_sb", [128, NC * TPC], BF16))
        wbuf = ctx.enter_context(nc.sbuf_tensor("wbuf", [128, NC * D], BF16))
        ps = [ctx.enter_context(nc.psum_tensor(f"ps{n}", [128, 512], F32)) for n in range(4)]
        with ExitStack() as ph1:
            x_sb = ph1.enter_context(nc.sbuf_tensor("x_sb", [128, NT * D], F32))
            xbf = ph1.enter_context(nc.sbuf_tensor("xbf", [128, NT * D], BF16))
            with nc.Block() as blk:
                @blk.sync
                def _(e):
                    emit_load_tiles(e, sd, x_sb, x_in, NT, ds)
                    e.wait_ge(sd, ds[0])
            emit_ln(nc, ph1, x_sb, NT, xbf, "E0")
            with nc.Block() as blk:
                @blk.sync
                def _(e):
                    emit_store_tiles(e, sd, xn_d, xbf, NT, ds)
                    e.wait_ge(sd, ds[0])
        xT = ctx.enter_context(nc.sbuf_tensor("xT", [128, NC * TPC], BF16))
        with nc.Block() as blk:
            @blk.sync
            def _(e):
                for c in range(NC):
                    e.dma_start(out=xT[:, c * TPC:(c + 1) * TPC],
                                in_=xn_d[:, c * 128:(c + 1) * 128],
                                transpose=True).then_inc(sd, 16)
                    ds[0] += 16
                e.wait_ge(sd, ds[0])
            @blk.gpsimd
            def _(e):
                for c in range(NC):
                    e.dma_start(out=wbuf[:, c * D:(c + 1) * D],
                                in_=wvT_in[c * 128:(c + 1) * 128, :]).then_inc(sg, 16)
                    gs[0] += 16
                e.wait_ge(sg, gs[0])
        cnt = {"mm": 0, "ev": 0}

        def fm_matmul(dst_sb, act):
            b_mm, b_ev = cnt["mm"], cnt["ev"]
            with nc.Block() as blk:
                @blk.tensor
                def _(e):
                    for dt in range(NC):
                        if dt >= 1:
                            e.wait_ge(sev, b_ev + dt)
                        for c in range(NC):
                            lhsT = wbuf[:, c * D + dt * 128:c * D + (dt + 1) * 128]
                            for q in range(4):
                                ins = e.matmul(ps[q][:, :], lhsT,
                                               xT[:, c * TPC + q * 512:c * TPC + (q + 1) * 512],
                                               start=(c == 0), stop=(c == NC - 1))
                        ins.then_inc(smm, 1)
                @blk.scalar
                def _(e):
                    for dt in range(NC):
                        e.wait_ge(smm, b_mm + dt + 1)
                        for q in range(4):
                            ins = e.activation(
                                dst_sb[:, dt * TPC + q * 512:dt * TPC + (q + 1) * 512],
                                ps[q][:, :], act)
                        ins.then_inc(sev, 1)
            cnt["mm"] = b_mm + NC
            cnt["ev"] = b_ev + NC

        fm_matmul(vT_sb, AF.Copy)
        with nc.Block() as blk:
            @blk.gpsimd
            def _(e):
                for c in range(NC):
                    e.dma_start(out=wbuf[:, c * D:(c + 1) * D],
                                in_=wrT_in[c * 128:(c + 1) * 128, :]).then_inc(sg, 16)
                    gs[0] += 16
                e.wait_ge(sg, gs[0])
        fm_matmul(rT_sb, AF.Sigmoid)
        tdc = ctx.enter_context(nc.sbuf_tensor("tdc", [128, NC], F32))
        cc = ctx.enter_context(nc.sbuf_tensor("cc", [128, 2 * NC], F32))
        onesT = ctx.enter_context(nc.sbuf_tensor("onesT", [128, TPC], F32))
        c1bc = ctx.enter_context(nc.sbuf_tensor("c1bc", [128, TPC], F32))
        cv = ctx.enter_context(nc.sbuf_tensor("cv", [128, TPC], F32))
        Sa = ctx.enter_context(nc.sbuf_tensor("Sa", [128, TPC], F32))
        PB = ctx.enter_context(nc.sbuf_tensor("PB", [128, 2 * NC], F32))
        zz = ctx.enter_context(nc.sbuf_tensor("zzpb", [128, 2 * NC], F32))
        with nc.Block() as blk:
            @blk.gpsimd
            def _(e):
                for c in range(NC):
                    col_load(e, sg, tdc[:, c:c + 1], td_in[0:1, c * 128:(c + 1) * 128], gs)
                e.wait_ge(sg, gs[0])
            @blk.vector
            def _(e):
                e.memset(onesT[:, :], 1.0)
                e.memset(zz[:, :], 0.0)
        with nc.Block() as blk:
            @blk.vector
            def _(e):
                e.tensor_scalar_max(cc[:, 0:NC], tdc[:, :], 0.0)
                e.tensor_scalar_min(cc[:, NC:2 * NC], tdc[:, :], 0.0)
        with nc.Block() as blk:
            @blk.scalar
            def _(e):
                e.activation(cc[:, 0:NC], cc[:, 0:NC], AF.Exp, scale=-1.0)
                e.drain()
                e.activation(cc[:, NC:2 * NC], cc[:, NC:2 * NC], AF.Exp)
        for c in range(NC):
            with nc.Block() as blk:
                @blk.vector
                def _(e):
                    e.tensor_scalar(c1bc[:, :], onesT[:, :], cc[:, c:c + 1], 0.0,
                                    ALU.mult, ALU.add)
                    e.tensor_scalar(cv[:, :], vT_sb[:, c * TPC:(c + 1) * TPC],
                                    cc[:, NC + c:NC + c + 1], 0.0, ALU.mult, ALU.add)
                    e.drain()
                    e.tensor_tensor_scan(Sa[:, :], c1bc[:, :], cv[:, :], 0.0,
                                         ALU.mult, ALU.add)
                    e.drain()
                    e.tensor_copy(PB[:, c:c + 1], Sa[:, TPC - 1:TPC])
                    e.tensor_scalar(cv[:, :], onesT[:, :], cc[:, NC + c:NC + c + 1], 0.0,
                                    ALU.mult, ALU.add)
                    e.drain()
                    e.tensor_tensor_scan(Sa[:, :], c1bc[:, :], cv[:, :], 0.0,
                                         ALU.mult, ALU.add)
                    e.drain()
                    e.tensor_copy(PB[:, NC + c:NC + c + 1], Sa[:, TPC - 1:TPC])
        with nc.Block() as blk:
            @blk.sync
            def _(e):
                for c in range(NC):
                    e.dma_start(out=vT_out[c * 128:(c + 1) * 128, :],
                                in_=vT_sb[:, c * TPC:(c + 1) * TPC]).then_inc(sd, 16)
                    e.dma_start(out=rT_out[c * 128:(c + 1) * 128, :],
                                in_=rT_sb[:, c * TPC:(c + 1) * TPC]).then_inc(sd, 16)
                    ds[0] += 32
                if fused:
                    e.dma_start(out=io["PBfull"][0:128, :], in_=PB[:, :]).then_inc(sd, 16)
                    e.dma_start(out=io["PBfull"][128:256, :],
                                in_=zz[:, :]).then_inc(sd, 16)
                else:
                    e.dma_start(out=P_out, in_=PB[:, 0:NC]).then_inc(sd, 16)
                    e.dma_start(out=Bb_out, in_=PB[:, NC:2 * NC]).then_inc(sd, 16)
                ds[0] += 32
                e.wait_ge(sd, ds[0])



def build_E():
    nc = bass.Bass("TRN2", target_bir_lowering=False)
    io = {"x": nc.dram_tensor("x", [TPC, D], F32, kind="ExternalInput").ap(),
          "vT": nc.dram_tensor("vT", [D, TPC], BF16, kind="ExternalInput").ap(),
          "rT": nc.dram_tensor("rT", [D, TPC], BF16, kind="ExternalInput").ap(),
          "Pp": nc.dram_tensor("Pp", [128, 3 * NC], F32, kind="ExternalInput").ap(),
          "Bp": nc.dram_tensor("Bp", [128, 3 * NC], F32, kind="ExternalInput").ap(),
          "td": nc.dram_tensor("td", [1, D], F32, kind="ExternalInput").ap(),
          "tf": nc.dram_tensor("tf", [1, D], F32, kind="ExternalInput").ap(),
          "woT": nc.dram_tensor("woT", [D, D], F32, kind="ExternalInput").ap(),
          "x3": nc.dram_tensor("x3", [TPC, D], F32, kind="ExternalOutput").ap()}
    emit_E(nc, io)
    return nc


def emit_E(nc, io):
    x_in, vT_in, rT_in = io["x"], io["vT"], io["rT"]
    td_in, tf_in, woT_in, x3_out = io["td"], io["tf"], io["woT"], io["x3"]
    fused = "PB_g" in io
    if not fused:
        Pp_in, Bp_in = io["Pp"], io["Bp"]

    with ExitStack() as ctx:
        ds, gs = [0], [0]
        sd = ctx.enter_context(nc.semaphore("sd"))
        sg = ctx.enter_context(nc.semaphore("sg"))
        smm = ctx.enter_context(nc.semaphore("smm"))
        sev = ctx.enter_context(nc.semaphore("sev"))
        y_sb = ctx.enter_context(nc.sbuf_tensor("y_sb", [128, NT * D], F32))
        rwT = ctx.enter_context(nc.sbuf_tensor("rwT", [128, NC * TPC], BF16))
        wbuf = ctx.enter_context(nc.sbuf_tensor("wbuf", [128, NC * D], BF16))
        ps = [ctx.enter_context(nc.psum_tensor(f"ps{n}", [128, 512], F32)) for n in range(4)]
        with ExitStack() as ph1:
            vT_sb = ph1.enter_context(nc.sbuf_tensor("vT_sb", [128, NC * TPC], BF16))
            consts = ph1.enter_context(nc.sbuf_tensor("consts", [128, 8 * NC], F32))
            Pp = ph1.enter_context(nc.sbuf_tensor("Pp_sb", [128, 3 * NC], F32))
            Bp = ph1.enter_context(nc.sbuf_tensor("Bp_sb", [128, 3 * NC], F32))
            tdc = ph1.enter_context(nc.sbuf_tensor("tdc", [128, 2 * NC], F32))
            onesT = ph1.enter_context(nc.sbuf_tensor("onesT", [128, TPC], F32))
            c1bc = ph1.enter_context(nc.sbuf_tensor("c1bc", [128, TPC], F32))
            cv = ph1.enter_context(nc.sbuf_tensor("cv", [128, TPC], F32))
            Sa = ph1.enter_context(nc.sbuf_tensor("Sa", [128, TPC + 1], F32))
            Sb = ph1.enter_context(nc.sbuf_tensor("Sb", [128, TPC + 1], F32))
            num = ph1.enter_context(nc.sbuf_tensor("num", [128, TPC], F32))
            dde = ph1.enter_context(nc.sbuf_tensor("dde", [128, TPC], F32))
            C1, C2, E1, E2E, AAIN, BBIN, D2048, E2R = (0, NC, 2 * NC, 3 * NC, 4 * NC,
                                                       5 * NC, 6 * NC, 7 * NC)
            with nc.Block() as blk:
                @blk.sync
                def _(e):
                    for c in range(NC):
                        e.dma_start(out=vT_sb[:, c * TPC:(c + 1) * TPC],
                                    in_=vT_in[c * 128:(c + 1) * 128, :]).then_inc(sd, 16)
                        e.dma_start(out=rwT[:, c * TPC:(c + 1) * TPC],
                                    in_=rT_in[c * 128:(c + 1) * 128, :]).then_inc(sd, 16)
                        ds[0] += 32
                    if fused:
                        with e.register("dynoffp") as off:
                            for k in range(3):
                                e.reg_load(off, io["offsb"][0:1, 15 + k:16 + k])
                                e.dma_start(out=Pp[:, k * NC:(k + 1) * NC],
                                            in_=bass.AP(io["PB_g"].tensor, off,
                                                        [[2 * NC, 128], [1, NC]])
                                            ).then_inc(sd, 16)
                                e.reg_load(off, io["offsb"][0:1, 18 + k:19 + k])
                                e.dma_start(out=Bp[:, k * NC:(k + 1) * NC],
                                            in_=bass.AP(io["PB_g"].tensor, off,
                                                        [[2 * NC, 128], [1, NC]])
                                            ).then_inc(sd, 16)
                                ds[0] += 32
                    else:
                        e.dma_start(out=Pp[:, :], in_=Pp_in).then_inc(sd, 16)
                        e.dma_start(out=Bp[:, :], in_=Bp_in).then_inc(sd, 16)
                        ds[0] += 32
                    e.wait_ge(sd, ds[0])
                @blk.gpsimd
                def _(e):
                    for c in range(NC):
                        col_load(e, sg, tdc[:, c:c + 1], td_in[0:1, c * 128:(c + 1) * 128], gs)
                        col_load(e, sg, tdc[:, NC + c:NC + c + 1],
                                 tf_in[0:1, c * 128:(c + 1) * 128], gs)
                    e.wait_ge(sg, gs[0])
                @blk.vector
                def _(e):
                    e.memset(onesT[:, :], 1.0)
            with nc.Block() as blk:
                @blk.vector
                def _(e):
                    e.tensor_scalar_max(consts[:, C1:C1 + NC], tdc[:, 0:NC], 0.0)
                    e.tensor_scalar_min(consts[:, C2:C2 + NC], tdc[:, 0:NC], 0.0)
                    e.tensor_scalar_max(consts[:, E1:E1 + NC], tdc[:, NC:2 * NC], 0.0)
                    e.tensor_scalar_min(consts[:, E2E:E2E + NC], tdc[:, NC:2 * NC], 0.0)
                    e.drain()
                    e.tensor_scalar(consts[:, D2048:D2048 + NC], consts[:, C1:C1 + NC],
                                    -float(TPC), 0.0, ALU.mult, ALU.add)
            with nc.Block() as blk:
                @blk.scalar
                def _(e):
                    e.activation(consts[:, C1:C1 + NC], consts[:, C1:C1 + NC],
                                 AF.Exp, scale=-1.0)
                    e.drain()
                    e.activation(consts[:, C2:C2 + NC], consts[:, C2:C2 + NC], AF.Exp)
                    e.drain()
                    e.activation(consts[:, E1:E1 + NC], consts[:, E1:E1 + NC],
                                 AF.Exp, scale=-1.0)
                    e.drain()
                    e.activation(consts[:, E2E:E2E + NC], consts[:, E2E:E2E + NC], AF.Exp)
                    e.drain()
                    e.activation(consts[:, D2048:D2048 + NC], consts[:, D2048:D2048 + NC],
                                 AF.Exp)
            with nc.Block() as blk:
                @blk.vector
                def _(e):
                    e.tensor_copy(consts[:, E2R:E2R + NC], consts[:, E2E:E2E + NC])
                    e.drain()
                    e.tensor_scalar(consts[:, E2E:E2E + NC], consts[:, E2E:E2E + NC],
                                    1.0, 1e-6, ALU.mult, ALU.add)
                    e.drain()
                    for (dst, Psrc) in ((AAIN, Pp), (BBIN, Bp)):
                        e.tensor_tensor(consts[:, dst:dst + NC], Psrc[:, 2 * NC:3 * NC],
                                        consts[:, D2048:D2048 + NC], ALU.mult)
                        e.drain()
                        e.tensor_tensor(consts[:, dst:dst + NC], consts[:, dst:dst + NC],
                                        Psrc[:, NC:2 * NC], ALU.add)
                        e.drain()
                        e.tensor_tensor(consts[:, dst:dst + NC], consts[:, dst:dst + NC],
                                        consts[:, D2048:D2048 + NC], ALU.mult)
                        e.drain()
                        e.tensor_tensor(consts[:, dst:dst + NC], consts[:, dst:dst + NC],
                                        Psrc[:, 0:NC], ALU.add)
                        e.drain()
            for c in range(NC):
                with nc.Block() as blk:
                    @blk.vector
                    def _(e):
                        e.tensor_scalar(c1bc[:, :], onesT[:, :],
                                        consts[:, C1 + c:C1 + c + 1], 0.0, ALU.mult, ALU.add)
                        e.tensor_scalar(cv[:, :], vT_sb[:, c * TPC:(c + 1) * TPC],
                                        consts[:, C2 + c:C2 + c + 1], 0.0, ALU.mult, ALU.add)
                        e.tensor_copy(Sa[:, 0:1], consts[:, AAIN + c:AAIN + c + 1])
                        e.drain()
                        e.tensor_tensor_scan(Sa[:, 1:TPC + 1], c1bc[:, :], cv[:, :],
                                             consts[:, AAIN + c:AAIN + c + 1],
                                             ALU.mult, ALU.add)
                        e.tensor_scalar(cv[:, :], onesT[:, :],
                                        consts[:, C2 + c:C2 + c + 1], 0.0, ALU.mult, ALU.add)
                        e.tensor_copy(Sb[:, 0:1], consts[:, BBIN + c:BBIN + c + 1])
                        e.drain()
                        e.tensor_tensor_scan(Sb[:, 1:TPC + 1], c1bc[:, :], cv[:, :],
                                             consts[:, BBIN + c:BBIN + c + 1],
                                             ALU.mult, ALU.add)
                        e.drain()
                        e.tensor_scalar(num[:, :], Sa[:, 0:TPC],
                                        consts[:, E1 + c:E1 + c + 1], 0.0, ALU.mult, ALU.add)
                        e.drain()
                        e.scalar_tensor_tensor(num[:, :], vT_sb[:, c * TPC:(c + 1) * TPC],
                                               consts[:, E2R + c:E2R + c + 1],
                                               num[:, :], ALU.mult, ALU.add)
                        e.tensor_scalar(dde[:, :], Sb[:, 0:TPC],
                                        consts[:, E1 + c:E1 + c + 1],
                                        consts[:, E2E + c:E2E + c + 1], ALU.mult, ALU.add)
                        e.drain()
                        e.reciprocal(dde[:, :], dde[:, :])
                        e.drain()
                        e.tensor_tensor(num[:, :], num[:, :], dde[:, :], ALU.mult)
                        e.drain()
                        e.tensor_tensor(rwT[:, c * TPC:(c + 1) * TPC], num[:, :],
                                        rwT[:, c * TPC:(c + 1) * TPC], ALU.mult)
        with nc.Block() as blk:
            @blk.gpsimd
            def _(e):
                for c in range(NC):
                    e.dma_start(out=wbuf[:, c * D:(c + 1) * D],
                                in_=woT_in[c * 128:(c + 1) * 128, :]).then_inc(sg, 16)
                    gs[0] += 16
                e.wait_ge(sg, gs[0])
        with nc.Block() as blk:
            @blk.tensor
            def _(e):
                for t in range(NT):
                    if t >= 2:
                        e.wait_ge(sev, 2 * (t - 1))
                    for c in range(NC):
                        lhsT = rwT[:, c * TPC + t * 128:c * TPC + (t + 1) * 128]
                        for n in range(2):
                            ins = e.matmul(ps[2 * (t % 2) + n][:, :], lhsT,
                                           wbuf[:, c * D + n * 512:c * D + (n + 1) * 512],
                                           start=(c == 0), stop=(c == NC - 1))
                    ins.then_inc(smm, 1)
            @blk.scalar
            def _(e):
                for t in range(NT):
                    e.wait_ge(smm, t + 1)
                    for n in range(2):
                        e.copy(y_sb[:, t * D + n * 512:t * D + (n + 1) * 512],
                               ps[2 * (t % 2) + n][:, :]).then_inc(sev, 1)
        emit_ln_resid_streamx(nc, ctx, y_sb, x_in, x3_out, NT, "E")



def build_G(debug=False):
    nc = bass.Bass("TRN2", target_bir_lowering=False)
    io = {"x": nc.dram_tensor("x", [TPC, D], F32, kind="ExternalInput").ap(),
          "k0": nc.dram_tensor("k0", [2 * D, 4096], U8, kind="ExternalInput").ap(),
          "k1": nc.dram_tensor("k1", [4096, 4096], U8, kind="ExternalInput").ap(),
          "k2": nc.dram_tensor("k2", [4096, D], U8, kind="ExternalInput").ap(),
          "sv": nc.dram_tensor("sv", [1, 3], F32, kind="ExternalInput").ap(),
          "out": nc.dram_tensor("out", [TPC, D], F32, kind="ExternalOutput").ap()}
    emit_G(nc, io, debug=debug)
    return nc


def emit_G(nc, io, debug=False):
    x_in = io["x"]
    k0_in, k1_in, k2_in, s_in = io["k0"], io["k1"], io["k2"], io["sv"]
    out_d = io["out"]
    if debug:
        dbg_h1 = nc.dram_tensor("dbg_h1", [32 * 128, 1024], BF16, kind="ExternalOutput").ap()
        dbg_h2 = nc.dram_tensor("dbg_h2", [32 * 128, 1024], BF16, kind="ExternalOutput").ap()
        dbg_y = nc.dram_tensor("dbg_y", [D, TPC], BF16, kind="ExternalOutput").ap()
        dbg_xn = nc.dram_tensor("dbg_xn", [D, TPC], BF16, kind="ExternalOutput").ap()
    xn_d = nc.dram_tensor("xn_d", [TPC, D], BF16)
    w0_d = nc.dram_tensor("w0_d", [D, 4096], BF16)
    w1_d = nc.dram_tensor("w1_d", [32 * 32 * 128, 128], BF16)
    w2_d = nc.dram_tensor("w2_d", [8 * 32 * 128, 128], BF16)
    y_d = nc.dram_tensor("y_d", [D, TPC], BF16)
    N0 = 4096
    TB = 1024
    NB = TPC // TB

    with ExitStack() as ctx:
        ds, gs = [0], [0]
        sd = ctx.enter_context(nc.semaphore("sd"))
        sg = ctx.enter_context(nc.semaphore("sg"))
        smm = ctx.enter_context(nc.semaphore("smm"))
        sev = ctx.enter_context(nc.semaphore("sev"))
        scv = ctx.enter_context(nc.semaphore("scv"))
        syp = [ctx.enter_context(nc.semaphore(f"syp{j}")) for j in range(2)]
        sypn = [0, 0]
        slt = [ctx.enter_context(nc.semaphore(f"slt{j}")) for j in range(4)]
        sltn = [0, 0, 0, 0]
        scl = [ctx.enter_context(nc.semaphore(f"scl{j}")) for j in range(2)]
        scln = [0, 0]
        sgs = [ctx.enter_context(nc.semaphore(f"sgs{j}")) for j in range(2)]
        sgsn = [0, 0]
        cnt = {"mm": 0, "ev": 0, "cv": 0}
        scol = ctx.enter_context(nc.sbuf_tensor("scol", [128, 4], F32))
        hashpad = ctx.enter_context(nc.sbuf_tensor("hashpad", [1, 8], F32))
        srow = ctx.enter_context(nc.sbuf_tensor("srow", [1, 3], F32))
        ones32 = ctx.enter_context(nc.sbuf_tensor("ones32", [1, 128], F32))
        ybf = ctx.enter_context(nc.sbuf_tensor("ybf", [128, NT * D], BF16))
        with ExitStack() as ph1:
            x_sb = ph1.enter_context(nc.sbuf_tensor("x_sb", [128, NT * D], F32))
            xbf = ph1.enter_context(nc.sbuf_tensor("xbf", [128, NT * D], BF16))
            with nc.Block() as blk:
                @blk.sync
                def _(e):
                    emit_load_tiles(e, sd, x_sb, x_in, NT, ds)
                    e.dma_start(out=srow[:, :], in_=s_in).then_inc(sd, 16)
                    ds[0] += 16
                    e.wait_ge(sd, ds[0])
                @blk.vector
                def _(e):
                    e.memset(ones32[:, :], 1.0)
                    e.memset(hashpad[:, :], 3.0)
            emit_ln(nc, ph1, x_sb, NT, xbf, "G")
            with nc.Block() as blk:
                @blk.sync
                def _(e):
                    emit_store_tiles(e, sd, xn_d, xbf, NT, ds)
                    e.wait_ge(sd, ds[0])
        psb = ctx.enter_context(nc.psum_tensor("psb", [128, 4], F32))
        with nc.Block() as blk:
            @blk.tensor
            def _(e):
                e.matmul(psb[:, 0:3], ones32[:1, :], srow[:1, :],
                         start=True, stop=True).then_inc(smm, 1)
            @blk.vector
            def _(e):
                e.wait_ge(smm, 1)
                e.tensor_scalar(scol[:, 0:3], psb[:, 0:3], 2.0, 0.0, ALU.mult, ALU.add)
        cnt["mm"] = 1
        with nc.named_scope("conv"), ExitStack() as cvc:
            u8a = cvc.enter_context(nc.sbuf_tensor("u8a", [128, 2 * 4096], U8))
            u8b = cvc.enter_context(nc.sbuf_tensor("u8b", [128, 2 * 4096], U8))
            wcv = cvc.enter_context(nc.sbuf_tensor("wcv", [128, 2 * 4096], BF16))
            b_cv = cnt["cv"]
            b_scl, b_sgs = list(scln), list(sgsn)
            with nc.Block() as blk:
                @blk.sync
                def _(e):
                    for c in range(NC):
                        if c >= 2:
                            e.wait_ge(scv, b_cv + c - 1)
                        e.dma_start(out=u8a[:, (c % 2) * 4096:(c % 2 + 1) * 4096],
                                    in_=k0_in[c * 128:(c + 1) * 128, :]
                                    ).then_inc(scl[c % 2], 16)
                        e.dma_start(out=u8b[:, (c % 2) * 4096:(c % 2 + 1) * 4096],
                                    in_=k0_in[D + c * 128:D + (c + 1) * 128, :]
                                    ).then_inc(scl[c % 2], 16)
                        scln[c % 2] += 32
                @blk.vector
                def _(e):
                    for c in range(NC):
                        e.wait_ge(scl[c % 2], b_scl[c % 2] + 32 * (c // 2 + 1))
                        if c >= 2:
                            e.wait_ge(sgs[c % 2], b_sgs[c % 2] + 16 * (c // 2))
                        e.tensor_tensor(wcv[:, (c % 2) * 4096:(c % 2 + 1) * 4096],
                                        u8a[:, (c % 2) * 4096:(c % 2 + 1) * 4096],
                                        u8b[:, (c % 2) * 4096:(c % 2 + 1) * 4096],
                                        ALU.subtract).then_inc(scv, 1)
                @blk.gpsimd
                def _(e):
                    for c in range(NC):
                        e.wait_ge(scv, b_cv + c + 1)
                        e.dma_start(out=w0_d[c * 128:(c + 1) * 128, :],
                                    in_=wcv[:, (c % 2) * 4096:(c % 2 + 1) * 4096]
                                    ).then_inc(sgs[c % 2], 16)
                        sgsn[c % 2] += 16
            cnt["cv"] = b_cv + NC
            for (ksrc, wdst, nchunks, ntiles) in ((k1_in, w1_d, 32, 32),
                                                  (k2_in, w2_d, 32, 8)):
                b_cv = cnt["cv"]
                b_scl, b_sgs = list(scln), list(sgsn)
                width = ntiles * 128
                with nc.Block() as blk:
                    @blk.sync
                    def _(e):
                        for c in range(nchunks):
                            if c >= 2:
                                e.wait_ge(scv, b_cv + c - 1)
                            e.dma_start(out=u8a[:, (c % 2) * width:(c % 2) * width + width],
                                        in_=ksrc[c * 128:(c + 1) * 128, :]
                                        ).then_inc(scl[c % 2], 16)
                            scln[c % 2] += 16
                    @blk.vector
                    def _(e):
                        for c in range(nchunks):
                            e.wait_ge(scl[c % 2], b_scl[c % 2] + 16 * (c // 2 + 1))
                            if c >= 2:
                                e.wait_ge(sgs[c % 2], b_sgs[c % 2] + 16 * (c // 2))
                            e.tensor_scalar(wcv[:, (c % 2) * width:(c % 2) * width + width],
                                            u8a[:, (c % 2) * width:(c % 2) * width + width],
                                            2.0, 1.0, ALU.mult,
                                            ALU.subtract).then_inc(scv, 1)
                    @blk.gpsimd
                    def _(e):
                        for c in range(nchunks):
                            e.wait_ge(scv, b_cv + c + 1)
                            e.dma_start(
                                out=bass.AP(wdst.ap().tensor, c * 128 * 128,
                                            [[128, 128],
                                             [nchunks * 128 * 128, ntiles],
                                             [1, 128]]),
                                in_=wcv[:, (c % 2) * width:(c % 2) * width + width]
                                ).then_inc(sgs[c % 2], 16)
                            sgsn[c % 2] += 16
                cnt["cv"] = b_cv + nchunks
        xT = ctx.enter_context(nc.sbuf_tensor("xT", [128, NC * TPC], BF16))
        with nc.Block() as blk:
            @blk.gpsimd
            def _(e):
                e.wait_ge(sgs[0], sgsn[0])
                e.wait_ge(sgs[1], sgsn[1])
            @blk.sync
            def _(e):
                for c in range(NC):
                    e.dma_start(out=xT[:, c * TPC:(c + 1) * TPC],
                                in_=xn_d[:, c * 128:(c + 1) * 128],
                                transpose=True).then_inc(sd, 16)
                    ds[0] += 16
                e.wait_ge(sd, ds[0])
        ps4 = [ctx.enter_context(nc.psum_tensor(f"ps4{n}", [128, 512], F32))
               for n in range(4)]
        for tb in range(NB):
            t0 = tb * TB
            with nc.named_scope(f"tb{tb}"), ExitStack() as tbs:
                h1fm = tbs.enter_context(nc.sbuf_tensor(f"h1fm{tb}", [128, 32 * TB], BF16))
                with nc.named_scope("mm0"), ExitStack() as w0s:
                    w0b = w0s.enter_context(nc.sbuf_tensor(f"w0b{tb}", [128, NC * 4096], BF16))
                    with nc.Block() as blk:
                        @blk.sync
                        def _(e):
                            for c in range(NC):
                                e.dma_start(out=w0b[:, c * 4096:(c + 1) * 4096],
                                            in_=w0_d[c * 128:(c + 1) * 128, :]).then_inc(sd, 16)
                                ds[0] += 16
                            e.wait_ge(sd, ds[0])
                    b_mm, b_ev = cnt["mm"], cnt["ev"]
                    with nc.Block() as blk:
                        @blk.tensor
                        def _(e):
                            for n0 in range(32):
                                if n0 >= 2:
                                    e.wait_ge(sev, b_ev + n0 - 1)
                                for c in range(NC):
                                    lhsT = w0b[:, c * 4096 + n0 * 128:c * 4096 + (n0 + 1) * 128]
                                    for q in range(2):
                                        ins = e.matmul(
                                            ps4[2 * (n0 % 2) + q][:, :], lhsT,
                                            xT[:, c * TPC + t0 + q * 512:c * TPC + t0 + (q + 1) * 512],
                                            start=(c == 0), stop=(c == NC - 1))
                                ins.then_inc(smm, 1)
                        @blk.scalar
                        def _(e):
                            for n0 in range(32):
                                e.wait_ge(smm, b_mm + n0 + 1)
                                for q in range(2):
                                    ins = e.activation(
                                        h1fm[:, n0 * TB + q * 512:n0 * TB + (q + 1) * 512],
                                        ps4[2 * (n0 % 2) + q][:, :], AF.Relu,
                                        scale=scol[:, 0:1])
                                ins.then_inc(sev, 1)
                    cnt["mm"] = b_mm + 32
                    cnt["ev"] = b_ev + 32
                h2fm = tbs.enter_context(nc.sbuf_tensor(f"h2fm{tb}", [128, 32 * TB], BF16))
                w1st = tbs.enter_context(nc.sbuf_tensor(f"w1st{tb}", [128, 4 * 128], BF16))
                b_mm, b_ev, b_sd = cnt["mm"], cnt["ev"], ds[0]
                b_slt = list(sltn)
                with nc.named_scope("mm1"), nc.Block() as blk:
                    @blk.sync
                    def _(e):
                        for i in range(32 * 32):
                            if i >= 4:
                                e.wait_ge(smm, b_mm + i - 3)
                            e.dma_start(out=w1st[:, (i % 4) * 128:(i % 4 + 1) * 128],
                                        in_=w1_d[i * 128:(i + 1) * 128, :]).then_inc(slt[i % 4], 16)
                            sltn[i % 4] += 16
                        for j in range(4):
                            e.wait_ge(slt[j], sltn[j])
                    @blk.tensor
                    def _(e):
                        for i in range(32 * 32):
                            n1, c = divmod(i, 32)
                            e.wait_ge(slt[i % 4], b_slt[i % 4] + 16 * (i // 4 + 1))
                            if c == 0 and n1 >= 2:
                                e.wait_ge(sev, b_ev + n1 - 1)
                            lhsT = w1st[:, (i % 4) * 128:(i % 4 + 1) * 128]
                            for q in range(2):
                                ins = e.matmul(
                                    ps4[2 * (n1 % 2) + q][:, :], lhsT,
                                    h1fm[:, c * TB + q * 512:c * TB + (q + 1) * 512],
                                    start=(c == 0), stop=(c == 31))
                            ins.then_inc(smm, 1)
                    @blk.scalar
                    def _(e):
                        for n1 in range(32):
                            e.wait_ge(smm, b_mm + 32 * (n1 + 1))
                            for q in range(2):
                                ins = e.activation(
                                    h2fm[:, n1 * TB + q * 512:n1 * TB + (q + 1) * 512],
                                    ps4[2 * (n1 % 2) + q][:, :], AF.Relu,
                                    scale=scol[:, 1:2])
                            ins.then_inc(sev, 1)
                cnt["mm"] = b_mm + 32 * 32
                cnt["ev"] = b_ev + 32
                if debug and tb == 0:
                    with nc.Block() as blk:
                        @blk.sync
                        def _(e):
                            for n0 in range(32):
                                e.dma_start(out=dbg_h1[n0 * 128:(n0 + 1) * 128, :],
                                            in_=h1fm[:, n0 * TB:(n0 + 1) * TB]).then_inc(sd, 16)
                                e.dma_start(out=dbg_h2[n0 * 128:(n0 + 1) * 128, :],
                                            in_=h2fm[:, n0 * TB:(n0 + 1) * TB]).then_inc(sd, 16)
                                ds[0] += 32
                            e.wait_ge(sd, ds[0])
                ystage = tbs.enter_context(nc.sbuf_tensor(f"yst{tb}", [128, 2 * TB], BF16))
                w2st = tbs.enter_context(nc.sbuf_tensor(f"w2st{tb}", [128, 4 * 128], BF16))
                b_mm, b_ev = cnt["mm"], cnt["ev"]
                b_slt = list(sltn)
                b_syp = list(sypn)
                with nc.named_scope("mm2"), nc.Block() as blk:
                    @blk.sync
                    def _(e):
                        for i in range(8 * 32):
                            if i >= 4:
                                e.wait_ge(smm, b_mm + i - 3)
                            e.dma_start(out=w2st[:, (i % 4) * 128:(i % 4 + 1) * 128],
                                        in_=w2_d[i * 128:(i + 1) * 128, :]).then_inc(slt[i % 4], 16)
                            sltn[i % 4] += 16
                            if i % 32 == 31:
                                dt = i // 32
                                e.wait_ge(sev, b_ev + dt + 1)
                                e.dma_start(out=y_d[dt * 128:(dt + 1) * 128, t0:t0 + TB],
                                            in_=ystage[:, (dt % 2) * TB:(dt % 2 + 1) * TB]
                                            ).then_inc(syp[dt % 2], 16)
                                sypn[dt % 2] += 16
                        for j in range(4):
                            e.wait_ge(slt[j], sltn[j])
                        e.wait_ge(syp[0], sypn[0])
                        e.wait_ge(syp[1], sypn[1])
                    @blk.tensor
                    def _(e):
                        for i in range(8 * 32):
                            dt, c = divmod(i, 32)
                            e.wait_ge(slt[i % 4], b_slt[i % 4] + 16 * (i // 4 + 1))
                            if c == 0 and dt >= 2:
                                e.wait_ge(sev, b_ev + dt - 1)
                            lhsT = w2st[:, (i % 4) * 128:(i % 4 + 1) * 128]
                            for q in range(2):
                                ins = e.matmul(
                                    ps4[2 * (dt % 2) + q][:, :], lhsT,
                                    h2fm[:, c * TB + q * 512:c * TB + (q + 1) * 512],
                                    start=(c == 0), stop=(c == 31))
                            ins.then_inc(smm, 1)
                    @blk.scalar
                    def _(e):
                        for dt in range(8):
                            e.wait_ge(smm, b_mm + 32 * (dt + 1))
                            if dt >= 2:
                                e.wait_ge(syp[dt % 2], b_syp[dt % 2] + 16 * (dt // 2))
                            for q in range(2):
                                ins = e.activation(
                                    ystage[:, (dt % 2) * TB + q * 512:(dt % 2) * TB + (q + 1) * 512],
                                    ps4[2 * (dt % 2) + q][:, :], AF.Copy,
                                    scale=scol[:, 2:3])
                            ins.then_inc(sev, 1)
                cnt["mm"] = b_mm + 8 * 32
                cnt["ev"] = b_ev + 8
        if debug:
            with nc.Block() as blk:
                @blk.sync
                def _(e):
                    for c in range(NC):
                        e.dma_start(out=dbg_y[c * 128:(c + 1) * 128, :],
                                    in_=y_d[c * 128:(c + 1) * 128, :]).then_inc(sd, 16)
                        e.dma_start(out=dbg_xn[c * 128:(c + 1) * 128, :],
                                    in_=xT[:, c * TPC:(c + 1) * TPC]).then_inc(sd, 16)
                        ds[0] += 32
                    e.wait_ge(sd, ds[0])
        with nc.named_scope("tail"):
            with nc.Block() as blk:
                @blk.sync
                def _(e):
                    for t in range(NT):
                        e.dma_start(out=ybf[:, t * D:(t + 1) * D],
                                    in_=y_d[:, t * 128:(t + 1) * 128],
                                    transpose=True).then_inc(sd, 16)
                        ds[0] += 16
                    e.wait_ge(sd, ds[0])
            emit_ln_resid_streamx(nc, ctx, ybf, x_in, out_d, NT, "G2")



def build_full(dbg=False):
    nc = bass.Bass("TRN2", target_bir_lowering=False)
    I32 = mybir.dt.int32
    ein = lambda n, sh, dt: nc.dram_tensor(n, sh, dt, kind="ExternalInput").ap()
    x_in = ein("x", [TPC, D], F32)
    wq_in = ein("wq", [D, 3 * D], F32)
    bq_in = ein("bq", [1, 3 * D], F32)
    wo_in = ein("wo", [D, D], F32)
    bo_in = ein("bo", [1, D], F32)
    wrT_in = ein("wrT", [D, D], F32)
    wkT_in = ein("wkT", [D, D], F32)
    wvT_in = ein("wvT", [D, D], F32)
    mur_in = ein("mur", [1, D], F32)
    muk_in = ein("muk", [1, D], F32)
    wv2_in = ein("wv2T", [D, D], F32)
    wr2_in = ein("wr2T", [D, D], F32)
    td_in = ein("td", [1, D], F32)
    tf_in = ein("tf", [1, D], F32)
    wo2_in = ein("wo2T", [D, D], F32)
    k0_in = ein("k0", [2 * D, 4096], U8)
    k1_in = ein("k1", [4096, 4096], U8)
    k2_in = ein("k2", [4096, D], U8)
    sv_in = ein("sv", [1, 3], F32)
    offs_in = ein("offs", [1, 64], I32)
    out_d = nc.dram_tensor("out", [TPC, D], F32, kind="ExternalOutput").ap()

    kindh = "ExternalOutput" if dbg else "Internal"
    qkv_ds = [nc.dram_tensor(f"qkv_d{k}", [512, 3 * D], BF16).ap() for k in range(4)]
    qkv_gs = [nc.dram_tensor(f"qkv_g{k}", [NCORES * 512, 3 * D], BF16).ap()
              for k in range(4)]
    qkv_loc = nc.dram_tensor("qkv_loc", [S, 768], BF16).ap()
    ctxn_d = nc.dram_tensor("ctxn_d", [4 * 64, S], BF16).ap()
    ctxn_g = nc.dram_tensor("ctxn_g", [NCORES * 4 * 64, S], BF16).ap()
    x1_d = nc.dram_tensor("x1_d", [TPC, D], F32, kind=kindh).ap()
    lr_in = nc.dram_tensor("lr_in", [2, D], F32).ap()
    lr_g = nc.dram_tensor("lr_g", [2 * NCORES, D], F32).ap()
    x2_d = nc.dram_tensor("x2_d", [TPC, D], F32, kind=kindh).ap()
    vT_d = nc.dram_tensor("vT_d", [D, TPC], BF16).ap()
    rT_d = nc.dram_tensor("rT_d", [D, TPC], BF16).ap()
    PB_in = nc.dram_tensor("PB_in", [256, 2 * NC], F32).ap()
    PB_g = nc.dram_tensor("PB_g", [256 * NCORES, 2 * NC], F32).ap()
    x3_d = nc.dram_tensor("x3_d", [TPC, D], F32, kind=kindh).ap()

    grp = [list(range(NCORES))]
    with ExitStack() as prog:
        set_pfx(nc, "F_")
        nc._sem_stack = prog
        osb = prog.enter_context(nc.sbuf_tensor("osb", [1, 64], I32))
        soff = prog.enter_context(nc.semaphore("soff"))
        ccs = prog.enter_context(nc.semaphore("ccs"))
        with nc.Block() as blk:
            @blk.sync
            def _(e):
                e.dma_start(out=osb[:, :], in_=offs_in).then_inc(soff, 16)
                e.wait_ge(soff, 16)

        set_pfx(nc, "A_")
        with nc.named_scope("phA"):
            emit_A(nc, {"x": x_in, "wq": wq_in, "bq": bq_in, "qkv": qkv_ds,
                        "gath": (list(zip(qkv_ds, qkv_gs)), ccs)})

        set_pfx(nc, "B_")
        with nc.named_scope("phB"):
            emit_B(nc, {"qkv_gs": qkv_gs, "offsb": osb, "ctxn": ctxn_d,
                        "qkv_loc": qkv_loc, "ccs": ccs, "cc0": 0}, SS=S)
        set_pfx(nc, "c2_")
        with nc.named_scope("cc2"), nc.Block() as blk:
            @blk.gpsimd
            def _(e):
                e.collective_compute("AllGather", ALU.bypass, replica_groups=grp,
                                     ins=[ctxn_d], outs=[ctxn_g]).then_inc(ccs, 1)
                e.wait_ge(ccs, 5)

        set_pfx(nc, "C_")
        with nc.named_scope("phC"):
            emit_C(nc, {"wo": wo_in, "bo": bo_in, "x": x_in, "x1": x1_d,
                        "ctxn_g": ctxn_g, "offsb": osb})
        set_pfx(nc, "c3_")
        with nc.named_scope("cc3"), ExitStack() as hctx:
            z1 = hctx.enter_context(nc.sbuf_tensor("z1d", [1, D], F32))
            slr = prog.enter_context(nc.semaphore("slr"))
            with nc.Block() as blk:
                @blk.vector
                def _(e):
                    e.memset(z1[:, :], 0.0)
            with nc.Block() as blk:
                @blk.sync
                def _(e):
                    e.dma_start(out=lr_in[0:1, :],
                                in_=x1_d[TPC - 1:TPC, :]).then_inc(slr, 16)
                    e.dma_start(out=lr_in[1:2, :], in_=z1[:, :]).then_inc(slr, 16)
                    e.wait_ge(slr, 32)
            with nc.Block() as blk:
                @blk.gpsimd
                def _(e):
                    e.collective_compute("AllGather", ALU.bypass, replica_groups=grp,
                                         ins=[lr_in], outs=[lr_g]).then_inc(ccs, 1)
                    e.wait_ge(ccs, 6)

        set_pfx(nc, "D_")
        with nc.named_scope("phD"):
            emit_D(nc, {"x": x1_d, "wrT": wrT_in, "wkT": wkT_in, "wvT": wvT_in,
                        "mur": mur_in, "muk": muk_in, "x2": x2_d,
                        "lr_g": lr_g, "offsb": osb})
        set_pfx(nc, "E0_")
        with nc.named_scope("phE0"):
            emit_E0(nc, {"x": x2_d, "wvT": wv2_in, "wrT": wr2_in, "td": td_in,
                         "vT": vT_d, "rT": rT_d, "PBfull": PB_in})
        set_pfx(nc, "c4_")
        with nc.named_scope("cc4"), nc.Block() as blk:
            @blk.gpsimd
            def _(e):
                e.collective_compute("AllGather", ALU.bypass, replica_groups=grp,
                                     ins=[PB_in], outs=[PB_g]).then_inc(ccs, 1)
                e.wait_ge(ccs, 7)

        set_pfx(nc, "E_")
        with nc.named_scope("phE"):
            emit_E(nc, {"x": x2_d, "vT": vT_d, "rT": rT_d, "td": td_in, "tf": tf_in,
                        "woT": wo2_in, "x3": x3_d, "PB_g": PB_g, "offsb": osb})
        set_pfx(nc, "G_")
        with nc.named_scope("phG"):
            emit_G(nc, {"x": x3_d, "k0": k0_in, "k1": k1_in, "k2": k2_in, "sv": sv_in,
                        "out": out_d})
        if dbg:
            set_pfx(nc, "")
            qloc_s = nc.dram_tensor("qloc_s", [128, 768], BF16,
                                    kind="ExternalOutput").ap()
            qg_s = nc.dram_tensor("qg_s", [128, 3 * D], BF16,
                                  kind="ExternalOutput").ap()
            cg_s = nc.dram_tensor("cg_s", [128, S], BF16,
                                  kind="ExternalOutput").ap()
            set_pfx(nc, "dbg_")
            with ExitStack() as dctx:
                t1s = dctx.enter_context(nc.sbuf_tensor("smp1", [128, 3 * D], BF16))
                t2s = dctx.enter_context(nc.sbuf_tensor("smp2", [128, S], BF16))
                sdd = prog.enter_context(nc.semaphore("sdd"))
                with nc.Block() as blk:
                    @blk.sync
                    def _(e):
                        with e.register("dqs") as off:
                            e.reg_load(off, osb[0:1, 21:22])
                            e.dma_start(out=t1s[:, :],
                                        in_=bass.AP(qkv_gs[0].tensor, off,
                                                    [[32 * 3 * D, 128], [1, 3 * D]])
                                        ).then_inc(sdd, 16)
                        e.dma_start(out=t2s[:, :],
                                    in_=bass.AP(ctxn_g.tensor, 0,
                                                [[16 * S, 128], [1, S]])
                                    ).then_inc(sdd, 16)
                        e.wait_ge(sdd, 32)
                        e.dma_start(out=qg_s, in_=t1s[:, :]).then_inc(sdd, 16)
                        e.dma_start(out=cg_s, in_=t2s[:, :]).then_inc(sdd, 16)
                        e.dma_start(out=qloc_s,
                                    in_=bass.AP(qkv_loc.tensor, 0,
                                                [[64 * 768, 128], [1, 768]])
                                    ).then_inc(sdd, 16)
                        e.wait_ge(sdd, 80)
        nc._sem_stack = None
    return nc


HEADPERM = [h for g in range(NG) for h in (g, NG + 3 * g, NG + 3 * g + 1,
                                           NG + 3 * g + 2)]
COLPERM = np.concatenate([np.arange(HD * h, HD * (h + 1)) for h in HEADPERM])


def core_offsets(c):
    bb, g = c // 4, c % 4
    off = np.zeros(64, np.int32)
    off[22] = 4 * bb * 512 * 3 * D + 768 * g
    for cc in range(NC):
        off[6 + cc] = (256 * (4 * bb + cc // 2) + 128 * (cc % 2)) * S + g * TPC
    off[14] = D if g == 0 else 2 * (c - 1) * D
    off[21] = 0
    for k in (1, 2, 3):
        if g >= k:
            off[14 + k] = 256 * (c - k) * 2 * NC
            off[17 + k] = 256 * (c - k) * 2 * NC + NC
        else:
            off[14 + k] = 128 * 2 * NC
            off[17 + k] = 128 * 2 * NC + NC
    return off.reshape(1, 64)



_CACHE = {}


def _run(name, builder, in_maps):
    if name not in _CACHE:
        _CACHE[name] = builder()
    res = run_bass_kernel_spmd(_CACHE[name], in_maps, core_ids=list(range(NCORES)))
    return res.results


def kernel_unfused(L0_attn_x, L0_attn_w_qkv, L0_attn_b_qkv, L0_attn_w_out,
           L0_attn_b_out,
           L0_ffn_W_r, L0_ffn_W_k, L0_ffn_W_v, L0_ffn_mu_r, L0_ffn_mu_k,
           L1_attn_time_decay, L1_attn_time_first, L1_attn_key_w, L1_attn_value_w,
           L1_attn_recep_w, L1_attn_out_w, L1_ffn_k0, L1_ffn_k1, L1_ffn_k2,
           L1_ffn_s0, L1_ffn_s1, L1_ffn_s2):
    C = np.ascontiguousarray
    xflat = np.asarray(L0_attn_x, np.float32).reshape(B * S, D)
    xsl = [C(xflat[c * TPC:(c + 1) * TPC]) for c in range(NCORES)]
    wq = C(np.asarray(L0_attn_w_qkv, np.float32))
    bq = C(np.asarray(L0_attn_b_qkv, np.float32).reshape(1, 3 * D))

    resA = _run("A", build_A, [{"x": xsl[c], "wq": wq, "bq": bq} for c in range(NCORES)])
    qkv = np.concatenate([resA[c]["qkv"] for c in range(NCORES)], 0).reshape(B, S, 3 * D)

    inB = []
    for c in range(NCORES):
        m = {}
        for p in range(2):
            for nm, off in (("kt", D), ("qt", 0), ("vt", 2 * D)):
                arr = np.empty((128, S), BF)
                for s2 in range(2):
                    bb, hh = CORE_UNITS[c][2 * p + s2]
                    arr[64 * s2:64 * s2 + 64] = qkv[bb, :, off + HD * hh:off + HD * (hh + 1)].T
                m[f"{nm}{p}"] = C(arr)
        inB.append(m)
    resB = _run("B", build_B, inB)
    ctx_fm = np.empty((B, D, S), BF)
    for c in range(NCORES):
        for ui, (bb, hh) in enumerate(CORE_UNITS[c]):
            ctx_fm[bb, HD * hh:HD * (hh + 1), :] = resB[c]["ctxn"][64 * ui:64 * ui + 64]

    wo = C(np.asarray(L0_attn_w_out, np.float32))
    bo = C(np.asarray(L0_attn_b_out, np.float32).reshape(1, D))
    inC = [{"ctxf": C(ctx_fm[c // 4][:, (c % 4) * TPC:(c % 4 + 1) * TPC]),
            "wo": wo, "bo": bo, "x": xsl[c]} for c in range(NCORES)]
    resC = _run("C", build_C, inC)
    x1 = [np.asarray(resC[c]["x1"]) for c in range(NCORES)]

    wrT = C(np.asarray(L0_ffn_W_r, np.float32).T)
    wkT = C(np.asarray(L0_ffn_W_k, np.float32).T)
    wvT = C(np.asarray(L0_ffn_W_v, np.float32).T)
    mur = C(np.asarray(L0_ffn_mu_r, np.float32).reshape(1, D))
    muk = C(np.asarray(L0_ffn_mu_k, np.float32).reshape(1, D))
    zrow = np.zeros((1, D), np.float32)
    inD = []
    for c in range(NCORES):
        halo = zrow if c % 4 == 0 else C(x1[c - 1][TPC - 1:TPC, :])
        inD.append({"x": C(x1[c]), "halo": halo, "wrT": wrT, "wkT": wkT, "wvT": wvT,
                    "mur": mur, "muk": muk})
    resD = _run("D", build_D, inD)
    x2 = [np.asarray(resD[c]["x2"]) for c in range(NCORES)]

    wvT2 = C(np.asarray(L1_attn_value_w, np.float32).T)
    wrT2 = C(np.asarray(L1_attn_recep_w, np.float32).T)
    td = C(np.asarray(L1_attn_time_decay, np.float32).reshape(1, D))
    tf = C(np.asarray(L1_attn_time_first, np.float32).reshape(1, D))
    inE0 = [{"x": C(x2[c]), "wvT": wvT2, "wrT": wrT2, "td": td} for c in range(NCORES)]
    resE0 = _run("E0", build_E0, inE0)

    woT2 = C(np.asarray(L1_attn_out_w, np.float32).T)
    z128 = np.zeros((128, NC), np.float32)
    inE = []
    for c in range(NCORES):
        Pp = np.concatenate([np.asarray(resE0[c - k]["P"]) if c % 4 >= k else z128
                             for k in (1, 2, 3)], 1)
        Bp = np.concatenate([np.asarray(resE0[c - k]["Bb"]) if c % 4 >= k else z128
                             for k in (1, 2, 3)], 1)
        inE.append({"x": C(x2[c]), "vT": C(np.asarray(resE0[c]["vT"])),
                    "rT": C(np.asarray(resE0[c]["rT"])),
                    "Pp": C(Pp), "Bp": C(Bp), "td": td, "tf": tf, "woT": woT2})
    resE = _run("E", build_E, inE)
    x3 = [np.asarray(resE[c]["x3"]) for c in range(NCORES)]

    k0 = np.asarray(L1_ffn_k0).view(np.uint8)
    k1 = np.asarray(L1_ffn_k1).view(np.uint8)
    k2 = np.asarray(L1_ffn_k2).view(np.uint8)
    sv = C(np.stack([np.float32(L1_ffn_s0), np.float32(L1_ffn_s1),
                     np.float32(L1_ffn_s2)]).reshape(1, 3))
    inG = [{"x": C(x3[c]), "k0": C(k0), "k1": C(k1), "k2": C(k2), "sv": sv}
           for c in range(NCORES)]
    resG = _run("G", build_G, inG)
    out = np.concatenate([np.asarray(resG[c]["out"], np.float32)
                          for c in range(NCORES)], 0)
    return out.reshape(B, S, D)


def kernel(L0_attn_x, L0_attn_w_qkv, L0_attn_b_qkv, L0_attn_w_out, L0_attn_b_out,
           L0_ffn_W_r, L0_ffn_W_k, L0_ffn_W_v, L0_ffn_mu_r, L0_ffn_mu_k,
           L1_attn_time_decay, L1_attn_time_first, L1_attn_key_w, L1_attn_value_w,
           L1_attn_recep_w, L1_attn_out_w, L1_ffn_k0, L1_ffn_k1, L1_ffn_k2,
           L1_ffn_s0, L1_ffn_s1, L1_ffn_s2):
    C = np.ascontiguousarray
    xflat = np.asarray(L0_attn_x, np.float32).reshape(B * S, D)
    perm3 = np.concatenate(
        [np.arange(sec * D + HD * h, sec * D + HD * (h + 1))
         for g in range(NG) for sec in range(3)
         for h in HEADPERM[4 * g:4 * g + 4]])
    wq = C(np.asarray(L0_attn_w_qkv, np.float32)[:, perm3])
    bq = C(np.asarray(L0_attn_b_qkv, np.float32).reshape(1, 3 * D)[:, perm3])
    wo = C(np.asarray(L0_attn_w_out, np.float32)[COLPERM, :])
    bo = C(np.asarray(L0_attn_b_out, np.float32).reshape(1, D))
    wrT = C(np.asarray(L0_ffn_W_r, np.float32).T)
    wkT = C(np.asarray(L0_ffn_W_k, np.float32).T)
    wvT = C(np.asarray(L0_ffn_W_v, np.float32).T)
    mur = C(np.asarray(L0_ffn_mu_r, np.float32).reshape(1, D))
    muk = C(np.asarray(L0_ffn_mu_k, np.float32).reshape(1, D))
    wv2T = C(np.asarray(L1_attn_value_w, np.float32).T)
    wr2T = C(np.asarray(L1_attn_recep_w, np.float32).T)
    td = C(np.asarray(L1_attn_time_decay, np.float32).reshape(1, D))
    tf = C(np.asarray(L1_attn_time_first, np.float32).reshape(1, D))
    wo2T = C(np.asarray(L1_attn_out_w, np.float32).T)
    k0 = C(np.asarray(L1_ffn_k0).view(np.uint8))
    k1 = C(np.asarray(L1_ffn_k1).view(np.uint8))
    k2 = C(np.asarray(L1_ffn_k2).view(np.uint8))
    sv = C(np.stack([np.float32(L1_ffn_s0), np.float32(L1_ffn_s1),
                     np.float32(L1_ffn_s2)]).reshape(1, 3))
    inF = []
    for c in range(NCORES):
        inF.append({"x": C(xflat[c * TPC:(c + 1) * TPC]), "wq": wq, "bq": bq,
                    "wo": wo, "bo": bo, "wrT": wrT, "wkT": wkT, "wvT": wvT,
                    "mur": mur, "muk": muk, "wv2T": wv2T, "wr2T": wr2T,
                    "td": td, "tf": tf, "wo2T": wo2T,
                    "k0": k0, "k1": k1, "k2": k2, "sv": sv,
                    "offs": core_offsets(c)})
    res = _run("F", build_full, inF)
    out = np.concatenate([np.asarray(res[c]["out"], np.float32)
                          for c in range(NCORES)], 0)
    return out.reshape(B, S, D)
